# version 04
# baseline (speedup 1.0000x reference)
import contextlib
import numpy as np
import concourse.bass as bass
import concourse.mybir as mybir
from concourse.bass_utils import run_bass_kernel_spmd

F32 = mybir.dt.float32
BF16 = mybir.dt.bfloat16
AF = mybir.ActivationFunctionType
ALU = mybir.AluOpType

B, T, C = 4, 4096, 1024
NCH = 8
TCS = 512
NTCS = T // TCS
LW = 512
NLW = T // LW
SLOT = 512
NSLOT = 4
SW = SLOT + 1
LN_EPS = 1e-5
WKV_EPS = 1e-8
NEG = -1e38

ENGINES = ("sync", "tensor", "vector", "scalar", "gpsimd")


class Prog:

    def __init__(self, nc, es):
        self.nc = nc
        self.es = es
        self.ops = {e: [] for e in ENGINES}
        self.sems = {}
        self.cnt = {e: 0 for e in ENGINES}
        self.lastw = {}
        self.reads = {}
        self.dramw = {}
        self.dsem = {}
        self.ndsem = 0
        self.water = {e: {} for e in ENGINES}
        self.pending = {e: [] for e in ENGINES}

    def make_sems(self):
        for e in ENGINES:
            self.sems[e] = self.es.enter_context(self.nc.semaphore(f"sem_{e}"))

    @staticmethod
    def _key(ap):
        t = getattr(ap, "tensor", ap)
        return id(t)

    def _dma_sem(self, dkey):
        k = self._key(dkey)
        if k not in self.dsem:
            nm = f"dma{self.ndsem}"
            self.ndsem += 1
            self.sems[nm] = self.es.enter_context(self.nc.semaphore(nm))
            self.cnt[nm] = 0
            self.dsem[k] = nm
        return self.dsem[k]

    def op(self, eng, fn, reads=(), writes=(), dma=False, dkey=None,
           dram_in=None, dram_out=None, extra_waits=()):
        waits = {}

        def addw(ev):
            if ev is None:
                return
            sem, val = ev[0], ev[1]
            if sem == eng:
                return
            if waits.get(sem, 0) < val:
                waits[sem] = val

        for ap in reads:
            addw(self.lastw.get(self._key(ap)))
        for ap in writes:
            k = self._key(ap)
            addw(self.lastw.get(k))
            for ev in self.reads.get(k, ()):
                addw(ev)
        if dram_in is not None:
            for sem, val in self.dramw.get(self._key(dram_in), {}).items():
                addw((sem, val))
        for ev in extra_waits:
            addw(ev)
        for ev in self.pending[eng]:
            addw(ev)
        self.pending[eng] = []
        final_waits = []
        for sem, val in waits.items():
            if self.water[eng].get(sem, 0) >= val:
                continue
            self.water[eng][sem] = val
            final_waits.append((sem, val))
        if dma:
            assert dkey is not None, "dma op needs dkey (sbuf-side tile)"
            semn = self._dma_sem(dkey)
            self.cnt[semn] += 16
            ev = (semn, self.cnt[semn])
        else:
            self.cnt[eng] += 1
            ev = (eng, self.cnt[eng])
        self.ops[eng].append((fn, tuple(final_waits), 16 if dma else 1, ev[0]))
        for ap in writes:
            k = self._key(ap)
            self.lastw[k] = ev
            self.reads[k] = []
        for ap in reads:
            self.reads.setdefault(self._key(ap), []).append(ev)
        if dram_out is not None:
            d = self.dramw.setdefault(self._key(dram_out), {})
            d[ev[0]] = max(d.get(ev[0], 0), ev[1])
        return ev

    def all_events(self):
        return [(s, c) for s, c in self.cnt.items() if c > 0]

    def barrier(self):
        evs = self.all_events()
        for e in ENGINES:
            self.pending[e] = [ev for ev in evs if ev[0] != e]

    def replay(self, block):
        for e in ENGINES:
            eng_ops = self.ops[e]
            sems = self.sems

            def mk(ops_list):
                def body(engine):
                    for fn, waits, amt, incsem in ops_list:
                        for (sem, val) in waits:
                            engine.wait_ge(sems[sem], val)
                        inst = fn(engine)
                        inst.then_inc(sems[incsem], amt)
                return body
            getattr(block, e)(mk(eng_ops))


def build_program():
    nc = bass.Bass("TRN2", target_bir_lowering=False)

    def dram(name, shape, dt=F32, kind="Internal"):
        return nc.dram_tensor(name, shape, dt, kind=kind).ap()

    xT = dram("xT", [C, T], kind="ExternalInput")
    xe_in = dram("xe_in", [C, NSLOT * SW], kind="ExternalInput")
    xpr_in = dram("xpr_in", [C, NSLOT * SW], kind="ExternalInput")
    wtmkv_in = dram("wtmkv_in", [C, 2048], kind="ExternalInput")
    wtmr_in = dram("wtmr_in", [8 * 128, 8 * 128], kind="ExternalInput")
    wcm_in = dram("wcm_in", [32 * 128, 8 * 128], kind="ExternalInput")
    wcp_in = dram("wcp_in", [8 * 128, 32 * 128], kind="ExternalInput")
    wpj_in = dram("wpj_in", [32 * 128, 32 * 128], kind="ExternalInput")
    rm_in = dram("rm_in", [32 * 128, 128], kind="ExternalInput")
    ru_in = dram("ru_in", [128 * 128, 128], kind="ExternalInput")
    VNAMES = ("vw", "vu", "vtmk", "vtmv", "vtmr", "vcmk", "vcmr", "vg1", "vb1", "vg2", "vb2")
    vecs_in = {nm: dram(nm, [C, 1], kind="ExternalInput") for nm in VNAMES}
    mask_in = dram("mask_in", [128, 1], kind="ExternalInput")
    out = dram("out", [C, NSLOT * SLOT], kind="ExternalOutput")

    kbuf = dram("kbuf", [C, T])
    vbuf = dram("vbuf", [C, T])
    ybuf = dram("ybuf", [C, T])
    wtmr_b = dram("wtmr_b", [8 * 128, 8 * 128], BF16)
    wcm_b = dram("wcm_b", [32 * 128, 8 * 128], BF16)
    wcp_b = dram("wcp_b", [8 * 128, 32 * 128], BF16)
    wpj_b = dram("wpj_b", [32 * 128, 32 * 128], BF16)
    cbd_b = dram("cbd_b", [4 * 128, 32 * 128], BF16)

    with contextlib.ExitStack() as es:
        E = es.enter_context
        p = Prog(nc, es)
        p.make_sems()

        vt = {nm: E(nc.sbuf_tensor(nm + "_t", [128, NCH], F32)) for nm in VNAMES}
        OMN = ("vtmk", "vtmv", "vtmr", "vcmk", "vcmr")
        om = {nm: E(nc.sbuf_tensor("om_" + nm, [128, NCH], F32)) for nm in OMN}
        maskt = E(nc.sbuf_tensor("maskt", [128, 1], F32))
        epsT = E(nc.sbuf_tensor("epsT", [128, 1], F32))
        ones_b = E(nc.sbuf_tensor("ones_b", [128, 128], BF16))
        carryL = [E(nc.sbuf_tensor(f"carry{c}", [128, 4], F32)) for c in range(NCH)]

        def vs(tile, c):
            return tile[:, c:c + 1]

        for nm, d in vecs_in.items():
            for c in range(NCH):
                p.op("sync", (lambda nm, c, d: lambda e: e.dma_start(out=vs(vt[nm], c), in_=d[c * 128:(c + 1) * 128, :]))(nm, c, d),
                     writes=[vt[nm]], dma=True, dkey=vt[nm])
        p.op("sync", lambda e: e.dma_start(out=maskt[:], in_=mask_in), writes=[maskt], dma=True, dkey=maskt)
        maskom = E(nc.sbuf_tensor("maskom", [128, 1], F32))
        p.op("vector", lambda e: e.tensor_scalar(maskom[:], maskt[:], -1.0, 1.0, ALU.mult, ALU.add),
             reads=[maskt], writes=[maskom])
        for nm in OMN:
            for c in range(NCH):
                p.op("vector", (lambda nm, c: lambda e: e.tensor_scalar(vs(om[nm], c), vs(vt[nm], c), -1.0, 1.0, ALU.mult, ALU.add))(nm, c),
                     reads=[vt[nm]], writes=[om[nm]])
        p.op("vector", lambda e: e.memset(epsT[:], LN_EPS), writes=[epsT])
        p.op("vector", lambda e: e.memset(ones_b[:], 1.0), writes=[ones_b])
        for c in range(NCH):
            p.op("vector", (lambda c: lambda e: e.memset(carryL[c][:], 0.0))(c), writes=[carryL[c]])
            p.op("vector", (lambda c: lambda e: e.memset(carryL[c][:, 0:1], NEG))(c), writes=[carryL[c]])
            p.op("vector", (lambda c: lambda e: e.memset(carryL[c][:, 1:3], 1.0))(c), writes=[carryL[c]])

        with contextlib.ExitStack() as es1:
            E1 = es1.enter_context
            NSTG = 4
            stgL = [E1(nc.sbuf_tensor(f"stg{j}", [128, 2048], F32)) for j in range(NSTG)]
            stgbL = [E1(nc.sbuf_tensor(f"stgb{j}", [128, 2048], BF16)) for j in range(NSTG)]
            stg = stgL[0]
            gi = 0
            CENG = ("gpsimd", "vector", "scalar")
            for (src, dst) in ((wtmr_in, wtmr_b), (wcm_in, wcm_b), (wcp_in, wcp_b), (wpj_in, wpj_b)):
                nelem = int(np.prod(src.shape))
                ns = nelem // (128 * 2048)
                fs = src.rearrange("a b -> (a b)").rearrange("(s p f) -> s p f", s=ns, p=128, f=2048)
                fd = dst.rearrange("a b -> (a b)").rearrange("(s p f) -> s p f", s=ns, p=128, f=2048)
                for s in range(ns):
                    sg = stgL[gi % NSTG]
                    sb = stgbL[gi % NSTG]
                    ceng = CENG[gi % 3]
                    gi += 1
                    p.op("sync", (lambda s, fs, sg: lambda e: e.dma_start(out=sg[:], in_=fs[s]))(s, fs, sg),
                         writes=[sg], dma=True, dkey=sg)
                    if ceng == "scalar":
                        p.op(ceng, (lambda sg, sb: lambda e: e.activation(sb[:], sg[:], AF.Copy))(sg, sb),
                             reads=[sg], writes=[sb])
                    else:
                        p.op(ceng, (lambda sg, sb: lambda e: e.tensor_copy(sb[:], sg[:]))(sg, sb),
                             reads=[sg], writes=[sb])
                    p.op("gpsimd", (lambda s, fd, dst, sb: lambda e: e.dma_start(out=fd[s], in_=sb[:]))(s, fd, dst, sb),
                         reads=[sb], dma=True, dkey=sb, dram_out=dst)
            rmt = E1(nc.sbuf_tensor("rmt", [128, 32 * 128], BF16))
            rut = E1(nc.sbuf_tensor("rut", [128, 32 * 128], BF16))
            cbt = E1(nc.sbuf_tensor("cbt", [128, 32 * 128], BF16))
            pmc = E1(nc.psum_tensor("pmc", [128, 512], F32))
            rm_f = rm_in.rearrange("(t p) f -> t p f", p=128)
            ru_f = ru_in.rearrange("(t p) f -> t p f", p=128)
            for t_ in range(32):
                p.op("sync", (lambda t_: lambda e: e.dma_start(out=stg[:, 0:128], in_=rm_f[t_]))(t_),
                     writes=[stg], dma=True, dkey=stg)
                p.op("gpsimd", (lambda t_: lambda e: e.tensor_copy(rmt[:, t_ * 128:(t_ + 1) * 128], stg[:, 0:128]))(t_),
                     reads=[stg], writes=[rmt])
            for kg in range(4):
                for kl in range(32):
                    k = kg * 32 + kl
                    p.op("sync", (lambda k: lambda e: e.dma_start(out=stg[:, 128:256], in_=ru_f[k]))(k),
                         writes=[stg], dma=True, dkey=stg)
                    p.op("gpsimd", (lambda kl: lambda e: e.tensor_copy(rut[:, kl * 128:(kl + 1) * 128], stg[:, 128:256]))(kl),
                         reads=[stg], writes=[rut])
                    hc = k // 4
                    p.op("tensor", (lambda kl, hc: lambda e: e.matmul(pmc[:, 0:128], rmt[:, hc * 128:(hc + 1) * 128],
                                                                      rut[:, kl * 128:(kl + 1) * 128], start=True, stop=True))(kl, hc),
                         reads=[rmt, rut], writes=[pmc])
                    p.op("scalar", (lambda kl: lambda e: e.activation(cbt[:, kl * 128:(kl + 1) * 128], pmc[:, 0:128], AF.Copy))(kl),
                         reads=[pmc], writes=[cbt])
                p.op("gpsimd", (lambda kg: lambda e: e.dma_start(out=cbd_b[kg * 128:(kg + 1) * 128, :], in_=cbt[:]))(kg),
                     reads=[cbt], dma=True, dkey=cbt, dram_out=cbd_b)
        p.barrier()

        with contextlib.ExitStack() as es2:
            E2 = es2.enter_context
            wtmkv_sb = E2(nc.sbuf_tensor("wtmkv_sb", [128, NCH * 2048], BF16))
            stg2 = E2(nc.sbuf_tensor("stgkv", [128, 2048], F32))
            for c in range(NCH):
                p.op("sync", (lambda c: lambda e: e.dma_start(out=stg2[:], in_=wtmkv_in[c * 128:(c + 1) * 128, :]))(c),
                     writes=[stg2], dma=True, dkey=stg2)
                p.op("gpsimd", (lambda c: lambda e: e.tensor_copy(wtmkv_sb[:, c * 2048:(c + 1) * 2048], stg2[:]))(c),
                     reads=[stg2], writes=[wtmkv_sb])

            xt = [E2(nc.sbuf_tensor(f"xt{c}", [128, 1 + TCS], F32)) for c in range(NCH)]
            xb16L = [E2(nc.sbuf_tensor(f"xb16_{j}", [128, TCS], BF16)) for j in range(2)]
            xsqL = [E2(nc.sbuf_tensor(f"xsq{j}", [128, TCS], BF16)) for j in range(2)]
            xn1b = [E2(nc.sbuf_tensor(f"xn1b{c}", [128, TCS], BF16)) for c in range(NCH)]
            mtile = E2(nc.sbuf_tensor("mtile", [128, TCS], F32))
            rstdt = E2(nc.sbuf_tensor("rstdt", [128, TCS], F32))
            vart = E2(nc.sbuf_tensor("vart", [128, TCS], F32))
            msqt = E2(nc.sbuf_tensor("msqt", [128, TCS], F32))
            dscL = [E2(nc.sbuf_tensor(f"dsc{j}", [128, TCS], F32)) for j in range(2)]
            kvevL = [E2(nc.sbuf_tensor(f"kvev{j}", [128, TCS], F32)) for j in range(4)]
            xppL = [E2(nc.sbuf_tensor(f"xpp{j}", [128, TCS], F32)) for j in range(2)]
            ps_s = E2(nc.psum_tensor("ps_s", [128, TCS], F32))
            ps_q = E2(nc.psum_tensor("ps_q", [128, TCS], F32))
            ps_kv = [E2(nc.psum_tensor(f"ps_kv{i}", [128, TCS], F32)) for i in range(2)]

            for tc in range(NTCS):
                t0 = tc * TCS
                for c in range(NCH):
                    if tc == 0:
                        p.op("sync", (lambda c: lambda e: e.dma_start(out=xt[c][:, 1:1 + TCS], in_=xT[c * 128:(c + 1) * 128, 0:TCS]))(c),
                             writes=[xt[c]], dma=True, dkey=xt[c])
                        p.op("sync", (lambda c: lambda e: e.dma_start(out=xt[c][:, 0:1], in_=xT[c * 128:(c + 1) * 128, 0:1], allow_slow_non_contiguous=True))(c),
                             writes=[xt[c]], dma=True, dkey=xt[c])
                    else:
                        p.op("sync", (lambda c, t0: lambda e: e.dma_start(out=xt[c][:], in_=xT[c * 128:(c + 1) * 128, t0 - 1:t0 + TCS]))(c, t0),
                             writes=[xt[c]], dma=True, dkey=xt[c])
                for c in range(NCH):
                    xb = xb16L[c % 2]
                    p.op("scalar", (lambda c, xb: lambda e: e.activation(xb[:], xt[c][:, 1:1 + TCS], AF.Copy))(c, xb),
                         reads=[xt[c]], writes=[xb])
                    p.op("tensor", (lambda c, xb: lambda e: e.matmul(ps_s[:], ones_b[:], xb[:], start=(c == 0), stop=(c == NCH - 1)))(c, xb),
                         reads=[ones_b, xb], writes=[ps_s])
                for c in range(NCH):
                    xq = xsqL[c % 2]
                    p.op("scalar", (lambda c, xq: lambda e: e.activation(xq[:], xt[c][:, 1:1 + TCS], AF.Square))(c, xq),
                         reads=[xt[c]], writes=[xq])
                    p.op("tensor", (lambda c, xq: lambda e: e.matmul(ps_q[:], ones_b[:], xq[:], start=(c == 0), stop=(c == NCH - 1)))(c, xq),
                         reads=[ones_b, xq], writes=[ps_q])
                p.op("vector", lambda e: e.tensor_scalar(mtile[:], ps_s[:], 1.0 / C, None, ALU.mult), reads=[ps_s], writes=[mtile])
                p.op("scalar", lambda e: e.activation(msqt[:], mtile[:], AF.Square), reads=[mtile], writes=[msqt])
                p.op("vector", lambda e: e.scalar_tensor_tensor(vart[:], ps_q[:], 1.0 / C, msqt[:], ALU.mult, ALU.subtract),
                     reads=[ps_q, msqt], writes=[vart])
                p.op("scalar", lambda e: e.activation(vart[:], vart[:], AF.Ln, bias=epsT[:]), reads=[vart, epsT], writes=[vart])
                p.op("scalar", lambda e: e.activation(rstdt[:], vart[:], AF.Exp, scale=-0.5), reads=[vart], writes=[rstdt])
                for c in range(NCH):
                    dsc = dscL[c % 2]
                    p.op("gpsimd", (lambda c, dsc: lambda e: e.tensor_sub(dsc[:], xt[c][:, 1:1 + TCS], mtile[:]))(c, dsc),
                         reads=[xt[c], mtile], writes=[dsc])
                    p.op("vector", (lambda c, dsc: lambda e: e.scalar_tensor_tensor(dsc[:], dsc[:], vs(vt["vg1"], c), rstdt[:], ALU.mult, ALU.mult))(c, dsc),
                         reads=[dsc, rstdt, vt["vg1"]], writes=[dsc])
                    p.op("vector", (lambda c, dsc: lambda e: e.tensor_scalar(xn1b[c][:], dsc[:], vs(vt["vb1"], c), None, ALU.add))(c, dsc),
                         reads=[dsc, vt["vb1"]], writes=[xn1b[c]])
                for m in range(16):
                    pkv = ps_kv[m % 2]

                    def mmgrp(e, m=m, pkv=pkv):
                        for c in range(NCH):
                            inst = e.matmul(pkv[:], wtmkv_sb[:, c * 2048 + m * 128:c * 2048 + (m + 1) * 128],
                                            xn1b[c][:], start=(c == 0), stop=(c == NCH - 1))
                        return inst
                    p.op("tensor", mmgrp, reads=[wtmkv_sb] + [xn1b[c] for c in range(NCH)], writes=[pkv])
                    isk = m < 8
                    cc = m % 8
                    mixv = vt["vtmk"] if isk else vt["vtmv"]
                    omv = om["vtmk"] if isk else om["vtmv"]
                    kvt = kvevL[m % 4]
                    xpp = xppL[m % 2]
                    dstbuf = kbuf if isk else vbuf
                    p.op("gpsimd", (lambda cc, omv, xpp: lambda e: e.tensor_scalar(xpp[:], xt[cc][:, 0:TCS], vs(omv, cc), None, ALU.mult))(cc, omv, xpp),
                         reads=[xt[cc], omv], writes=[xpp])
                    p.op("vector", (lambda pkv, mixv, cc, kvt, xpp: lambda e: e.scalar_tensor_tensor(kvt[:], pkv[:], vs(mixv, cc), xpp[:], ALU.mult, ALU.add))(pkv, mixv, cc, kvt, xpp),
                         reads=[pkv, mixv, xpp], writes=[kvt])
                    p.op("gpsimd", (lambda dstbuf, cc, t0, kvt: lambda e: e.dma_start(out=dstbuf[cc * 128:(cc + 1) * 128, t0:t0 + TCS], in_=kvt[:]))(dstbuf, cc, t0, kvt),
                         reads=[kvt], dma=True, dkey=kvt, dram_out=dstbuf)

            wbc = E2(nc.sbuf_tensor("wbc", [128, NCH * LW], F32))
            for c in range(NCH):
                p.op("scalar", (lambda c: lambda e: e.memzero(wbc[:, c * LW:(c + 1) * LW]))(c), writes=[wbc])
            for c in range(NCH):
                p.op("vector", (lambda c: lambda e: e.tensor_scalar(wbc[:, c * LW:(c + 1) * LW], wbc[:, c * LW:(c + 1) * LW], vs(vt["vw"], c), None, ALU.add))(c),
                     reads=[vt["vw"]], writes=[wbc])

            def mkset(j):
                return {nm: E2(nc.sbuf_tensor(f"wkv{nm}{j}", [128, 1 + LW], F32)) for nm in ("mT", "SgH", "Sg", "aaT", "sT")} |                        {nm: E2(nc.sbuf_tensor(f"wkv{nm}{j}", [128, LW], F32)) for nm in ("KT", "VT", "w1", "w2", "w3", "w4", "w5", "w6", "yt")}
            sets = [mkset(0), mkset(1)]

            unit = 0
            for wt in range(NLW):
                t0 = wt * LW
                for c in range(NCH):
                    S = sets[unit % 2]
                    unit += 1
                    mT, SgH, Sg, aaT, sT = S["mT"], S["SgH"], S["Sg"], S["aaT"], S["sT"]
                    KT, VT = S["KT"], S["VT"]
                    w1, w2, w3, w4, w5, w6, yt = S["w1"], S["w2"], S["w3"], S["w4"], S["w5"], S["w6"], S["yt"]
                    car = carryL[c]
                    wv_c = vs(vt["vw"], c)
                    wu_c = vs(vt["vu"], c)
                    wbc_c = wbc[:, c * LW:(c + 1) * LW]
                    p.op("sync", (lambda c, t0, KT: lambda e: e.dma_start(out=KT[:], in_=kbuf[c * 128:(c + 1) * 128, t0:t0 + LW]))(c, t0, KT),
                         writes=[KT], dma=True, dkey=KT, dram_in=kbuf)
                    p.op("sync", (lambda c, t0, VT: lambda e: e.dma_start(out=VT[:], in_=vbuf[c * 128:(c + 1) * 128, t0:t0 + LW]))(c, t0, VT),
                         writes=[VT], dma=True, dkey=VT, dram_in=vbuf)
                    p.op("gpsimd", (lambda car, mT: lambda e: e.tensor_copy(mT[:, 0:1], car[:, 0:1]))(car, mT), reads=[car], writes=[mT])
                    p.op("vector", (lambda wbc_c, car, mT, KT: lambda e: e.tensor_tensor_scan(mT[:, 1:1 + LW], wbc_c, KT[:], car[:, 0:1], ALU.add, ALU.max))(wbc_c, car, mT, KT),
                         reads=[wbc, KT, car], writes=[mT])
                    p.op("vector", (lambda wv_c, mT, w1: lambda e: e.scalar_tensor_tensor(w1[:], mT[:, 0:LW], wv_c, mT[:, 1:1 + LW], ALU.add, ALU.subtract))(wv_c, mT, w1),
                         reads=[mT, vt["vw"]], writes=[w1])
                    p.op("scalar", (lambda w1: lambda e: e.activation(w1[:], w1[:], AF.Exp))(w1), reads=[w1], writes=[w1])
                    p.op("gpsimd", (lambda KT, mT, w2: lambda e: e.tensor_sub(w2[:], KT[:], mT[:, 1:1 + LW]))(KT, mT, w2), reads=[KT, mT], writes=[w2])
                    p.op("scalar", (lambda w2: lambda e: e.activation(w2[:], w2[:], AF.Exp))(w2), reads=[w2], writes=[w2])
                    p.op("gpsimd", (lambda car, SgH: lambda e: e.tensor_copy(SgH[:, 0:1], car[:, 1:2]))(car, SgH), reads=[car], writes=[SgH])
                    p.op("vector", (lambda car, SgH, w1, w2: lambda e: e.tensor_tensor_scan(SgH[:, 1:1 + LW], w1[:], w2[:], car[:, 1:2], ALU.mult, ALU.add))(car, SgH, w1, w2),
                         reads=[w1, w2, car], writes=[SgH])
                    p.op("gpsimd", (lambda w3, w1, SgH: lambda e: e.tensor_mul(w3[:], w1[:], SgH[:, 0:LW]))(w3, w1, SgH), reads=[w1, SgH], writes=[w3])
                    p.op("vector", (lambda w3, w2: lambda e: e.tensor_max(w3[:], w3[:], w2[:]))(w3, w2), reads=[w3, w2], writes=[w3])
                    p.op("vector", (lambda w3, w2: lambda e: e.scalar_tensor_tensor(w3[:], w3[:], WKV_EPS, w2[:], ALU.mult, ALU.add))(w3, w2),
                         reads=[w3, w2], writes=[w3])
                    p.op("gpsimd", (lambda car, Sg: lambda e: e.tensor_copy(Sg[:, 0:1], car[:, 2:3]))(car, Sg), reads=[car], writes=[Sg])
                    p.op("vector", (lambda car, Sg, w1, w3: lambda e: e.tensor_tensor_scan(Sg[:, 1:1 + LW], w1[:], w3[:], car[:, 2:3], ALU.mult, ALU.add))(car, Sg, w1, w3),
                         reads=[w1, w3, car], writes=[Sg])
                    p.op("scalar", (lambda Sg, sT: lambda e: e.activation(sT[:], Sg[:], AF.Ln))(Sg, sT), reads=[Sg], writes=[sT])
                    p.op("gpsimd", (lambda sT, mT: lambda e: e.tensor_add(sT[:], mT[:], sT[:]))(sT, mT), reads=[mT, sT], writes=[sT])
                    p.op("vector", (lambda wv_c, sT, KT, w4: lambda e: e.scalar_tensor_tensor(w4[:], sT[:, 0:LW], wv_c, KT[:], ALU.add, ALU.subtract))(wv_c, sT, KT, w4),
                         reads=[sT, vt["vw"], KT], writes=[w4])
                    p.op("gpsimd", (lambda w5, w4: lambda e: e.tensor_scalar_min(w5[:], w4[:], 0.0))(w5, w4), reads=[w4], writes=[w5])
                    p.op("scalar", (lambda w5: lambda e: e.activation(w5[:], w5[:], AF.Exp))(w5), reads=[w5], writes=[w5])
                    p.op("gpsimd", (lambda w4: lambda e: e.tensor_scalar_max(w4[:], w4[:], 0.0))(w4), reads=[w4], writes=[w4])
                    p.op("scalar", (lambda w4: lambda e: e.activation(w4[:], w4[:], AF.Exp, scale=-1.0))(w4), reads=[w4], writes=[w4])
                    p.op("gpsimd", (lambda w4, VT: lambda e: e.tensor_mul(w4[:], w4[:], VT[:]))(w4, VT), reads=[w4, VT], writes=[w4])
                    p.op("gpsimd", (lambda car, aaT: lambda e: e.tensor_copy(aaT[:, 0:1], car[:, 3:4]))(car, aaT), reads=[car], writes=[aaT])
                    p.op("vector", (lambda car, aaT, w5, w4: lambda e: e.tensor_tensor_scan(aaT[:, 1:1 + LW], w5[:], w4[:], car[:, 3:4], ALU.mult, ALU.add))(car, aaT, w5, w4),
                         reads=[w5, w4, car], writes=[aaT])
                    p.op("gpsimd", (lambda car, mT: lambda e: e.tensor_copy(car[:, 0:1], mT[:, LW:1 + LW]))(car, mT), reads=[mT], writes=[car])
                    p.op("gpsimd", (lambda car, SgH: lambda e: e.tensor_copy(car[:, 1:2], SgH[:, LW:1 + LW]))(car, SgH), reads=[SgH], writes=[car])
                    p.op("gpsimd", (lambda car, Sg: lambda e: e.tensor_copy(car[:, 2:3], Sg[:, LW:1 + LW]))(car, Sg), reads=[Sg], writes=[car])
                    p.op("gpsimd", (lambda car, aaT: lambda e: e.tensor_copy(car[:, 3:4], aaT[:, LW:1 + LW]))(car, aaT), reads=[aaT], writes=[car])
                    p.op("vector", (lambda wu_c, KT, sT, w6: lambda e: e.scalar_tensor_tensor(w6[:], KT[:], wu_c, sT[:, 0:LW], ALU.add, ALU.subtract))(wu_c, KT, sT, w6),
                         reads=[KT, vt["vu"], sT], writes=[w6])
                    p.op("gpsimd", (lambda w1, w6: lambda e: e.tensor_scalar_min(w1[:], w6[:], 0.0))(w1, w6), reads=[w6], writes=[w1])
                    p.op("scalar", (lambda w1: lambda e: e.activation(w1[:], w1[:], AF.Exp))(w1), reads=[w1], writes=[w1])
                    p.op("gpsimd", (lambda w6: lambda e: e.tensor_scalar_max(w6[:], w6[:], 0.0))(w6), reads=[w6], writes=[w6])
                    p.op("scalar", (lambda w6: lambda e: e.activation(w6[:], w6[:], AF.Exp, scale=-1.0))(w6), reads=[w6], writes=[w6])
                    p.op("gpsimd", (lambda w2, w6, aaT: lambda e: e.tensor_mul(w2[:], w6[:], aaT[:, 0:LW]))(w2, w6, aaT), reads=[w6, aaT], writes=[w2])
                    p.op("vector", (lambda w3, w1, VT: lambda e: e.tensor_mul(w3[:], w1[:], VT[:]))(w3, w1, VT), reads=[w1, VT], writes=[w3])
                    p.op("vector", (lambda w2, w3: lambda e: e.tensor_add(w2[:], w2[:], w3[:]))(w2, w3), reads=[w2, w3], writes=[w2])
                    p.op("vector", (lambda w3, w6, w1: lambda e: e.scalar_tensor_tensor(w3[:], w6[:], WKV_EPS, w1[:], ALU.add, ALU.add))(w3, w6, w1),
                         reads=[w6, w1], writes=[w3])
                    p.op("scalar", (lambda w3: lambda e: e.activation(w3[:], w3[:], AF.Ln))(w3), reads=[w3], writes=[w3])
                    p.op("scalar", (lambda w3: lambda e: e.activation(w3[:], w3[:], AF.Exp, scale=-1.0))(w3), reads=[w3], writes=[w3])
                    p.op("vector", (lambda yt, w2, w3: lambda e: e.tensor_mul(yt[:], w2[:], w3[:]))(yt, w2, w3), reads=[w2, w3], writes=[yt])
                    p.op("gpsimd", (lambda c, t0, yt: lambda e: e.dma_start(out=ybuf[c * 128:(c + 1) * 128, t0:t0 + LW], in_=yt[:]))(c, t0, yt),
                         reads=[yt], dma=True, dkey=yt, dram_out=ybuf)
        p.barrier()

        with contextlib.ExitStack() as es3:
            E3 = es3.enter_context
            bufA = [E3(nc.sbuf_tensor(f"bufA{c}", [128, SW], F32)) for c in range(NCH)]
            bufB = [E3(nc.sbuf_tensor(f"bufB{c}", [128, SW], F32)) for c in range(NCH)]
            bufC = [E3(nc.sbuf_tensor(f"bufC{c}", [128, SW], F32)) for c in range(NCH)]
            bufD = [E3(nc.sbuf_tensor(f"bufD{c}", [128, SW], F32)) for c in range(NCH)]
            bufE = [E3(nc.sbuf_tensor(f"bufE{c}", [128, SW], F32)) for c in range(NCH)]
            x2d = [E3(nc.sbuf_tensor(f"x2d{c}", [128, SW], F32)) for c in range(NCH)]
            pool = [E3(nc.sbuf_tensor(f"pool{i}", [128, SW], BF16)) for i in range(32)]
            nb16 = [E3(nc.sbuf_tensor(f"nb16_{c}", [128, SLOT], BF16)) for c in range(NCH)]
            ycn = [E3(nc.sbuf_tensor(f"ycn{j}", [128, 544], F32)) for j in range(4)]
            dscA = [E3(nc.sbuf_tensor(f"dscA{j}", [128, SW], F32)) for j in range(2)]
            cbt2 = E3(nc.sbuf_tensor("cbt2", [128, 4096], BF16))
            mtile2 = E3(nc.sbuf_tensor("mtile2", [128, SW], F32))
            rstdt2 = E3(nc.sbuf_tensor("rstdt2", [128, SW], F32))
            vart2 = E3(nc.sbuf_tensor("vart2", [128, SW], F32))
            msqt2 = E3(nc.sbuf_tensor("msqt2", [128, SW], F32))
            dsc2 = E3(nc.sbuf_tensor("dsc2", [128, SW], F32))
            dsc3 = E3(nc.sbuf_tensor("dsc3", [128, SW], F32))
            xn2f = E3(nc.sbuf_tensor("xn2f", [128, SLOT], F32))
            xb16pL = [E3(nc.sbuf_tensor(f"xb16p{j}", [128, SW], BF16)) for j in range(2)]
            xsqpL = [E3(nc.sbuf_tensor(f"xsqp{j}", [128, SW], BF16)) for j in range(2)]
            wsmL = [E3(nc.sbuf_tensor(f"wsm{j}", [128, 1024], BF16)) for j in range(2)]
            wbigL = [E3(nc.sbuf_tensor(f"wbig{j}", [128, 4096], BF16)) for j in range(2)]
            outt = E3(nc.sbuf_tensor("outt", [128, SLOT], F32))
            ps_a = E3(nc.psum_tensor("ps_a", [128, SW], F32))
            ps_b = E3(nc.psum_tensor("ps_b", [128, SW], F32))
            ps_c = E3(nc.psum_tensor("ps_c", [128, SLOT], F32))
            ps_d = E3(nc.psum_tensor("ps_d", [128, SLOT], F32))
            ps_n1 = E3(nc.psum_tensor("ps_n1", [128, 1], F32))

            wtmr_r = wtmr_b.rearrange("(m p) f -> m p f", p=128)
            wcm_r = wcm_b.rearrange("(m p) f -> m p f", p=128)
            wcp_r = wcp_b.rearrange("(o p) f -> o p f", p=128)
            wpj_r = wpj_b.rearrange("(g p) f -> g p f", p=128)

            def ln_stats(src_tiles, width, off):
                pieces = [(0, min(width, SLOT))]
                if width > SLOT:
                    pieces.append((SLOT, width))
                for (ps, actf, scrL) in ((ps_a, AF.Copy, xb16pL), (ps_b, AF.Square, xsqpL)):
                    for lo, hi in pieces:
                        for c in range(NCH):
                            scratch = scrL[c % 2]
                            p.op("scalar", (lambda c, lo, hi, actf, scratch: lambda e: e.activation(
                                scratch[:, lo:hi], src_tiles[c][:, off + lo:off + hi], actf))(c, lo, hi, actf, scratch),
                                 reads=[src_tiles[c]], writes=[scratch])
                            p.op("tensor", (lambda c, lo, hi, ps, scratch: lambda e: e.matmul(
                                ps[:, lo:hi], ones_b[:], scratch[:, lo:hi],
                                start=(c == 0), stop=(c == NCH - 1)))(c, lo, hi, ps, scratch),
                                 reads=[ones_b, scratch], writes=[ps])
                p.op("vector", lambda e: e.tensor_scalar(mtile2[:, 0:width], ps_a[:, 0:width], 1.0 / C, None, ALU.mult),
                     reads=[ps_a], writes=[mtile2])
                p.op("scalar", lambda e: e.activation(msqt2[:, 0:width], mtile2[:, 0:width], AF.Square),
                     reads=[mtile2], writes=[msqt2])
                p.op("vector", lambda e: e.scalar_tensor_tensor(vart2[:, 0:width], ps_b[:, 0:width], 1.0 / C, msqt2[:, 0:width], ALU.mult, ALU.subtract),
                     reads=[ps_b, msqt2], writes=[vart2])
                p.op("scalar", lambda e: e.activation(vart2[:, 0:width], vart2[:, 0:width], AF.Ln, bias=epsT[:]),
                     reads=[vart2, epsT], writes=[vart2])
                p.op("scalar", lambda e: e.activation(rstdt2[:, 0:width], vart2[:, 0:width], AF.Exp, scale=-0.5),
                     reads=[vart2], writes=[rstdt2])

            import os as _os
            if _os.environ.get("KDBG2"):
                dbg_early = dram("dbg_early", [C, T], kind="ExternalOutput")
                dearly = E3(nc.sbuf_tensor("dearly", [128, 1024], F32))
                for c in range(NCH):
                    for tt_ in range(4):
                        p.op("sync", (lambda c, tt_: lambda e: e.dma_start(out=dearly[:], in_=ybuf[c * 128:(c + 1) * 128, tt_ * 1024:(tt_ + 1) * 1024]))(c, tt_),
                             writes=[dearly], dma=True, dkey=dearly, dram_in=ybuf)
                        p.op("gpsimd", (lambda c, tt_: lambda e: e.dma_start(out=dbg_early[c * 128:(c + 1) * 128, tt_ * 1024:(tt_ + 1) * 1024], in_=dearly[:]))(c, tt_),
                             reads=[dearly], dma=True, dkey=dearly)
                dbg_x2 = dram("dbg_x2", [C, NSLOT * SW], kind="ExternalOutput")
                dbg_yb = dram("dbg_yb", [C, NSLOT * SW], kind="ExternalOutput")
                dbg_rt = dram("dbg_rt", [C, NSLOT * SW], kind="ExternalOutput")
                dbg_x1 = dram("dbg_x1", [C, NSLOT * SLOT], kind="ExternalOutput")
                dbg_fb = dram("dbg_fb", [C, NSLOT * SLOT], kind="ExternalOutput")
            for i in range(NSLOT):
                xe = bufD
                xpr = bufA
                yb = bufB
                rt = bufC
                x2 = bufE
                for c in range(NCH):
                    p.op("sync", (lambda c, i: lambda e: e.dma_start(out=bufD[c][:], in_=xe_in[c * 128:(c + 1) * 128, i * SW:(i + 1) * SW]))(c, i),
                         writes=[bufD[c]], dma=True, dkey=bufD[c])
                    p.op("sync", (lambda c, i: lambda e: e.dma_start(out=bufA[c][:], in_=xpr_in[c * 128:(c + 1) * 128, i * SW:(i + 1) * SW]))(c, i),
                         writes=[bufA[c]], dma=True, dkey=bufA[c])
                for c in range(NCH):
                    s0 = (2 * i) * 512
                    s1 = (2 * i + 1) * 512
                    y0t = ycn[(c % 2) * 2]
                    y1t = ycn[(c % 2) * 2 + 1]
                    y0 = y0t[:, 0:SW]
                    y1 = y1t[:, 0:SW]
                    if i == 0:
                        p.op("sync", (lambda c, y0: lambda e: e.dma_start(out=y0[:, 0:1], in_=ybuf[c * 128:(c + 1) * 128, T - 1:T], allow_slow_non_contiguous=True))(c, y0),
                             writes=[y0], dma=True, dkey=y0t, dram_in=ybuf)
                        p.op("sync", (lambda c, y0: lambda e: e.dma_start(out=y0[:, 1:SW], in_=ybuf[c * 128:(c + 1) * 128, 0:512]))(c, y0),
                             writes=[y0], dma=True, dkey=y0t, dram_in=ybuf)
                    else:
                        p.op("sync", (lambda c, s0, y0: lambda e: e.dma_start(out=y0, in_=ybuf[c * 128:(c + 1) * 128, s0 - 1:s0 + 512]))(c, s0, y0),
                             writes=[y0], dma=True, dkey=y0t, dram_in=ybuf)
                    p.op("sync", (lambda c, s1, y1: lambda e: e.dma_start(out=y1, in_=ybuf[c * 128:(c + 1) * 128, s1 - 1:s1 + 512]))(c, s1, y1),
                         writes=[y1], dma=True, dkey=y1t, dram_in=ybuf)
                    p.op("vector", (lambda c, y1: lambda e: e.tensor_scalar(bufB[c][:], y1, maskt[:], None, ALU.mult))(c, y1),
                         reads=[y1, maskt], writes=[bufB[c]])
                    p.op("vector", (lambda c, y0: lambda e: e.scalar_tensor_tensor(bufB[c][:], y0, maskom[:], bufB[c][:], ALU.mult, ALU.add))(c, y0),
                         reads=[bufB[c], maskom, y0], writes=[bufB[c]])
                ln_stats(xe, SW, 0)
                for c in range(NCH):
                    ds = dscA[c % 2]
                    p.op("gpsimd", (lambda c, ds: lambda e: e.tensor_sub(ds[:], xe[c][:], mtile2[:]))(c, ds),
                         reads=[xe[c], mtile2], writes=[ds])
                    p.op("vector", (lambda c, ds: lambda e: e.scalar_tensor_tensor(ds[:], ds[:], vs(vt["vg1"], c), rstdt2[:], ALU.mult, ALU.mult))(c, ds),
                         reads=[ds, rstdt2, vt["vg1"]], writes=[ds])
                    p.op("vector", (lambda c, ds: lambda e: e.tensor_scalar(pool[c][:], ds[:], vs(vt["vb1"], c), None, ALU.add))(c, ds),
                         reads=[ds, vt["vb1"]], writes=[pool[c]])
                psr = [ps_c, ps_d]
                for m in range(NCH):
                    wbt = wsmL[m % 2]
                    wb = wbt[:]
                    pr = psr[m % 2]
                    p.op("sync", (lambda m, wb: lambda e: e.dma_start(out=wb, in_=wtmr_r[m]))(m, wb),
                         writes=[wbt], dma=True, dkey=wbt, dram_in=wtmr_b)

                    def rgrp(e, wb=wb, pr=pr):
                        for c in range(NCH):
                            inst = e.matmul(pr[:], wb[:, c * 128:(c + 1) * 128],
                                            pool[c][:, 0:SLOT], start=(c == 0), stop=(c == NCH - 1))
                        return inst
                    p.op("tensor", rgrp, reads=[wbt] + [pool[c] for c in range(NCH)], writes=[pr])

                    def rgrp1(e, wb=wb):
                        for c in range(NCH):
                            inst = e.matmul(ps_n1[:], wb[:, c * 128:(c + 1) * 128],
                                            pool[c][:, SLOT:SW], start=(c == 0), stop=(c == NCH - 1))
                        return inst
                    p.op("tensor", rgrp1, reads=[wbt] + [pool[c] for c in range(NCH)], writes=[ps_n1])
                    ds = dscA[m % 2]
                    p.op("gpsimd", (lambda m, ds: lambda e: e.tensor_scalar(ds[:], xpr[m][:], vs(om["vtmr"], m), None, ALU.mult))(m, ds),
                         reads=[xpr[m], om["vtmr"]], writes=[ds])
                    p.op("vector", (lambda m, ds, pr: lambda e: e.scalar_tensor_tensor(rt[m][:, 0:SLOT], pr[:], vs(vt["vtmr"], m), ds[:, 0:SLOT], ALU.mult, ALU.add))(m, ds, pr),
                         reads=[pr, vt["vtmr"], ds], writes=[rt[m]])
                    p.op("vector", (lambda m, ds: lambda e: e.scalar_tensor_tensor(rt[m][:, SLOT:SW], ps_n1[:], vs(vt["vtmr"], m), ds[:, SLOT:SW], ALU.mult, ALU.add))(m, ds),
                         reads=[ps_n1, vt["vtmr"], ds], writes=[rt[m]])
                    p.op("scalar", (lambda m: lambda e: e.activation(rt[m][:], rt[m][:], AF.Sigmoid))(m),
                         reads=[rt[m]], writes=[rt[m]])
                for c in range(NCH):
                    p.op("gpsimd", (lambda c: lambda e: e.tensor_mul(x2[c][:], rt[c][:], yb[c][:]))(c),
                         reads=[rt[c], yb[c]], writes=[x2[c]])
                    p.op("vector", (lambda c: lambda e: e.tensor_add(x2[c][:], x2[c][:], xe[c][:]))(c),
                         reads=[x2[c], xe[c]], writes=[x2[c]])
                    p.op("gpsimd", (lambda c: lambda e: e.tensor_add(x2d[c][:], x2[c][:], xe[c][:]))(c),
                         reads=[x2[c], xe[c]], writes=[x2d[c]])
                import os as _os
                if _os.environ.get("KDBG2"):
                    for c in range(NCH):
                        p.op("gpsimd", (lambda c, i: lambda e: e.dma_start(out=dbg_x2[c * 128:(c + 1) * 128, i * SW:(i + 1) * SW], in_=x2[c][:]))(c, i),
                             reads=[x2[c]], dma=True, dkey=x2[c])
                        p.op("gpsimd", (lambda c, i: lambda e: e.dma_start(out=dbg_yb[c * 128:(c + 1) * 128, i * SW:(i + 1) * SW], in_=bufB[c][:]))(c, i),
                             reads=[bufB[c]], dma=True, dkey=bufB[c])
                        p.op("gpsimd", (lambda c, i: lambda e: e.dma_start(out=dbg_rt[c * 128:(c + 1) * 128, i * SW:(i + 1) * SW], in_=rt[c][:]))(c, i),
                             reads=[rt[c]], dma=True, dkey=rt[c])
                ln_stats(x2, SLOT, 1)
                xppre = bufA
                rr = bufB
                xn2fL = [xn2f, dsc2]
                for c in range(NCH):
                    ds = dscA[c % 2]
                    xf = xn2fL[c % 2]
                    p.op("gpsimd", (lambda c, ds: lambda e: e.tensor_sub(ds[:, 0:SLOT], x2[c][:, 1:SW], mtile2[:, 0:SLOT]))(c, ds),
                         reads=[x2[c], mtile2], writes=[ds])
                    p.op("vector", (lambda c, ds, xf: lambda e: e.scalar_tensor_tensor(xf[:, 0:SLOT], ds[:, 0:SLOT], vs(vt["vg2"], c), rstdt2[:, 0:SLOT], ALU.mult, ALU.mult))(c, ds, xf),
                         reads=[ds, rstdt2, vt["vg2"]], writes=[xf])
                    p.op("vector", (lambda c, xf: lambda e: e.tensor_scalar(xf[:, 0:SLOT], xf[:, 0:SLOT], vs(vt["vb2"], c), None, ALU.add))(c, xf),
                         reads=[xf, vt["vb2"]], writes=[xf])
                    p.op("scalar", (lambda c, xf: lambda e: e.activation(nb16[c][:], xf[:, 0:SLOT], AF.Copy))(c, xf),
                         reads=[xf], writes=[nb16[c]])
                    p.op("gpsimd", (lambda c, ds: lambda e: e.tensor_scalar(ds[:, 0:SLOT], x2[c][:, 0:SLOT], vs(om["vcmr"], c), None, ALU.mult))(c, ds),
                         reads=[x2[c], om["vcmr"]], writes=[ds])
                    p.op("vector", (lambda c, ds, xf: lambda e: e.scalar_tensor_tensor(rr[c][:, 0:SLOT], xf[:, 0:SLOT], vs(vt["vcmr"], c), ds[:, 0:SLOT], ALU.mult, ALU.add))(c, ds, xf),
                         reads=[xf, vt["vcmr"], ds], writes=[rr[c]])
                    p.op("scalar", (lambda c: lambda e: e.activation(rr[c][:, 0:SLOT], rr[c][:, 0:SLOT], AF.Sigmoid))(c),
                         reads=[rr[c]], writes=[rr[c]])
                    p.op("gpsimd", (lambda c: lambda e: e.tensor_scalar(xppre[c][:, 0:SLOT], x2[c][:, 0:SLOT], vs(om["vcmk"], c), None, ALU.mult))(c),
                         reads=[x2[c], om["vcmk"]], writes=[xppre[c]])
                for m in range(32):
                    wbt = wsmL[m % 2]
                    wb = wbt[:]
                    p.op("sync", (lambda m, wb: lambda e: e.dma_start(out=wb, in_=wcm_r[m]))(m, wb),
                         writes=[wbt], dma=True, dkey=wbt, dram_in=wcm_b)
                    pk = ps_c if (m % 2 == 0) else ps_d

                    def cmgrp(e, wb=wb, pk=pk):
                        for c in range(NCH):
                            inst = e.matmul(pk[:], wb[:, c * 128:(c + 1) * 128],
                                            nb16[c][:], start=(c == 0), stop=(c == NCH - 1))
                        return inst
                    p.op("tensor", cmgrp, reads=[wbt] + [nb16[c] for c in range(NCH)], writes=[pk])
                    cc = m % 8
                    ds = dscA[m % 2]
                    p.op("vector", (lambda cc, pk, ds: lambda e: e.scalar_tensor_tensor(ds[:, 0:SLOT], pk[:], vs(vt["vcmk"], cc), xppre[cc][:, 0:SLOT], ALU.mult, ALU.add))(cc, pk, ds),
                         reads=[pk, vt["vcmk"], xppre[cc]], writes=[ds])
                    p.op("vector", (lambda m, ds: lambda e: e.scalar_tensor_tensor(pool[m][:, 0:SLOT], ds[:, 0:SLOT], 0.0, ds[:, 0:SLOT], ALU.max, ALU.mult))(m, ds),
                         reads=[ds], writes=[pool[m]])
                for o in range(NCH):
                    wbt = wbigL[o % 2]
                    wb = wbt[:]
                    p.op("sync", (lambda o, wb: lambda e: e.dma_start(out=wb, in_=wcp_r[o]))(o, wb),
                         writes=[wbt], dma=True, dkey=wbt, dram_in=wcp_b)
                    pk = ps_c if (o % 2 == 0) else ps_d

                    def cpgrp(e, wb=wb, pk=pk):
                        for k in range(32):
                            inst = e.matmul(pk[:], wb[:, k * 128:(k + 1) * 128],
                                            pool[k][:, 0:SLOT], start=(k == 0), stop=(k == 31))
                        return inst
                    p.op("tensor", cpgrp, reads=[wbt] + [pool[k] for k in range(32)], writes=[pk])
                    ds = dscA[o % 2]
                    p.op("vector", (lambda o, pk, ds: lambda e: e.tensor_mul(ds[:, 0:SLOT], pk[:], rr[o][:, 0:SLOT]))(o, pk, ds),
                         reads=[pk, rr[o]], writes=[ds])
                    p.op("vector", (lambda o, ds: lambda e: e.tensor_add(x2d[o][:, 1:SW], x2d[o][:, 1:SW], ds[:, 0:SLOT]))(o, ds),
                         reads=[x2d[o], ds], writes=[x2d[o]])
                ln_stats(x2d, SLOT, 1)
                x1f = bufC
                x1b = nb16
                for c in range(NCH):
                    p.op("gpsimd", (lambda c: lambda e: e.tensor_sub(x1f[c][:, 0:SLOT], x2d[c][:, 1:SW], mtile2[:, 0:SLOT]))(c),
                         reads=[x2d[c], mtile2], writes=[x1f[c]])
                    p.op("vector", (lambda c: lambda e: e.tensor_mul(x1f[c][:, 0:SLOT], x1f[c][:, 0:SLOT], rstdt2[:, 0:SLOT]))(c),
                         reads=[x1f[c], rstdt2], writes=[x1f[c]])
                    p.op("scalar", (lambda c: lambda e: e.activation(x1b[c][:], x1f[c][:, 0:SLOT], AF.Copy))(c),
                         reads=[x1f[c]], writes=[x1b[c]])
                faccA = bufD
                faccB = bufE
                for kg in range(4):
                    cbt_ = cbt2
                    cb = cbt_[:]
                    p.op("sync", (lambda kg, cb: lambda e: e.dma_start(out=cb, in_=cbd_b[kg * 128:(kg + 1) * 128, :]))(kg, cb),
                         writes=[cbt_], dma=True, dkey=cbt_, dram_in=cbd_b)
                    for kl in range(32):
                        k = kg * 32 + kl
                        c = (k // 4) % 8
                        pz = ps_c if (kl % 2 == 0) else ps_d
                        p.op("tensor", (lambda kl, c, pz, cb: lambda e: e.matmul(pz[:], cb[:, kl * 128:(kl + 1) * 128],
                                                                                 x1b[c][:], start=True, stop=True))(kl, c, pz, cb),
                             reads=[cbt_, x1b[c]], writes=[pz])
                        p.op("scalar", (lambda kl, pz: lambda e: e.activation(pool[kl][:, 0:SLOT], pz[:], AF.Gelu))(kl, pz),
                             reads=[pz], writes=[pool[kl]])
                    for o in range(NCH):
                        g = kg * 8 + o
                        wbt = wbigL[o % 2]
                        wb = wbt[:]
                        p.op("sync", (lambda g, wb: lambda e: e.dma_start(out=wb, in_=wpj_r[g]))(g, wb),
                             writes=[wbt], dma=True, dkey=wbt, dram_in=wpj_b)
                        pf = ps_a if (o % 2 == 0) else ps_b

                        def pjgrp(e, wb=wb, pf=pf):
                            for kl in range(32):
                                inst = e.matmul(pf[:, 0:SLOT], wb[:, kl * 128:(kl + 1) * 128],
                                                pool[kl][:, 0:SLOT], start=(kl == 0), stop=(kl == 31))
                            return inst
                        p.op("tensor", pjgrp, reads=[wbt] + [pool[kl] for kl in range(32)], writes=[pf])
                        if kg == 0:
                            p.op("scalar", (lambda o, pf: lambda e: e.activation(faccA[o][:, 0:SLOT], pf[:, 0:SLOT], AF.Copy))(o, pf),
                                 reads=[pf], writes=[faccA[o]])
                        elif kg % 2 == 1:
                            p.op("vector", (lambda o, pf: lambda e: e.tensor_add(faccB[o][:, 0:SLOT], faccA[o][:, 0:SLOT], pf[:, 0:SLOT]))(o, pf),
                                 reads=[faccA[o], pf], writes=[faccB[o]])
                        else:
                            p.op("vector", (lambda o, pf: lambda e: e.tensor_add(faccA[o][:, 0:SLOT], faccB[o][:, 0:SLOT], pf[:, 0:SLOT]))(o, pf),
                                 reads=[faccB[o], pf], writes=[faccA[o]])
                if _os.environ.get("KDBG2"):
                    for c in range(NCH):
                        p.op("gpsimd", (lambda c, i: lambda e: e.dma_start(out=dbg_x1[c * 128:(c + 1) * 128, i * SLOT:(i + 1) * SLOT], in_=x1f[c][:, 0:SLOT]))(c, i),
                             reads=[x1f[c]], dma=True, dkey=x1f[c])
                        p.op("gpsimd", (lambda c, i: lambda e: e.dma_start(out=dbg_fb[c * 128:(c + 1) * 128, i * SLOT:(i + 1) * SLOT], in_=faccB[c][:, 0:SLOT]))(c, i),
                             reads=[faccB[c]], dma=True, dkey=faccB[c])
                for o in range(NCH):
                    p.op("gpsimd", (lambda o: lambda e: e.tensor_add(faccB[o][:, 0:SLOT], faccB[o][:, 0:SLOT], x1f[o][:, 0:SLOT]))(o),
                         reads=[faccB[o], x1f[o]], writes=[faccB[o]])
                ln_stats(faccB, SLOT, 0)
                for c in range(NCH):
                    p.op("gpsimd", (lambda c: lambda e: e.tensor_sub(outt[:], faccB[c][:, 0:SLOT], mtile2[:, 0:SLOT]))(c),
                         reads=[faccB[c], mtile2], writes=[outt])
                    p.op("vector", (lambda c: lambda e: e.tensor_mul(outt[:], outt[:], rstdt2[:, 0:SLOT]))(c),
                         reads=[outt, rstdt2], writes=[outt])
                    p.op("gpsimd", (lambda c, i: lambda e: e.dma_start(out=out[c * 128:(c + 1) * 128, i * SLOT:(i + 1) * SLOT], in_=outt[:]))(c, i),
                         reads=[outt], dma=True, dkey=outt, dram_out=out)

            import os
            if os.environ.get("KDBG"):
                dbg_k = dram("dbg_k", [C, T], kind="ExternalOutput")
                dbg_v = dram("dbg_v", [C, T], kind="ExternalOutput")
                dbg_y = dram("dbg_y", [C, T], kind="ExternalOutput")
                dtile = dearly if os.environ.get("KDBG2") else E3(nc.sbuf_tensor("dtile", [128, 1024], F32))
                for (srcb, dstb) in ((kbuf, dbg_k), (vbuf, dbg_v), (ybuf, dbg_y)):
                    for c in range(NCH):
                        for tt_ in range(4):
                            p.op("sync", (lambda c, tt_, srcb: lambda e: e.dma_start(out=dtile[:], in_=srcb[c * 128:(c + 1) * 128, tt_ * 1024:(tt_ + 1) * 1024]))(c, tt_, srcb),
                                 writes=[dtile], dma=True, dkey=dtile, dram_in=srcb)
                            p.op("gpsimd", (lambda c, tt_, dstb: lambda e: e.dma_start(out=dstb[c * 128:(c + 1) * 128, tt_ * 1024:(tt_ + 1) * 1024], in_=dtile[:]))(c, tt_, dstb),
                                 reads=[dtile], dma=True, dkey=dtile)
            p.op("gpsimd", lambda e: e.engine_nop(),
                 extra_waits=[ev for ev in p.all_events() if ev[0] != "gpsimd"])
            p.op("gpsimd", lambda e: e.engine_nop(),
                 extra_waits=[("gpsimd", p.cnt["gpsimd"])])

            block = E(nc.Block())
            p.replay(block)
    return nc


_CACHE = {}


def kernel(L0_attn_x, L0_attn_time_decay, L0_attn_time_first, L0_attn_W_tm,
           L0_attn_g1, L0_attn_b1, L0_attn_tmk, L0_attn_tmv, L0_attn_tmr,
           L0_attn_W_cm, L0_attn_W_cp, L0_attn_g2, L0_attn_b2, L0_attn_cmk,
           L0_attn_cmr, L0_ffn_rail_maker, L0_ffn_rail_up_proj, L0_ffn_w_cproj):
    f32 = lambda a: np.ascontiguousarray(np.asarray(a), dtype=np.float32)
    x = f32(L0_attn_x)
    Wtm = f32(L0_attn_W_tm)
    Wcm = f32(L0_attn_W_cm)
    Wcp = f32(L0_attn_W_cp)
    Wpj = f32(L0_ffn_w_cproj)
    RM = f32(L0_ffn_rail_maker)
    RU = f32(L0_ffn_rail_up_proj)

    wtm_kv = np.ascontiguousarray(Wtm[:, :2048])
    wtm_r = Wtm[:, 2048:]
    wtmr_host = np.ascontiguousarray(wtm_r.reshape(8, 128, 8, 128).transpose(2, 1, 0, 3).reshape(8 * 128, 8 * 128))
    wcm_host = np.ascontiguousarray(Wcm.reshape(8, 128, 32, 128).transpose(2, 1, 0, 3).reshape(32 * 128, 8 * 128))
    wcp_host = np.ascontiguousarray(Wcp.reshape(32, 128, 8, 128).transpose(2, 1, 0, 3).reshape(8 * 128, 32 * 128))
    WpjT = Wpj.T
    hh, cc_, gg, rl, k16 = np.meshgrid(np.arange(4), np.arange(8), np.arange(4), np.arange(8), np.arange(16), indexing="ij")
    morig = (hh * 4096 + (cc_ * 8 + rl) * 64 + gg * 16 + k16).reshape(-1)
    wpjT_perm = WpjT[morig]
    wpj_host = np.ascontiguousarray(
        wpjT_perm.reshape(4, 32, 128, 8, 128).transpose(0, 3, 2, 1, 4).reshape(32 * 128, 32 * 128))
    rm_host = np.zeros((4, 8, 128, 128), np.float32)
    ru_host = np.zeros((4, 8, 4, 128, 128), np.float32)
    for h4 in range(4):
        for c8 in range(8):
            for rl8 in range(8):
                r = c8 * 8 + rl8
                rm_host[h4, c8, rl8 * 16:(rl8 + 1) * 16, rl8 * 16:(rl8 + 1) * 16] = RM[h4, r].T
                for g4 in range(4):
                    ru_host[h4, c8, g4, rl8 * 16:(rl8 + 1) * 16, rl8 * 16:(rl8 + 1) * 16] = RU[h4, r, g4 * 16:(g4 + 1) * 16, :].T
    rm_host = np.ascontiguousarray(rm_host.reshape(32 * 128, 128))
    ru_host = np.ascontiguousarray(ru_host.reshape(128 * 128, 128))

    vecs = {
        "vw": f32(L0_attn_time_decay).reshape(C, 1),
        "vu": f32(L0_attn_time_first).reshape(C, 1),
        "vtmk": f32(L0_attn_tmk).reshape(C, 1),
        "vtmv": f32(L0_attn_tmv).reshape(C, 1),
        "vtmr": f32(L0_attn_tmr).reshape(C, 1),
        "vcmk": f32(L0_attn_cmk).reshape(C, 1),
        "vcmr": f32(L0_attn_cmr).reshape(C, 1),
        "vg1": f32(L0_attn_g1).reshape(C, 1),
        "vb1": f32(L0_attn_b1).reshape(C, 1),
        "vg2": f32(L0_attn_g2).reshape(C, 1),
        "vb2": f32(L0_attn_b2).reshape(C, 1),
    }

    if "nc" not in _CACHE:
        _CACHE["nc"] = build_program()
    nc = _CACHE["nc"]

    in_maps = []
    for core in range(8):
        b, h = core // 2, core % 2
        xb = np.ascontiguousarray(x[b].T)
        xe_cols = []
        xpr_cols = []
        for i in range(NSLOT):
            s = (2 * i + h) * 512
            toks = (np.arange(-1, SLOT) + s) % T
            xe_cols.append(toks)
            xpr_cols.append(np.where(toks == 0, 0, (toks - 1) % T))
        xe_idx = np.concatenate(xe_cols)
        xpr_idx = np.concatenate(xpr_cols)
        m = {
            "xT": xb,
            "xe_in": np.ascontiguousarray(xb[:, xe_idx]),
            "xpr_in": np.ascontiguousarray(xb[:, xpr_idx]),
            "wtmkv_in": wtm_kv,
            "wtmr_in": wtmr_host,
            "wcm_in": wcm_host,
            "wcp_in": wcp_host,
            "wpj_in": wpj_host,
            "rm_in": rm_host,
            "ru_in": ru_host,
            "mask_in": np.full((128, 1), float(h), np.float32),
        }
        m.update(vecs)
        in_maps.append(m)

    res = run_bass_kernel_spmd(nc, in_maps, core_ids=list(range(8)))
    _CACHE["res"] = res
    full = np.empty((B, T, C), np.float32)
    for core in range(8):
        b, h = core // 2, core % 2
        shard = res.results[core]["out"]
        for i in range(NSLOT):
            s = (2 * i + h) * 512
            full[b, s:s + 512, :] = shard[:, i * SLOT:(i + 1) * SLOT].T
    return full


# version 05
# speedup vs baseline: 1.1022x; 1.1022x over previous
import contextlib
import numpy as np
import concourse.bass as bass
import concourse.mybir as mybir
from concourse.bass_utils import run_bass_kernel_spmd

F32 = mybir.dt.float32
BF16 = mybir.dt.bfloat16
AF = mybir.ActivationFunctionType
ALU = mybir.AluOpType

B, T, C = 4, 4096, 1024
NCH = 8
TCS = 512
NTCS = T // TCS
LW = 512
NLW = T // LW
SLOT = 512
NSLOT = 4
SW = SLOT + 1
LN_EPS = 1e-5
WKV_EPS = 1e-8
NEG = -1e38

ENGINES = ("sync", "tensor", "vector", "scalar", "gpsimd")


class Prog:

    def __init__(self, nc, es):
        self.nc = nc
        self.es = es
        self.ops = {e: [] for e in ENGINES}
        self.sems = {}
        self.cnt = {e: 0 for e in ENGINES}
        self.lastw = {}
        self.reads = {}
        self.dramw = {}
        self.dsem = {}
        self.ndsem = 0
        self.water = {e: {} for e in ENGINES}
        self.pending = {e: [] for e in ENGINES}

    def make_sems(self):
        for e in ENGINES:
            self.sems[e] = self.es.enter_context(self.nc.semaphore(f"sem_{e}"))

    @staticmethod
    def _key(ap):
        t = getattr(ap, "tensor", ap)
        return id(t)

    def _dma_sem(self, dkey):
        k = dkey if isinstance(dkey, str) else self._key(dkey)
        if k not in self.dsem:
            nm = f"dma{self.ndsem}"
            self.ndsem += 1
            self.sems[nm] = self.es.enter_context(self.nc.semaphore(nm))
            self.cnt[nm] = 0
            self.dsem[k] = nm
        return self.dsem[k]

    def op(self, eng, fn, reads=(), writes=(), dma=False, dkey=None,
           dram_in=None, dram_out=None, extra_waits=()):
        waits = {}

        def addw(ev):
            if ev is None:
                return
            sem, val = ev[0], ev[1]
            if sem == eng:
                return
            if waits.get(sem, 0) < val:
                waits[sem] = val

        for ap in reads:
            addw(self.lastw.get(self._key(ap)))
        for ap in writes:
            k = self._key(ap)
            addw(self.lastw.get(k))
            for ev in self.reads.get(k, ()):
                addw(ev)
        if dram_in is not None:
            if not isinstance(dram_in, (list, tuple)):
                dram_in = [dram_in]
            for k_ in dram_in:
                for sem, val in self.dramw.get(k_, {}).items():
                    addw((sem, val))
        for ev in extra_waits:
            addw(ev)
        for ev in self.pending[eng]:
            addw(ev)
        self.pending[eng] = []
        final_waits = []
        for sem, val in waits.items():
            if self.water[eng].get(sem, 0) >= val:
                continue
            self.water[eng][sem] = val
            final_waits.append((sem, val))
        if dma:
            assert dkey is not None, "dma op needs dkey (sbuf-side tile)"
            semn = self._dma_sem(dkey)
            self.cnt[semn] += 16
            ev = (semn, self.cnt[semn])
        else:
            self.cnt[eng] += 1
            ev = (eng, self.cnt[eng])
        self.ops[eng].append((fn, tuple(final_waits), 16 if dma else 1, ev[0]))
        for ap in writes:
            k = self._key(ap)
            self.lastw[k] = ev
            self.reads[k] = []
        for ap in reads:
            self.reads.setdefault(self._key(ap), []).append(ev)
        if dram_out is not None:
            d = self.dramw.setdefault(dram_out, {})
            d[ev[0]] = max(d.get(ev[0], 0), ev[1])
        return ev

    def all_events(self):
        return [(s, c) for s, c in self.cnt.items() if c > 0]

    def barrier(self):
        evs = self.all_events()
        for e in ENGINES:
            self.pending[e] = [ev for ev in evs if ev[0] != e]

    def replay(self, block):
        for e in ENGINES:
            eng_ops = self.ops[e]
            sems = self.sems

            def mk(ops_list):
                def body(engine):
                    for fn, waits, amt, incsem in ops_list:
                        for (sem, val) in waits:
                            engine.wait_ge(sems[sem], val)
                        inst = fn(engine)
                        inst.then_inc(sems[incsem], amt)
                return body
            getattr(block, e)(mk(eng_ops))


def build_program():
    nc = bass.Bass("TRN2", target_bir_lowering=False)

    def dram(name, shape, dt=F32, kind="Internal"):
        return nc.dram_tensor(name, shape, dt, kind=kind).ap()

    xT = dram("xT", [C, T], kind="ExternalInput")
    xe_in = dram("xe_in", [C, NSLOT * SW], kind="ExternalInput")
    xpr_in = dram("xpr_in", [C, NSLOT * SW], kind="ExternalInput")
    wtmkv_in = dram("wtmkv_in", [C, 2048], kind="ExternalInput")
    wtmr_in = dram("wtmr_in", [8 * 128, 8 * 128], kind="ExternalInput")
    wcm_in = dram("wcm_in", [32 * 128, 8 * 128], kind="ExternalInput")
    wcp_in = dram("wcp_in", [8 * 128, 32 * 128], kind="ExternalInput")
    wpj_in = dram("wpj_in", [32 * 128, 32 * 128], kind="ExternalInput")
    rm_in = dram("rm_in", [32 * 128, 128], kind="ExternalInput")
    ru_in = dram("ru_in", [128 * 128, 128], kind="ExternalInput")
    VNAMES = ("vw", "vu", "vtmk", "vtmv", "vtmr", "vcmk", "vcmr", "vg1", "vb1", "vg2", "vb2")
    vecs_in = {nm: dram(nm, [C, 1], kind="ExternalInput") for nm in VNAMES}
    mask_in = dram("mask_in", [128, 1], kind="ExternalInput")
    out = dram("out", [C, NSLOT * SLOT], kind="ExternalOutput")

    kbuf = dram("kbuf", [C, T])
    vbuf = dram("vbuf", [C, T])
    ybuf = dram("ybuf", [C, T])
    wtmr_b = dram("wtmr_b", [8 * 128, 8 * 128], BF16)
    wcm_b = dram("wcm_b", [32 * 128, 8 * 128], BF16)
    wcp_b = dram("wcp_b", [8 * 128, 32 * 128], BF16)
    wpj_b = dram("wpj_b", [32 * 128, 32 * 128], BF16)
    cbd_b = dram("cbd_b", [4 * 128, 32 * 128], BF16)

    with contextlib.ExitStack() as es:
        E = es.enter_context
        p = Prog(nc, es)
        p.make_sems()

        vt = {nm: E(nc.sbuf_tensor(nm + "_t", [128, NCH], F32)) for nm in VNAMES}
        OMN = ("vtmk", "vtmv", "vtmr", "vcmk", "vcmr")
        om = {nm: E(nc.sbuf_tensor("om_" + nm, [128, NCH], F32)) for nm in OMN}
        maskt = E(nc.sbuf_tensor("maskt", [128, 1], F32))
        epsT = E(nc.sbuf_tensor("epsT", [128, 1], F32))
        ones_b = E(nc.sbuf_tensor("ones_b", [128, 128], BF16))
        carryL = [E(nc.sbuf_tensor(f"carry{c}", [128, 4], F32)) for c in range(NCH)]

        def vs(tile, c):
            return tile[:, c:c + 1]

        for nm, d in vecs_in.items():
            for c in range(NCH):
                p.op("sync", (lambda nm, c, d: lambda e: e.dma_start(out=vs(vt[nm], c), in_=d[c * 128:(c + 1) * 128, :]))(nm, c, d),
                     writes=[vt[nm]], dma=True, dkey=vt[nm])
        p.op("sync", lambda e: e.dma_start(out=maskt[:], in_=mask_in), writes=[maskt], dma=True, dkey=maskt)
        maskom = E(nc.sbuf_tensor("maskom", [128, 1], F32))
        p.op("vector", lambda e: e.tensor_scalar(maskom[:], maskt[:], -1.0, 1.0, ALU.mult, ALU.add),
             reads=[maskt], writes=[maskom])
        for nm in OMN:
            for c in range(NCH):
                p.op("vector", (lambda nm, c: lambda e: e.tensor_scalar(vs(om[nm], c), vs(vt[nm], c), -1.0, 1.0, ALU.mult, ALU.add))(nm, c),
                     reads=[vt[nm]], writes=[om[nm]])
        p.op("vector", lambda e: e.memset(epsT[:], LN_EPS), writes=[epsT])
        p.op("vector", lambda e: e.memset(ones_b[:], 1.0), writes=[ones_b])
        for c in range(NCH):
            p.op("vector", (lambda c: lambda e: e.memset(carryL[c][:], 0.0))(c), writes=[carryL[c]])
            p.op("vector", (lambda c: lambda e: e.memset(carryL[c][:, 0:1], NEG))(c), writes=[carryL[c]])
            p.op("vector", (lambda c: lambda e: e.memset(carryL[c][:, 1:3], 1.0))(c), writes=[carryL[c]])

        with contextlib.ExitStack() as es1:
            E1 = es1.enter_context
            for (srcw, dstw, nm) in ((wcm_in, wcm_b, "wcm"), (wcp_in, wcp_b, "wcp"), (wpj_in, wpj_b, "wpj")):
                p.op("gpsimd", (lambda srcw, dstw: lambda e: e.dma_start(out=dstw, in_=srcw))(srcw, dstw),
                     dma=True, dkey="d2d_" + nm, dram_out=nm)
            rmt = E1(nc.sbuf_tensor("rmt", [128, 32 * 128], BF16))
            rut = E1(nc.sbuf_tensor("rut", [128, 128 * 128], BF16))
            cbt = E1(nc.sbuf_tensor("cbt", [128, 32 * 128], BF16))
            pmc = E1(nc.psum_tensor("pmc", [128, 512], F32))
            rm_r = rm_in.rearrange("(t p) f -> p t f", p=128)
            ru_r = ru_in.rearrange("(t p) f -> p t f", p=128)
            p.op("gpsimd", lambda e: e.dma_start(out=rmt[:].rearrange("p (t f) -> p t f", f=128), in_=rm_r),
                 writes=[rmt], dma=True, dkey=rmt)
            for q_ in range(4):
                p.op("gpsimd", (lambda q_: lambda e: e.dma_start(
                    out=rut[:, q_ * 4096:(q_ + 1) * 4096].rearrange("p (t f) -> p t f", f=128),
                    in_=ru_r[:, q_ * 32:(q_ + 1) * 32, :]))(q_),
                    writes=[rut], dma=True, dkey=rut)
            for kg in range(4):
                for kl in range(32):
                    k = kg * 32 + kl
                    hc = k // 4
                    p.op("tensor", (lambda kl, hc, k: lambda e: e.matmul(pmc[:, 0:128], rmt[:, hc * 128:(hc + 1) * 128],
                                                                         rut[:, k * 128:(k + 1) * 128], start=True, stop=True))(kl, hc, k),
                         reads=[rmt, rut], writes=[pmc])
                    p.op("scalar", (lambda kl: lambda e: e.activation(cbt[:, kl * 128:(kl + 1) * 128], pmc[:, 0:128], AF.Copy))(kl),
                         reads=[pmc], writes=[cbt])
                p.op("gpsimd", (lambda kg: lambda e: e.dma_start(out=cbd_b[kg * 128:(kg + 1) * 128, :], in_=cbt[:]))(kg),
                     reads=[cbt], dma=True, dkey=cbt, dram_out="cbd")
        p.barrier()

        with contextlib.ExitStack() as es2:
            E2 = es2.enter_context
            wtmkv_sb = E2(nc.sbuf_tensor("wtmkv_sb", [128, NCH * 2048], BF16))
            for c in range(NCH):
                p.op("gpsimd", (lambda c: lambda e: e.dma_start(out=wtmkv_sb[:, c * 2048:(c + 1) * 2048], in_=wtmkv_in[c * 128:(c + 1) * 128, :]))(c),
                     writes=[wtmkv_sb], dma=True, dkey=wtmkv_sb)

            xt = [E2(nc.sbuf_tensor(f"xt{c}", [128, 1 + TCS], F32)) for c in range(NCH)]
            xb16L = [E2(nc.sbuf_tensor(f"xb16_{j}", [128, TCS], BF16)) for j in range(2)]
            xsqL = [E2(nc.sbuf_tensor(f"xsq{j}", [128, TCS], BF16)) for j in range(2)]
            xn1bL = [[E2(nc.sbuf_tensor(f"xn1b{j}_{c}", [128, TCS], BF16)) for c in range(NCH)] for j in range(2)]
            mtileL = [E2(nc.sbuf_tensor(f"mtile{j}", [128, TCS], F32)) for j in range(2)]
            rstdtL = [E2(nc.sbuf_tensor(f"rstdt{j}", [128, TCS], F32)) for j in range(2)]
            vartL = [E2(nc.sbuf_tensor(f"vart{j}", [128, TCS], F32)) for j in range(2)]
            msqtL = [E2(nc.sbuf_tensor(f"msqt{j}", [128, TCS], F32)) for j in range(2)]
            dscL = [E2(nc.sbuf_tensor(f"sdsc{j}", [128, TCS], F32)) for j in range(4)]
            kvevL = [E2(nc.sbuf_tensor(f"kvev{j}", [128, TCS], F32)) for j in range(4)]
            xppL = [E2(nc.sbuf_tensor(f"xpp{j}", [128, TCS], F32)) for j in range(4)]
            ps_sL = [E2(nc.psum_tensor(f"ps_s{j}", [128, TCS], F32)) for j in range(2)]
            ps_qL = [E2(nc.psum_tensor(f"ps_q{j}", [128, TCS], F32)) for j in range(2)]
            ps_kv = [E2(nc.psum_tensor(f"ps_kv{i}", [128, TCS], F32)) for i in range(2)]

            for tc in range(NTCS):
                t0 = tc * TCS
                par = tc % 2
                xn1b = xn1bL[par]
                mtile, rstdt, vart, msqt = mtileL[par], rstdtL[par], vartL[par], msqtL[par]
                ps_s, ps_q = ps_sL[par], ps_qL[par]
                for c in range(NCH):
                    if tc == 0:
                        p.op("sync", (lambda c: lambda e: e.dma_start(out=xt[c][:, 1:1 + TCS], in_=xT[c * 128:(c + 1) * 128, 0:TCS]))(c),
                             writes=[xt[c]], dma=True, dkey=xt[c])
                        p.op("sync", (lambda c: lambda e: e.dma_start(out=xt[c][:, 0:1], in_=xT[c * 128:(c + 1) * 128, 0:1], allow_slow_non_contiguous=True))(c),
                             writes=[xt[c]], dma=True, dkey=xt[c])
                    else:
                        p.op("sync", (lambda c, t0: lambda e: e.dma_start(out=xt[c][:], in_=xT[c * 128:(c + 1) * 128, t0 - 1:t0 + TCS]))(c, t0),
                             writes=[xt[c]], dma=True, dkey=xt[c])
                for c in range(NCH):
                    xb = xb16L[c % 2]
                    p.op("scalar", (lambda c, xb: lambda e: e.activation(xb[:], xt[c][:, 1:1 + TCS], AF.Copy))(c, xb),
                         reads=[xt[c]], writes=[xb])
                    p.op("tensor", (lambda c, xb: lambda e: e.matmul(ps_s[:], ones_b[:], xb[:], start=(c == 0), stop=(c == NCH - 1)))(c, xb),
                         reads=[ones_b, xb], writes=[ps_s])
                for c in range(NCH):
                    xq = xsqL[c % 2]
                    p.op("scalar", (lambda c, xq: lambda e: e.activation(xq[:], xt[c][:, 1:1 + TCS], AF.Square))(c, xq),
                         reads=[xt[c]], writes=[xq])
                    p.op("tensor", (lambda c, xq: lambda e: e.matmul(ps_q[:], ones_b[:], xq[:], start=(c == 0), stop=(c == NCH - 1)))(c, xq),
                         reads=[ones_b, xq], writes=[ps_q])
                p.op("vector", lambda e: e.tensor_scalar(mtile[:], ps_s[:], 1.0 / C, None, ALU.mult), reads=[ps_s], writes=[mtile])
                p.op("scalar", lambda e: e.activation(msqt[:], mtile[:], AF.Square), reads=[mtile], writes=[msqt])
                p.op("vector", lambda e: e.scalar_tensor_tensor(vart[:], ps_q[:], 1.0 / C, msqt[:], ALU.mult, ALU.subtract),
                     reads=[ps_q, msqt], writes=[vart])
                p.op("scalar", lambda e: e.activation(vart[:], vart[:], AF.Ln, bias=epsT[:]), reads=[vart, epsT], writes=[vart])
                p.op("scalar", lambda e: e.activation(rstdt[:], vart[:], AF.Exp, scale=-0.5), reads=[vart], writes=[rstdt])
                for c in range(NCH):
                    dsc = dscL[(tc * NCH + c) % 4]
                    p.op("gpsimd", (lambda c, dsc: lambda e: e.tensor_sub(dsc[:], xt[c][:, 1:1 + TCS], mtile[:]))(c, dsc),
                         reads=[xt[c], mtile], writes=[dsc])
                    p.op("vector", (lambda c, dsc: lambda e: e.scalar_tensor_tensor(dsc[:], dsc[:], vs(vt["vg1"], c), rstdt[:], ALU.mult, ALU.mult))(c, dsc),
                         reads=[dsc, rstdt, vt["vg1"]], writes=[dsc])
                    p.op("vector", (lambda c, dsc: lambda e: e.tensor_scalar(xn1b[c][:], dsc[:], vs(vt["vb1"], c), None, ALU.add))(c, dsc),
                         reads=[dsc, vt["vb1"]], writes=[xn1b[c]])
                m_order = [mm for cc_ in range(8) for mm in (cc_, cc_ + 8)]
                for mi, m in enumerate(m_order):
                    pkv = ps_kv[mi % 2]

                    def mmgrp(e, m=m, pkv=pkv):
                        for c in range(NCH):
                            inst = e.matmul(pkv[:], wtmkv_sb[:, c * 2048 + m * 128:c * 2048 + (m + 1) * 128],
                                            xn1b[c][:], start=(c == 0), stop=(c == NCH - 1))
                        return inst
                    p.op("tensor", mmgrp, reads=[wtmkv_sb] + [xn1b[c] for c in range(NCH)], writes=[pkv])
                    isk = m < 8
                    cc = m % 8
                    mixv = vt["vtmk"] if isk else vt["vtmv"]
                    omv = om["vtmk"] if isk else om["vtmv"]
                    kvt = kvevL[mi % 4]
                    xpp = xppL[mi % 4]
                    dstbuf = kbuf if isk else vbuf
                    p.op("gpsimd", (lambda cc, omv, xpp: lambda e: e.tensor_scalar(xpp[:], xt[cc][:, 0:TCS], vs(omv, cc), None, ALU.mult))(cc, omv, xpp),
                         reads=[xt[cc], omv], writes=[xpp])
                    p.op("vector", (lambda pkv, mixv, cc, kvt, xpp: lambda e: e.scalar_tensor_tensor(kvt[:], pkv[:], vs(mixv, cc), xpp[:], ALU.mult, ALU.add))(pkv, mixv, cc, kvt, xpp),
                         reads=[pkv, mixv, xpp], writes=[kvt])
                    p.op("gpsimd", (lambda dstbuf, cc, t0, kvt: lambda e: e.dma_start(out=dstbuf[cc * 128:(cc + 1) * 128, t0:t0 + TCS], in_=kvt[:]))(dstbuf, cc, t0, kvt),
                         reads=[kvt], dma=True, dkey=kvt, dram_out=("kv", 0 if isk else 1, tc))

            wbc = E2(nc.sbuf_tensor("wbc", [128, NCH * LW], F32))
            for c in range(NCH):
                p.op("scalar", (lambda c: lambda e: e.memzero(wbc[:, c * LW:(c + 1) * LW]))(c), writes=[wbc])
            for c in range(NCH):
                p.op("vector", (lambda c: lambda e: e.tensor_scalar(wbc[:, c * LW:(c + 1) * LW], wbc[:, c * LW:(c + 1) * LW], vs(vt["vw"], c), None, ALU.add))(c),
                     reads=[vt["vw"]], writes=[wbc])

            def mkset(j):
                return {nm: E2(nc.sbuf_tensor(f"wkv{nm}{j}", [128, 1 + LW], F32)) for nm in ("mT", "SgH", "Sg", "aaT", "sT")} |                        {nm: E2(nc.sbuf_tensor(f"wkv{nm}{j}", [128, LW], F32)) for nm in ("KT", "VT", "w1", "w2", "w3", "w4", "w5", "w6", "yt")}
            sets = [mkset(0), mkset(1)]

            unit = 0
            for wt in range(NLW):
                t0 = wt * LW
                for c in range(NCH):
                    S = sets[unit % 2]
                    unit += 1
                    mT, SgH, Sg, aaT, sT = S["mT"], S["SgH"], S["Sg"], S["aaT"], S["sT"]
                    KT, VT = S["KT"], S["VT"]
                    w1, w2, w3, w4, w5, w6, yt = S["w1"], S["w2"], S["w3"], S["w4"], S["w5"], S["w6"], S["yt"]
                    car = carryL[c]
                    wv_c = vs(vt["vw"], c)
                    wu_c = vs(vt["vu"], c)
                    wbc_c = wbc[:, c * LW:(c + 1) * LW]
                    p.op("sync", (lambda c, t0, KT: lambda e: e.dma_start(out=KT[:], in_=kbuf[c * 128:(c + 1) * 128, t0:t0 + LW]))(c, t0, KT),
                         writes=[KT], dma=True, dkey=KT, dram_in=[("kv", 0, 2 * wt), ("kv", 0, 2 * wt + 1)] if LW == 1024 else [("kv", 0, wt)])
                    p.op("sync", (lambda c, t0, VT: lambda e: e.dma_start(out=VT[:], in_=vbuf[c * 128:(c + 1) * 128, t0:t0 + LW]))(c, t0, VT),
                         writes=[VT], dma=True, dkey=VT, dram_in=[("kv", 1, 2 * wt), ("kv", 1, 2 * wt + 1)] if LW == 1024 else [("kv", 1, wt)])
                    p.op("scalar", (lambda car, mT: lambda e: e.copy(mT[:, 0:1], car[:, 0:1]))(car, mT), reads=[car], writes=[mT])
                    p.op("vector", (lambda wbc_c, car, mT, KT: lambda e: e.tensor_tensor_scan(mT[:, 1:1 + LW], wbc_c, KT[:], car[:, 0:1], ALU.add, ALU.max))(wbc_c, car, mT, KT),
                         reads=[wbc, KT, car], writes=[mT])
                    p.op("vector", (lambda wv_c, mT, w1: lambda e: e.scalar_tensor_tensor(w1[:], mT[:, 0:LW], wv_c, mT[:, 1:1 + LW], ALU.add, ALU.subtract))(wv_c, mT, w1),
                         reads=[mT, vt["vw"]], writes=[w1])
                    p.op("scalar", (lambda w1: lambda e: e.activation(w1[:], w1[:], AF.Exp))(w1), reads=[w1], writes=[w1])
                    p.op("gpsimd", (lambda KT, mT, w2: lambda e: e.tensor_sub(w2[:], KT[:], mT[:, 1:1 + LW]))(KT, mT, w2), reads=[KT, mT], writes=[w2])
                    p.op("scalar", (lambda w2: lambda e: e.activation(w2[:], w2[:], AF.Exp))(w2), reads=[w2], writes=[w2])
                    p.op("scalar", (lambda car, SgH: lambda e: e.copy(SgH[:, 0:1], car[:, 1:2]))(car, SgH), reads=[car], writes=[SgH])
                    p.op("vector", (lambda car, SgH, w1, w2: lambda e: e.tensor_tensor_scan(SgH[:, 1:1 + LW], w1[:], w2[:], car[:, 1:2], ALU.mult, ALU.add))(car, SgH, w1, w2),
                         reads=[w1, w2, car], writes=[SgH])
                    p.op("gpsimd", (lambda w3, w1, SgH: lambda e: e.tensor_mul(w3[:], w1[:], SgH[:, 0:LW]))(w3, w1, SgH), reads=[w1, SgH], writes=[w3])
                    p.op("vector", (lambda w3, w2: lambda e: e.tensor_max(w3[:], w3[:], w2[:]))(w3, w2), reads=[w3, w2], writes=[w3])
                    p.op("vector", (lambda w3, w2: lambda e: e.scalar_tensor_tensor(w3[:], w3[:], WKV_EPS, w2[:], ALU.mult, ALU.add))(w3, w2),
                         reads=[w3, w2], writes=[w3])
                    p.op("scalar", (lambda car, Sg: lambda e: e.copy(Sg[:, 0:1], car[:, 2:3]))(car, Sg), reads=[car], writes=[Sg])
                    p.op("vector", (lambda car, Sg, w1, w3: lambda e: e.tensor_tensor_scan(Sg[:, 1:1 + LW], w1[:], w3[:], car[:, 2:3], ALU.mult, ALU.add))(car, Sg, w1, w3),
                         reads=[w1, w3, car], writes=[Sg])
                    p.op("scalar", (lambda Sg, sT: lambda e: e.activation(sT[:], Sg[:], AF.Ln))(Sg, sT), reads=[Sg], writes=[sT])
                    p.op("gpsimd", (lambda sT, mT: lambda e: e.tensor_add(sT[:], mT[:], sT[:]))(sT, mT), reads=[mT, sT], writes=[sT])
                    p.op("vector", (lambda wv_c, sT, KT, w4: lambda e: e.scalar_tensor_tensor(w4[:], sT[:, 0:LW], wv_c, KT[:], ALU.add, ALU.subtract))(wv_c, sT, KT, w4),
                         reads=[sT, vt["vw"], KT], writes=[w4])
                    p.op("gpsimd", (lambda w5, w4: lambda e: e.tensor_scalar_min(w5[:], w4[:], 0.0))(w5, w4), reads=[w4], writes=[w5])
                    p.op("scalar", (lambda w5: lambda e: e.activation(w5[:], w5[:], AF.Exp))(w5), reads=[w5], writes=[w5])
                    p.op("gpsimd", (lambda w4: lambda e: e.tensor_scalar_max(w4[:], w4[:], 0.0))(w4), reads=[w4], writes=[w4])
                    p.op("scalar", (lambda w4: lambda e: e.activation(w4[:], w4[:], AF.Exp, scale=-1.0))(w4), reads=[w4], writes=[w4])
                    p.op("gpsimd", (lambda w4, VT: lambda e: e.tensor_mul(w4[:], w4[:], VT[:]))(w4, VT), reads=[w4, VT], writes=[w4])
                    p.op("scalar", (lambda car, aaT: lambda e: e.copy(aaT[:, 0:1], car[:, 3:4]))(car, aaT), reads=[car], writes=[aaT])
                    p.op("vector", (lambda car, aaT, w5, w4: lambda e: e.tensor_tensor_scan(aaT[:, 1:1 + LW], w5[:], w4[:], car[:, 3:4], ALU.mult, ALU.add))(car, aaT, w5, w4),
                         reads=[w5, w4, car], writes=[aaT])
                    p.op("scalar", (lambda car, mT: lambda e: e.copy(car[:, 0:1], mT[:, LW:1 + LW]))(car, mT), reads=[mT], writes=[car])
                    p.op("scalar", (lambda car, SgH: lambda e: e.copy(car[:, 1:2], SgH[:, LW:1 + LW]))(car, SgH), reads=[SgH], writes=[car])
                    p.op("scalar", (lambda car, Sg: lambda e: e.copy(car[:, 2:3], Sg[:, LW:1 + LW]))(car, Sg), reads=[Sg], writes=[car])
                    p.op("scalar", (lambda car, aaT: lambda e: e.copy(car[:, 3:4], aaT[:, LW:1 + LW]))(car, aaT), reads=[aaT], writes=[car])
                    p.op("vector", (lambda wu_c, KT, sT, w6: lambda e: e.scalar_tensor_tensor(w6[:], KT[:], wu_c, sT[:, 0:LW], ALU.add, ALU.subtract))(wu_c, KT, sT, w6),
                         reads=[KT, vt["vu"], sT], writes=[w6])
                    p.op("gpsimd", (lambda w1, w6: lambda e: e.tensor_scalar_min(w1[:], w6[:], 0.0))(w1, w6), reads=[w6], writes=[w1])
                    p.op("scalar", (lambda w1: lambda e: e.activation(w1[:], w1[:], AF.Exp))(w1), reads=[w1], writes=[w1])
                    p.op("gpsimd", (lambda w6: lambda e: e.tensor_scalar_max(w6[:], w6[:], 0.0))(w6), reads=[w6], writes=[w6])
                    p.op("scalar", (lambda w6: lambda e: e.activation(w6[:], w6[:], AF.Exp, scale=-1.0))(w6), reads=[w6], writes=[w6])
                    p.op("gpsimd", (lambda w2, w6, aaT: lambda e: e.tensor_mul(w2[:], w6[:], aaT[:, 0:LW]))(w2, w6, aaT), reads=[w6, aaT], writes=[w2])
                    p.op("vector", (lambda w3, w1, VT: lambda e: e.tensor_mul(w3[:], w1[:], VT[:]))(w3, w1, VT), reads=[w1, VT], writes=[w3])
                    p.op("vector", (lambda w2, w3: lambda e: e.tensor_add(w2[:], w2[:], w3[:]))(w2, w3), reads=[w2, w3], writes=[w2])
                    p.op("vector", (lambda w3, w6, w1: lambda e: e.scalar_tensor_tensor(w3[:], w6[:], WKV_EPS, w1[:], ALU.add, ALU.add))(w3, w6, w1),
                         reads=[w6, w1], writes=[w3])
                    p.op("scalar", (lambda w3: lambda e: e.activation(w3[:], w3[:], AF.Ln))(w3), reads=[w3], writes=[w3])
                    p.op("scalar", (lambda w3: lambda e: e.activation(w3[:], w3[:], AF.Exp, scale=-1.0))(w3), reads=[w3], writes=[w3])
                    p.op("vector", (lambda yt, w2, w3: lambda e: e.tensor_mul(yt[:], w2[:], w3[:]))(yt, w2, w3), reads=[w2, w3], writes=[yt])
                    p.op("gpsimd", (lambda c, t0, yt: lambda e: e.dma_start(out=ybuf[c * 128:(c + 1) * 128, t0:t0 + LW], in_=yt[:]))(c, t0, yt),
                         reads=[yt], dma=True, dkey=yt, dram_out=("y", wt))
        p.barrier()

        with contextlib.ExitStack() as es3:
            E3 = es3.enter_context
            bufA = [E3(nc.sbuf_tensor(f"bufA{c}", [128, SW], F32)) for c in range(NCH)]
            bufB = [E3(nc.sbuf_tensor(f"bufB{c}", [128, SW], F32)) for c in range(NCH)]
            bufC = [E3(nc.sbuf_tensor(f"bufC{c}", [128, SW], F32)) for c in range(NCH)]
            bufD = [E3(nc.sbuf_tensor(f"bufD{c}", [128, SW], F32)) for c in range(NCH)]
            bufE = [E3(nc.sbuf_tensor(f"bufE{c}", [128, SW], F32)) for c in range(NCH)]
            x2d = [E3(nc.sbuf_tensor(f"x2d{c}", [128, SW], F32)) for c in range(NCH)]
            pool = [E3(nc.sbuf_tensor(f"pool{i}", [128, SW], BF16)) for i in range(32)]
            nb16 = [E3(nc.sbuf_tensor(f"nb16_{c}", [128, SLOT], BF16)) for c in range(NCH)]
            ycn = [E3(nc.sbuf_tensor(f"ycn{j}", [128, 544], F32)) for j in range(4)]
            dscA = [E3(nc.sbuf_tensor(f"dscA{j}", [128, SW], F32)) for j in range(2)]
            cbt2 = E3(nc.sbuf_tensor("cbt2", [128, 4096], BF16))
            mtile2 = E3(nc.sbuf_tensor("mtile2", [128, SW], F32))
            rstdt2 = E3(nc.sbuf_tensor("rstdt2", [128, SW], F32))
            vart2 = E3(nc.sbuf_tensor("vart2", [128, SW], F32))
            msqt2 = E3(nc.sbuf_tensor("msqt2", [128, SW], F32))
            dsc2 = E3(nc.sbuf_tensor("dsc2", [128, SW], F32))
            dsc3 = E3(nc.sbuf_tensor("dsc3", [128, SW], F32))
            xn2f = E3(nc.sbuf_tensor("xn2f", [128, SLOT], F32))
            xb16pL = [E3(nc.sbuf_tensor(f"xb16p{j}", [128, SW], BF16)) for j in range(2)]
            xsqpL = [E3(nc.sbuf_tensor(f"xsqp{j}", [128, SW], BF16)) for j in range(2)]
            wsmL = [E3(nc.sbuf_tensor(f"wsm{j}", [128, 1024], BF16)) for j in range(2)]
            wbigL = [E3(nc.sbuf_tensor(f"wbig{j}", [128, 4096], BF16)) for j in range(2)]
            outt = E3(nc.sbuf_tensor("outt", [128, SLOT], F32))
            ps_a = E3(nc.psum_tensor("ps_a", [128, SW], F32))
            ps_b = E3(nc.psum_tensor("ps_b", [128, SW], F32))
            ps_c = E3(nc.psum_tensor("ps_c", [128, SLOT], F32))
            ps_d = E3(nc.psum_tensor("ps_d", [128, SLOT], F32))
            ps_n1 = E3(nc.psum_tensor("ps_n1", [128, 1], F32))

            wcm_r = wcm_b.rearrange("(m p) f -> m p f", p=128)
            wcp_r = wcp_b.rearrange("(o p) f -> o p f", p=128)
            wpj_r = wpj_b.rearrange("(g p) f -> g p f", p=128)

            def ln_stats(src_tiles, width, off):
                pieces = [(0, min(width, SLOT))]
                if width > SLOT:
                    pieces.append((SLOT, width))
                for (ps, actf, scrL) in ((ps_a, AF.Copy, xb16pL), (ps_b, AF.Square, xsqpL)):
                    for lo, hi in pieces:
                        for c in range(NCH):
                            scratch = scrL[c % 2]
                            p.op("scalar", (lambda c, lo, hi, actf, scratch: lambda e: e.activation(
                                scratch[:, lo:hi], src_tiles[c][:, off + lo:off + hi], actf))(c, lo, hi, actf, scratch),
                                 reads=[src_tiles[c]], writes=[scratch])
                            p.op("tensor", (lambda c, lo, hi, ps, scratch: lambda e: e.matmul(
                                ps[:, lo:hi], ones_b[:], scratch[:, lo:hi],
                                start=(c == 0), stop=(c == NCH - 1)))(c, lo, hi, ps, scratch),
                                 reads=[ones_b, scratch], writes=[ps])
                p.op("vector", lambda e: e.tensor_scalar(mtile2[:, 0:width], ps_a[:, 0:width], 1.0 / C, None, ALU.mult),
                     reads=[ps_a], writes=[mtile2])
                p.op("scalar", lambda e: e.activation(msqt2[:, 0:width], mtile2[:, 0:width], AF.Square),
                     reads=[mtile2], writes=[msqt2])
                p.op("vector", lambda e: e.scalar_tensor_tensor(vart2[:, 0:width], ps_b[:, 0:width], 1.0 / C, msqt2[:, 0:width], ALU.mult, ALU.subtract),
                     reads=[ps_b, msqt2], writes=[vart2])
                p.op("scalar", lambda e: e.activation(vart2[:, 0:width], vart2[:, 0:width], AF.Ln, bias=epsT[:]),
                     reads=[vart2, epsT], writes=[vart2])
                p.op("scalar", lambda e: e.activation(rstdt2[:, 0:width], vart2[:, 0:width], AF.Exp, scale=-0.5),
                     reads=[vart2], writes=[rstdt2])

            import os as _os
            if _os.environ.get("KDBG2"):
                dbg_early = dram("dbg_early", [C, T], kind="ExternalOutput")
                dearly = E3(nc.sbuf_tensor("dearly", [128, 1024], F32))
                for c in range(NCH):
                    for tt_ in range(4):
                        p.op("sync", (lambda c, tt_: lambda e: e.dma_start(out=dearly[:], in_=ybuf[c * 128:(c + 1) * 128, tt_ * 1024:(tt_ + 1) * 1024]))(c, tt_),
                             writes=[dearly], dma=True, dkey=dearly, dram_in=[("y", w_) for w_ in range(NLW)])
                        p.op("gpsimd", (lambda c, tt_: lambda e: e.dma_start(out=dbg_early[c * 128:(c + 1) * 128, tt_ * 1024:(tt_ + 1) * 1024], in_=dearly[:]))(c, tt_),
                             reads=[dearly], dma=True, dkey=dearly)
                dbg_x2 = dram("dbg_x2", [C, NSLOT * SW], kind="ExternalOutput")
                dbg_yb = dram("dbg_yb", [C, NSLOT * SW], kind="ExternalOutput")
                dbg_rt = dram("dbg_rt", [C, NSLOT * SW], kind="ExternalOutput")
                dbg_x1 = dram("dbg_x1", [C, NSLOT * SLOT], kind="ExternalOutput")
                dbg_fb = dram("dbg_fb", [C, NSLOT * SLOT], kind="ExternalOutput")
            for i in range(NSLOT):
                xe = bufD
                xpr = bufA
                yb = bufB
                rt = bufC
                x2 = bufE
                for c in range(NCH):
                    p.op("sync", (lambda c, i: lambda e: e.dma_start(out=bufD[c][:], in_=xe_in[c * 128:(c + 1) * 128, i * SW:(i + 1) * SW]))(c, i),
                         writes=[bufD[c]], dma=True, dkey=bufD[c])
                    p.op("sync", (lambda c, i: lambda e: e.dma_start(out=bufA[c][:], in_=xpr_in[c * 128:(c + 1) * 128, i * SW:(i + 1) * SW]))(c, i),
                         writes=[bufA[c]], dma=True, dkey=bufA[c])
                s0_ = (2 * i) * 512
                s1_ = (2 * i + 1) * 512
                yreg0 = sorted({("y", (s0_ // 512) % NLW), ("y", ((s0_ - 1) // 512) % NLW)})
                yreg1 = sorted({("y", (s1_ // 512) % NLW), ("y", ((s1_ - 1) // 512) % NLW)})
                for c in range(NCH):
                    s0 = (2 * i) * 512
                    s1 = (2 * i + 1) * 512
                    y0t = ycn[(c % 2) * 2]
                    y1t = ycn[(c % 2) * 2 + 1]
                    y0 = y0t[:, 0:SW]
                    y1 = y1t[:, 0:SW]
                    if i == 0:
                        p.op("sync", (lambda c, y0: lambda e: e.dma_start(out=y0[:, 0:1], in_=ybuf[c * 128:(c + 1) * 128, T - 1:T], allow_slow_non_contiguous=True))(c, y0),
                             writes=[y0], dma=True, dkey=y0t, dram_in=yreg0)
                        p.op("sync", (lambda c, y0: lambda e: e.dma_start(out=y0[:, 1:SW], in_=ybuf[c * 128:(c + 1) * 128, 0:512]))(c, y0),
                             writes=[y0], dma=True, dkey=y0t, dram_in=yreg0)
                    else:
                        p.op("sync", (lambda c, s0, y0: lambda e: e.dma_start(out=y0, in_=ybuf[c * 128:(c + 1) * 128, s0 - 1:s0 + 512]))(c, s0, y0),
                             writes=[y0], dma=True, dkey=y0t, dram_in=yreg0)
                    p.op("sync", (lambda c, s1, y1: lambda e: e.dma_start(out=y1, in_=ybuf[c * 128:(c + 1) * 128, s1 - 1:s1 + 512]))(c, s1, y1),
                         writes=[y1], dma=True, dkey=y1t, dram_in=yreg1)
                    p.op("vector", (lambda c, y1: lambda e: e.tensor_scalar(bufB[c][:], y1, maskt[:], None, ALU.mult))(c, y1),
                         reads=[y1, maskt], writes=[bufB[c]])
                    p.op("vector", (lambda c, y0: lambda e: e.scalar_tensor_tensor(bufB[c][:], y0, maskom[:], bufB[c][:], ALU.mult, ALU.add))(c, y0),
                         reads=[bufB[c], maskom, y0], writes=[bufB[c]])
                ln_stats(xe, SW, 0)
                for c in range(NCH):
                    ds = dscA[c % 2]
                    p.op("gpsimd", (lambda c, ds: lambda e: e.tensor_sub(ds[:], xe[c][:], mtile2[:]))(c, ds),
                         reads=[xe[c], mtile2], writes=[ds])
                    p.op("vector", (lambda c, ds: lambda e: e.scalar_tensor_tensor(ds[:], ds[:], vs(vt["vg1"], c), rstdt2[:], ALU.mult, ALU.mult))(c, ds),
                         reads=[ds, rstdt2, vt["vg1"]], writes=[ds])
                    p.op("vector", (lambda c, ds: lambda e: e.tensor_scalar(pool[c][:], ds[:], vs(vt["vb1"], c), None, ALU.add))(c, ds),
                         reads=[ds, vt["vb1"]], writes=[pool[c]])
                psr = [ps_c, ps_d]
                for m in range(NCH):
                    wbt = wsmL[m % 2]
                    wb = wbt[:]
                    pr = psr[m % 2]
                    p.op("gpsimd", (lambda m, wb: lambda e: e.dma_start(out=wb, in_=wtmr_in[m * 128:(m + 1) * 128, :]))(m, wb),
                         writes=[wbt], dma=True, dkey=wbt)

                    def rgrp(e, wb=wb, pr=pr):
                        for c in range(NCH):
                            inst = e.matmul(pr[:], wb[:, c * 128:(c + 1) * 128],
                                            pool[c][:, 0:SLOT], start=(c == 0), stop=(c == NCH - 1))
                        return inst
                    p.op("tensor", rgrp, reads=[wbt] + [pool[c] for c in range(NCH)], writes=[pr])

                    def rgrp1(e, wb=wb):
                        for c in range(NCH):
                            inst = e.matmul(ps_n1[:], wb[:, c * 128:(c + 1) * 128],
                                            pool[c][:, SLOT:SW], start=(c == 0), stop=(c == NCH - 1))
                        return inst
                    p.op("tensor", rgrp1, reads=[wbt] + [pool[c] for c in range(NCH)], writes=[ps_n1])
                    ds = dscA[m % 2]
                    p.op("gpsimd", (lambda m, ds: lambda e: e.tensor_scalar(ds[:], xpr[m][:], vs(om["vtmr"], m), None, ALU.mult))(m, ds),
                         reads=[xpr[m], om["vtmr"]], writes=[ds])
                    p.op("vector", (lambda m, ds, pr: lambda e: e.scalar_tensor_tensor(rt[m][:, 0:SLOT], pr[:], vs(vt["vtmr"], m), ds[:, 0:SLOT], ALU.mult, ALU.add))(m, ds, pr),
                         reads=[pr, vt["vtmr"], ds], writes=[rt[m]])
                    p.op("vector", (lambda m, ds: lambda e: e.scalar_tensor_tensor(rt[m][:, SLOT:SW], ps_n1[:], vs(vt["vtmr"], m), ds[:, SLOT:SW], ALU.mult, ALU.add))(m, ds),
                         reads=[ps_n1, vt["vtmr"], ds], writes=[rt[m]])
                    p.op("scalar", (lambda m: lambda e: e.activation(rt[m][:], rt[m][:], AF.Sigmoid))(m),
                         reads=[rt[m]], writes=[rt[m]])
                for c in range(NCH):
                    p.op("gpsimd", (lambda c: lambda e: e.tensor_mul(x2[c][:], rt[c][:], yb[c][:]))(c),
                         reads=[rt[c], yb[c]], writes=[x2[c]])
                    p.op("vector", (lambda c: lambda e: e.tensor_add(x2[c][:], x2[c][:], xe[c][:]))(c),
                         reads=[x2[c], xe[c]], writes=[x2[c]])
                    p.op("gpsimd", (lambda c: lambda e: e.tensor_add(x2d[c][:], x2[c][:], xe[c][:]))(c),
                         reads=[x2[c], xe[c]], writes=[x2d[c]])
                import os as _os
                if _os.environ.get("KDBG2"):
                    for c in range(NCH):
                        p.op("gpsimd", (lambda c, i: lambda e: e.dma_start(out=dbg_x2[c * 128:(c + 1) * 128, i * SW:(i + 1) * SW], in_=x2[c][:]))(c, i),
                             reads=[x2[c]], dma=True, dkey=x2[c])
                        p.op("gpsimd", (lambda c, i: lambda e: e.dma_start(out=dbg_yb[c * 128:(c + 1) * 128, i * SW:(i + 1) * SW], in_=bufB[c][:]))(c, i),
                             reads=[bufB[c]], dma=True, dkey=bufB[c])
                        p.op("gpsimd", (lambda c, i: lambda e: e.dma_start(out=dbg_rt[c * 128:(c + 1) * 128, i * SW:(i + 1) * SW], in_=rt[c][:]))(c, i),
                             reads=[rt[c]], dma=True, dkey=rt[c])
                ln_stats(x2, SLOT, 1)
                xppre = bufA
                rr = bufB
                xn2fL = [xn2f, dsc2]
                for c in range(NCH):
                    ds = dscA[c % 2]
                    xf = xn2fL[c % 2]
                    p.op("gpsimd", (lambda c, ds: lambda e: e.tensor_sub(ds[:, 0:SLOT], x2[c][:, 1:SW], mtile2[:, 0:SLOT]))(c, ds),
                         reads=[x2[c], mtile2], writes=[ds])
                    p.op("vector", (lambda c, ds, xf: lambda e: e.scalar_tensor_tensor(xf[:, 0:SLOT], ds[:, 0:SLOT], vs(vt["vg2"], c), rstdt2[:, 0:SLOT], ALU.mult, ALU.mult))(c, ds, xf),
                         reads=[ds, rstdt2, vt["vg2"]], writes=[xf])
                    p.op("vector", (lambda c, xf: lambda e: e.tensor_scalar(xf[:, 0:SLOT], xf[:, 0:SLOT], vs(vt["vb2"], c), None, ALU.add))(c, xf),
                         reads=[xf, vt["vb2"]], writes=[xf])
                    p.op("scalar", (lambda c, xf: lambda e: e.activation(nb16[c][:], xf[:, 0:SLOT], AF.Copy))(c, xf),
                         reads=[xf], writes=[nb16[c]])
                    p.op("gpsimd", (lambda c, ds: lambda e: e.tensor_scalar(ds[:, 0:SLOT], x2[c][:, 0:SLOT], vs(om["vcmr"], c), None, ALU.mult))(c, ds),
                         reads=[x2[c], om["vcmr"]], writes=[ds])
                    p.op("vector", (lambda c, ds, xf: lambda e: e.scalar_tensor_tensor(rr[c][:, 0:SLOT], xf[:, 0:SLOT], vs(vt["vcmr"], c), ds[:, 0:SLOT], ALU.mult, ALU.add))(c, ds, xf),
                         reads=[xf, vt["vcmr"], ds], writes=[rr[c]])
                    p.op("scalar", (lambda c: lambda e: e.activation(rr[c][:, 0:SLOT], rr[c][:, 0:SLOT], AF.Sigmoid))(c),
                         reads=[rr[c]], writes=[rr[c]])
                    p.op("gpsimd", (lambda c: lambda e: e.tensor_scalar(xppre[c][:, 0:SLOT], x2[c][:, 0:SLOT], vs(om["vcmk"], c), None, ALU.mult))(c),
                         reads=[x2[c], om["vcmk"]], writes=[xppre[c]])
                for m in range(32):
                    wbt = wsmL[m % 2]
                    wb = wbt[:]
                    p.op("sync", (lambda m, wb: lambda e: e.dma_start(out=wb, in_=wcm_r[m]))(m, wb),
                         writes=[wbt], dma=True, dkey=wbt, dram_in="wcm")
                    pk = ps_c if (m % 2 == 0) else ps_d

                    def cmgrp(e, wb=wb, pk=pk):
                        for c in range(NCH):
                            inst = e.matmul(pk[:], wb[:, c * 128:(c + 1) * 128],
                                            nb16[c][:], start=(c == 0), stop=(c == NCH - 1))
                        return inst
                    p.op("tensor", cmgrp, reads=[wbt] + [nb16[c] for c in range(NCH)], writes=[pk])
                    cc = m % 8
                    ds = dscA[m % 2]
                    p.op("vector", (lambda cc, pk, ds: lambda e: e.scalar_tensor_tensor(ds[:, 0:SLOT], pk[:], vs(vt["vcmk"], cc), xppre[cc][:, 0:SLOT], ALU.mult, ALU.add))(cc, pk, ds),
                         reads=[pk, vt["vcmk"], xppre[cc]], writes=[ds])
                    p.op("vector", (lambda m, ds: lambda e: e.scalar_tensor_tensor(pool[m][:, 0:SLOT], ds[:, 0:SLOT], 0.0, ds[:, 0:SLOT], ALU.max, ALU.mult))(m, ds),
                         reads=[ds], writes=[pool[m]])
                for o in range(NCH):
                    wbt = wbigL[o % 2]
                    wb = wbt[:]
                    p.op("sync", (lambda o, wb: lambda e: e.dma_start(out=wb, in_=wcp_r[o]))(o, wb),
                         writes=[wbt], dma=True, dkey=wbt, dram_in="wcp")
                    pk = ps_c if (o % 2 == 0) else ps_d

                    def cpgrp(e, wb=wb, pk=pk):
                        for k in range(32):
                            inst = e.matmul(pk[:], wb[:, k * 128:(k + 1) * 128],
                                            pool[k][:, 0:SLOT], start=(k == 0), stop=(k == 31))
                        return inst
                    p.op("tensor", cpgrp, reads=[wbt] + [pool[k] for k in range(32)], writes=[pk])
                    ds = dscA[o % 2]
                    p.op("vector", (lambda o, pk, ds: lambda e: e.tensor_mul(ds[:, 0:SLOT], pk[:], rr[o][:, 0:SLOT]))(o, pk, ds),
                         reads=[pk, rr[o]], writes=[ds])
                    p.op("vector", (lambda o, ds: lambda e: e.tensor_add(x2d[o][:, 1:SW], x2d[o][:, 1:SW], ds[:, 0:SLOT]))(o, ds),
                         reads=[x2d[o], ds], writes=[x2d[o]])
                ln_stats(x2d, SLOT, 1)
                x1f = bufC
                x1b = nb16
                for c in range(NCH):
                    p.op("gpsimd", (lambda c: lambda e: e.tensor_sub(x1f[c][:, 0:SLOT], x2d[c][:, 1:SW], mtile2[:, 0:SLOT]))(c),
                         reads=[x2d[c], mtile2], writes=[x1f[c]])
                    p.op("vector", (lambda c: lambda e: e.tensor_mul(x1f[c][:, 0:SLOT], x1f[c][:, 0:SLOT], rstdt2[:, 0:SLOT]))(c),
                         reads=[x1f[c], rstdt2], writes=[x1f[c]])
                    p.op("scalar", (lambda c: lambda e: e.activation(x1b[c][:], x1f[c][:, 0:SLOT], AF.Copy))(c),
                         reads=[x1f[c]], writes=[x1b[c]])
                faccA = bufD
                faccB = bufE
                for kg in range(4):
                    cbt_ = cbt2
                    cb = cbt_[:]
                    p.op("sync", (lambda kg, cb: lambda e: e.dma_start(out=cb, in_=cbd_b[kg * 128:(kg + 1) * 128, :]))(kg, cb),
                         writes=[cbt_], dma=True, dkey=cbt_, dram_in="cbd")
                    for kl in range(32):
                        k = kg * 32 + kl
                        c = (k // 4) % 8
                        pz = ps_c if (kl % 2 == 0) else ps_d
                        p.op("tensor", (lambda kl, c, pz, cb: lambda e: e.matmul(pz[:], cb[:, kl * 128:(kl + 1) * 128],
                                                                                 x1b[c][:], start=True, stop=True))(kl, c, pz, cb),
                             reads=[cbt_, x1b[c]], writes=[pz])
                        p.op("scalar", (lambda kl, pz: lambda e: e.activation(pool[kl][:, 0:SLOT], pz[:], AF.Gelu))(kl, pz),
                             reads=[pz], writes=[pool[kl]])
                    for o in range(NCH):
                        g = kg * 8 + o
                        wbt = wbigL[o % 2]
                        wb = wbt[:]
                        p.op("sync", (lambda g, wb: lambda e: e.dma_start(out=wb, in_=wpj_r[g]))(g, wb),
                             writes=[wbt], dma=True, dkey=wbt, dram_in="wpj")
                        pf = ps_a if (o % 2 == 0) else ps_b

                        def pjgrp(e, wb=wb, pf=pf):
                            for kl in range(32):
                                inst = e.matmul(pf[:, 0:SLOT], wb[:, kl * 128:(kl + 1) * 128],
                                                pool[kl][:, 0:SLOT], start=(kl == 0), stop=(kl == 31))
                            return inst
                        p.op("tensor", pjgrp, reads=[wbt] + [pool[kl] for kl in range(32)], writes=[pf])
                        if kg == 0:
                            p.op("scalar", (lambda o, pf: lambda e: e.activation(faccA[o][:, 0:SLOT], pf[:, 0:SLOT], AF.Copy))(o, pf),
                                 reads=[pf], writes=[faccA[o]])
                        elif kg % 2 == 1:
                            p.op("vector", (lambda o, pf: lambda e: e.tensor_add(faccB[o][:, 0:SLOT], faccA[o][:, 0:SLOT], pf[:, 0:SLOT]))(o, pf),
                                 reads=[faccA[o], pf], writes=[faccB[o]])
                        else:
                            p.op("vector", (lambda o, pf: lambda e: e.tensor_add(faccA[o][:, 0:SLOT], faccB[o][:, 0:SLOT], pf[:, 0:SLOT]))(o, pf),
                                 reads=[faccB[o], pf], writes=[faccA[o]])
                if _os.environ.get("KDBG2"):
                    for c in range(NCH):
                        p.op("gpsimd", (lambda c, i: lambda e: e.dma_start(out=dbg_x1[c * 128:(c + 1) * 128, i * SLOT:(i + 1) * SLOT], in_=x1f[c][:, 0:SLOT]))(c, i),
                             reads=[x1f[c]], dma=True, dkey=x1f[c])
                        p.op("gpsimd", (lambda c, i: lambda e: e.dma_start(out=dbg_fb[c * 128:(c + 1) * 128, i * SLOT:(i + 1) * SLOT], in_=faccB[c][:, 0:SLOT]))(c, i),
                             reads=[faccB[c]], dma=True, dkey=faccB[c])
                for o in range(NCH):
                    p.op("gpsimd", (lambda o: lambda e: e.tensor_add(faccB[o][:, 0:SLOT], faccB[o][:, 0:SLOT], x1f[o][:, 0:SLOT]))(o),
                         reads=[faccB[o], x1f[o]], writes=[faccB[o]])
                ln_stats(faccB, SLOT, 0)
                for c in range(NCH):
                    p.op("gpsimd", (lambda c: lambda e: e.tensor_sub(outt[:], faccB[c][:, 0:SLOT], mtile2[:, 0:SLOT]))(c),
                         reads=[faccB[c], mtile2], writes=[outt])
                    p.op("vector", (lambda c: lambda e: e.tensor_mul(outt[:], outt[:], rstdt2[:, 0:SLOT]))(c),
                         reads=[outt, rstdt2], writes=[outt])
                    p.op("gpsimd", (lambda c, i: lambda e: e.dma_start(out=out[c * 128:(c + 1) * 128, i * SLOT:(i + 1) * SLOT], in_=outt[:]))(c, i),
                         reads=[outt], dma=True, dkey=outt, dram_out="out")

            import os
            if os.environ.get("KDBG"):
                dbg_k = dram("dbg_k", [C, T], kind="ExternalOutput")
                dbg_v = dram("dbg_v", [C, T], kind="ExternalOutput")
                dbg_y = dram("dbg_y", [C, T], kind="ExternalOutput")
                dtile = dearly if os.environ.get("KDBG2") else E3(nc.sbuf_tensor("dtile", [128, 1024], F32))
                for (srcb, dstb) in ((kbuf, dbg_k), (vbuf, dbg_v), (ybuf, dbg_y)):
                    for c in range(NCH):
                        for tt_ in range(4):
                            p.op("sync", (lambda c, tt_, srcb: lambda e: e.dma_start(out=dtile[:], in_=srcb[c * 128:(c + 1) * 128, tt_ * 1024:(tt_ + 1) * 1024]))(c, tt_, srcb),
                                 writes=[dtile], dma=True, dkey=dtile, dram_in=[("kv", 0, t_) for t_ in range(NTCS)] + [("kv", 1, t_) for t_ in range(NTCS)] + [("y", w_) for w_ in range(NLW)])
                            p.op("gpsimd", (lambda c, tt_, dstb: lambda e: e.dma_start(out=dstb[c * 128:(c + 1) * 128, tt_ * 1024:(tt_ + 1) * 1024], in_=dtile[:]))(c, tt_, dstb),
                                 reads=[dtile], dma=True, dkey=dtile)
            p.op("gpsimd", lambda e: e.engine_nop(),
                 extra_waits=[ev for ev in p.all_events() if ev[0] != "gpsimd"])
            p.op("gpsimd", lambda e: e.engine_nop(),
                 extra_waits=[("gpsimd", p.cnt["gpsimd"])])

            block = E(nc.Block())
            p.replay(block)
    return nc


_CACHE = {}


def kernel(L0_attn_x, L0_attn_time_decay, L0_attn_time_first, L0_attn_W_tm,
           L0_attn_g1, L0_attn_b1, L0_attn_tmk, L0_attn_tmv, L0_attn_tmr,
           L0_attn_W_cm, L0_attn_W_cp, L0_attn_g2, L0_attn_b2, L0_attn_cmk,
           L0_attn_cmr, L0_ffn_rail_maker, L0_ffn_rail_up_proj, L0_ffn_w_cproj):
    f32 = lambda a: np.ascontiguousarray(np.asarray(a), dtype=np.float32)
    x = f32(L0_attn_x)
    Wtm = f32(L0_attn_W_tm)
    Wcm = f32(L0_attn_W_cm)
    Wcp = f32(L0_attn_W_cp)
    Wpj = f32(L0_ffn_w_cproj)
    RM = f32(L0_ffn_rail_maker)
    RU = f32(L0_ffn_rail_up_proj)

    wtm_kv = np.ascontiguousarray(Wtm[:, :2048])
    wtm_r = Wtm[:, 2048:]
    wtmr_host = np.ascontiguousarray(wtm_r.reshape(8, 128, 8, 128).transpose(2, 1, 0, 3).reshape(8 * 128, 8 * 128))
    wcm_host = np.ascontiguousarray(Wcm.reshape(8, 128, 32, 128).transpose(2, 1, 0, 3).reshape(32 * 128, 8 * 128))
    wcp_host = np.ascontiguousarray(Wcp.reshape(32, 128, 8, 128).transpose(2, 1, 0, 3).reshape(8 * 128, 32 * 128))
    WpjT = Wpj.T
    hh, cc_, gg, rl, k16 = np.meshgrid(np.arange(4), np.arange(8), np.arange(4), np.arange(8), np.arange(16), indexing="ij")
    morig = (hh * 4096 + (cc_ * 8 + rl) * 64 + gg * 16 + k16).reshape(-1)
    wpjT_perm = WpjT[morig]
    wpj_host = np.ascontiguousarray(
        wpjT_perm.reshape(4, 32, 128, 8, 128).transpose(0, 3, 2, 1, 4).reshape(32 * 128, 32 * 128))
    rm_host = np.zeros((4, 8, 128, 128), np.float32)
    ru_host = np.zeros((4, 8, 4, 128, 128), np.float32)
    for h4 in range(4):
        for c8 in range(8):
            for rl8 in range(8):
                r = c8 * 8 + rl8
                rm_host[h4, c8, rl8 * 16:(rl8 + 1) * 16, rl8 * 16:(rl8 + 1) * 16] = RM[h4, r].T
                for g4 in range(4):
                    ru_host[h4, c8, g4, rl8 * 16:(rl8 + 1) * 16, rl8 * 16:(rl8 + 1) * 16] = RU[h4, r, g4 * 16:(g4 + 1) * 16, :].T
    rm_host = np.ascontiguousarray(rm_host.reshape(32 * 128, 128))
    ru_host = np.ascontiguousarray(ru_host.reshape(128 * 128, 128))

    vecs = {
        "vw": f32(L0_attn_time_decay).reshape(C, 1),
        "vu": f32(L0_attn_time_first).reshape(C, 1),
        "vtmk": f32(L0_attn_tmk).reshape(C, 1),
        "vtmv": f32(L0_attn_tmv).reshape(C, 1),
        "vtmr": f32(L0_attn_tmr).reshape(C, 1),
        "vcmk": f32(L0_attn_cmk).reshape(C, 1),
        "vcmr": f32(L0_attn_cmr).reshape(C, 1),
        "vg1": f32(L0_attn_g1).reshape(C, 1),
        "vb1": f32(L0_attn_b1).reshape(C, 1),
        "vg2": f32(L0_attn_g2).reshape(C, 1),
        "vb2": f32(L0_attn_b2).reshape(C, 1),
    }

    if "nc" not in _CACHE:
        _CACHE["nc"] = build_program()
    nc = _CACHE["nc"]

    in_maps = []
    for core in range(8):
        b, h = core // 2, core % 2
        xb = np.ascontiguousarray(x[b].T)
        xe_cols = []
        xpr_cols = []
        for i in range(NSLOT):
            s = (2 * i + h) * 512
            toks = (np.arange(-1, SLOT) + s) % T
            xe_cols.append(toks)
            xpr_cols.append(np.where(toks == 0, 0, (toks - 1) % T))
        xe_idx = np.concatenate(xe_cols)
        xpr_idx = np.concatenate(xpr_cols)
        m = {
            "xT": xb,
            "xe_in": np.ascontiguousarray(xb[:, xe_idx]),
            "xpr_in": np.ascontiguousarray(xb[:, xpr_idx]),
            "wtmkv_in": wtm_kv,
            "wtmr_in": wtmr_host,
            "wcm_in": wcm_host,
            "wcp_in": wcp_host,
            "wpj_in": wpj_host,
            "rm_in": rm_host,
            "ru_in": ru_host,
            "mask_in": np.full((128, 1), float(h), np.float32),
        }
        m.update(vecs)
        in_maps.append(m)

    res = run_bass_kernel_spmd(nc, in_maps, core_ids=list(range(8)))
    _CACHE["res"] = res
    full = np.empty((B, T, C), np.float32)
    for core in range(8):
        b, h = core // 2, core % 2
        shard = res.results[core]["out"]
        for i in range(NSLOT):
            s = (2 * i + h) * 512
            full[b, s:s + 512, :] = shard[:, i * SLOT:(i + 1) * SLOT].T
    return full


# version 06
# speedup vs baseline: 1.1838x; 1.0740x over previous
import contextlib
import numpy as np
import concourse.bass as bass
import concourse.mybir as mybir
from concourse.bass_utils import run_bass_kernel_spmd

F32 = mybir.dt.float32
BF16 = mybir.dt.bfloat16
AF = mybir.ActivationFunctionType
ALU = mybir.AluOpType

B, T, C = 4, 4096, 1024
NCH = 8
TCS = 512
NTCS = T // TCS
LW = 512
NLW = T // LW
SLOT = 512
NSLOT = 4
SW = SLOT + 1
LN_EPS = 1e-5
WKV_EPS = 1e-8
NEG = -1e38

ENGINES = ("sync", "tensor", "vector", "scalar", "gpsimd")


class Prog:

    def __init__(self, nc, es):
        self.nc = nc
        self.es = es
        self.ops = {e: [] for e in ENGINES}
        self.sems = {}
        self.cnt = {e: 0 for e in ENGINES}
        self.lastw = {}
        self.reads = {}
        self.dramw = {}
        self.dsem = {}
        self.ndsem = 0
        self.water = {e: {} for e in ENGINES}
        self.pending = {e: [] for e in ENGINES}

    def make_sems(self):
        for e in ENGINES:
            self.sems[e] = self.es.enter_context(self.nc.semaphore(f"sem_{e}"))

    @staticmethod
    def _key(ap):
        t = getattr(ap, "tensor", ap)
        return id(t)

    def _dma_sem(self, dkey):
        k = dkey if isinstance(dkey, str) else self._key(dkey)
        if k not in self.dsem:
            nm = f"dma{self.ndsem}"
            self.ndsem += 1
            self.sems[nm] = self.es.enter_context(self.nc.semaphore(nm))
            self.cnt[nm] = 0
            self.dsem[k] = nm
        return self.dsem[k]

    def op(self, eng, fn, reads=(), writes=(), dma=False, dkey=None,
           dram_in=None, dram_out=None, extra_waits=()):
        waits = {}

        def addw(ev):
            if ev is None:
                return
            sem, val = ev[0], ev[1]
            if sem == eng:
                return
            if waits.get(sem, 0) < val:
                waits[sem] = val

        for ap in reads:
            addw(self.lastw.get(self._key(ap)))
        for ap in writes:
            k = self._key(ap)
            addw(self.lastw.get(k))
            for ev in self.reads.get(k, ()):
                addw(ev)
        if dram_in is not None:
            if not isinstance(dram_in, (list, tuple)):
                dram_in = [dram_in]
            for k_ in dram_in:
                for sem, val in self.dramw.get(k_, {}).items():
                    addw((sem, val))
        for ev in extra_waits:
            addw(ev)
        for ev in self.pending[eng]:
            addw(ev)
        self.pending[eng] = []
        final_waits = []
        for sem, val in waits.items():
            if self.water[eng].get(sem, 0) >= val:
                continue
            self.water[eng][sem] = val
            final_waits.append((sem, val))
        if dma:
            assert dkey is not None, "dma op needs dkey (sbuf-side tile)"
            semn = self._dma_sem(dkey)
            self.cnt[semn] += 16
            ev = (semn, self.cnt[semn])
        else:
            self.cnt[eng] += 1
            ev = (eng, self.cnt[eng])
        self.ops[eng].append((fn, tuple(final_waits), 16 if dma else 1, ev[0]))
        for ap in writes:
            k = self._key(ap)
            self.lastw[k] = ev
            self.reads[k] = []
        for ap in reads:
            self.reads.setdefault(self._key(ap), []).append(ev)
        if dram_out is not None:
            d = self.dramw.setdefault(dram_out, {})
            d[ev[0]] = max(d.get(ev[0], 0), ev[1])
        return ev

    def all_events(self):
        return [(s, c) for s, c in self.cnt.items() if c > 0]

    def barrier(self):
        evs = self.all_events()
        for e in ENGINES:
            self.pending[e] = [ev for ev in evs if ev[0] != e]

    def replay(self, block):
        for e in ENGINES:
            eng_ops = self.ops[e]
            sems = self.sems

            def mk(ops_list):
                def body(engine):
                    for fn, waits, amt, incsem in ops_list:
                        for (sem, val) in waits:
                            engine.wait_ge(sems[sem], val)
                        inst = fn(engine)
                        inst.then_inc(sems[incsem], amt)
                return body
            getattr(block, e)(mk(eng_ops))


def build_program():
    nc = bass.Bass("TRN2", target_bir_lowering=False)

    def dram(name, shape, dt=F32, kind="Internal"):
        return nc.dram_tensor(name, shape, dt, kind=kind).ap()

    xT = dram("xT", [C, T], kind="ExternalInput")
    xe_in = dram("xe_in", [C, NSLOT * SW], kind="ExternalInput")
    xpr_in = dram("xpr_in", [C, NSLOT * SW], kind="ExternalInput")
    wtmkv_in = dram("wtmkv_in", [C, 2048], kind="ExternalInput")
    wtmr_in = dram("wtmr_in", [8 * 128, 8 * 128], kind="ExternalInput")
    wcm_in = dram("wcm_in", [32 * 128, 8 * 128], kind="ExternalInput")
    wcp_in = dram("wcp_in", [8 * 128, 32 * 128], kind="ExternalInput")
    wpj_in = dram("wpj_in", [32 * 128, 32 * 128], kind="ExternalInput")
    rm_in = dram("rm_in", [32 * 128, 128], kind="ExternalInput")
    ru_in = dram("ru_in", [128 * 128, 128], kind="ExternalInput")
    VNAMES = ("vw", "vu", "vtmk", "vtmv", "vtmr", "vcmk", "vcmr", "vg1", "vb1", "vg2", "vb2")
    vecs_in = {nm: dram(nm, [C, 1], kind="ExternalInput") for nm in VNAMES}
    mask_in = dram("mask_in", [128, 1], kind="ExternalInput")
    out = dram("out", [C, NSLOT * SLOT], kind="ExternalOutput")

    kbuf = dram("kbuf", [C, T])
    vbuf = dram("vbuf", [C, T])
    ybuf = dram("ybuf", [C, T])
    wtmr_b = dram("wtmr_b", [8 * 128, 8 * 128], BF16)
    wcm_b = dram("wcm_b", [32 * 128, 8 * 128], BF16)
    wcp_b = dram("wcp_b", [8 * 128, 32 * 128], BF16)
    wpj_b = dram("wpj_b", [32 * 128, 32 * 128], BF16)
    cbd_b = dram("cbd_b", [4 * 128, 32 * 128], BF16)

    with contextlib.ExitStack() as es:
        E = es.enter_context
        p = Prog(nc, es)
        p.make_sems()

        vt = {nm: E(nc.sbuf_tensor(nm + "_t", [128, NCH], F32)) for nm in VNAMES}
        OMN = ("vtmk", "vtmv", "vtmr", "vcmk", "vcmr")
        om = {nm: E(nc.sbuf_tensor("om_" + nm, [128, NCH], F32)) for nm in OMN}
        maskt = E(nc.sbuf_tensor("maskt", [128, 1], F32))
        epsT = E(nc.sbuf_tensor("epsT", [128, 1], F32))
        ones_b = E(nc.sbuf_tensor("ones_b", [128, 128], BF16))
        carryL = [E(nc.sbuf_tensor(f"carry{c}", [128, 4], F32)) for c in range(NCH)]

        def vs(tile, c):
            return tile[:, c:c + 1]

        for nm, d in vecs_in.items():
            for c in range(NCH):
                p.op("sync", (lambda nm, c, d: lambda e: e.dma_start(out=vs(vt[nm], c), in_=d[c * 128:(c + 1) * 128, :]))(nm, c, d),
                     writes=[vt[nm]], dma=True, dkey=vt[nm])
        p.op("sync", lambda e: e.dma_start(out=maskt[:], in_=mask_in), writes=[maskt], dma=True, dkey=maskt)
        maskom = E(nc.sbuf_tensor("maskom", [128, 1], F32))
        p.op("vector", lambda e: e.tensor_scalar(maskom[:], maskt[:], -1.0, 1.0, ALU.mult, ALU.add),
             reads=[maskt], writes=[maskom])
        for nm in OMN:
            for c in range(NCH):
                p.op("vector", (lambda nm, c: lambda e: e.tensor_scalar(vs(om[nm], c), vs(vt[nm], c), -1.0, 1.0, ALU.mult, ALU.add))(nm, c),
                     reads=[vt[nm]], writes=[om[nm]])
        p.op("vector", lambda e: e.memset(epsT[:], LN_EPS), writes=[epsT])
        p.op("vector", lambda e: e.memset(ones_b[:], 1.0), writes=[ones_b])
        for c in range(NCH):
            p.op("vector", (lambda c: lambda e: e.memset(carryL[c][:], 0.0))(c), writes=[carryL[c]])
            p.op("vector", (lambda c: lambda e: e.memset(carryL[c][:, 0:1], NEG))(c), writes=[carryL[c]])
            p.op("vector", (lambda c: lambda e: e.memset(carryL[c][:, 1:3], 1.0))(c), writes=[carryL[c]])

        with contextlib.ExitStack() as es1:
            E1 = es1.enter_context
            for (srcw, dstw, nm) in ((wcm_in, wcm_b, "wcm"), (wcp_in, wcp_b, "wcp"), (wpj_in, wpj_b, "wpj")):
                p.op("gpsimd", (lambda srcw, dstw: lambda e: e.dma_start(out=dstw, in_=srcw))(srcw, dstw),
                     dma=True, dkey="d2d_" + nm, dram_out=nm)
            rmt = E1(nc.sbuf_tensor("rmt", [128, 32 * 128], BF16))
            rut = E1(nc.sbuf_tensor("rut", [128, 128 * 128], BF16))
            cbt = E1(nc.sbuf_tensor("cbt", [128, 32 * 128], BF16))
            pmc = E1(nc.psum_tensor("pmc", [128, 512], F32))
            rm_r = rm_in.rearrange("(t p) f -> p t f", p=128)
            ru_r = ru_in.rearrange("(t p) f -> p t f", p=128)
            p.op("gpsimd", lambda e: e.dma_start(out=rmt[:].rearrange("p (t f) -> p t f", f=128), in_=rm_r),
                 writes=[rmt], dma=True, dkey=rmt)
            for q_ in range(4):
                p.op("gpsimd", (lambda q_: lambda e: e.dma_start(
                    out=rut[:, q_ * 4096:(q_ + 1) * 4096].rearrange("p (t f) -> p t f", f=128),
                    in_=ru_r[:, q_ * 32:(q_ + 1) * 32, :]))(q_),
                    writes=[rut], dma=True, dkey=rut)
            for kg in range(4):
                for kl in range(32):
                    k = kg * 32 + kl
                    hc = k // 4
                    p.op("tensor", (lambda kl, hc, k: lambda e: e.matmul(pmc[:, 0:128], rmt[:, hc * 128:(hc + 1) * 128],
                                                                         rut[:, k * 128:(k + 1) * 128], start=True, stop=True))(kl, hc, k),
                         reads=[rmt, rut], writes=[pmc])
                    p.op("scalar", (lambda kl: lambda e: e.activation(cbt[:, kl * 128:(kl + 1) * 128], pmc[:, 0:128], AF.Copy))(kl),
                         reads=[pmc], writes=[cbt])
                p.op("gpsimd", (lambda kg: lambda e: e.dma_start(out=cbd_b[kg * 128:(kg + 1) * 128, :], in_=cbt[:]))(kg),
                     reads=[cbt], dma=True, dkey=cbt, dram_out="cbd")
        p.barrier()

        with contextlib.ExitStack() as es2:
            E2 = es2.enter_context
            wtmkv_sb = E2(nc.sbuf_tensor("wtmkv_sb", [128, NCH * 2048], BF16))
            for c in range(NCH):
                p.op("gpsimd", (lambda c: lambda e: e.dma_start(out=wtmkv_sb[:, c * 2048:(c + 1) * 2048], in_=wtmkv_in[c * 128:(c + 1) * 128, :]))(c),
                     writes=[wtmkv_sb], dma=True, dkey=wtmkv_sb)

            xt = [E2(nc.sbuf_tensor(f"xt{c}", [128, 1 + TCS], F32)) for c in range(NCH)]
            xb16L = [E2(nc.sbuf_tensor(f"xb16_{j}", [128, TCS], BF16)) for j in range(2)]
            xsqL = [E2(nc.sbuf_tensor(f"xsq{j}", [128, TCS], BF16)) for j in range(2)]
            xn1bL = [[E2(nc.sbuf_tensor(f"xn1b{j}_{c}", [128, TCS], BF16)) for c in range(NCH)] for j in range(2)]
            mtileL = [E2(nc.sbuf_tensor(f"mtile{j}", [128, TCS], F32)) for j in range(2)]
            rstdtL = [E2(nc.sbuf_tensor(f"rstdt{j}", [128, TCS], F32)) for j in range(2)]
            vartL = [E2(nc.sbuf_tensor(f"vart{j}", [128, TCS], F32)) for j in range(2)]
            msqtL = [E2(nc.sbuf_tensor(f"msqt{j}", [128, TCS], F32)) for j in range(2)]
            dscL = [E2(nc.sbuf_tensor(f"sdsc{j}", [128, TCS], F32)) for j in range(4)]
            kvevL = [E2(nc.sbuf_tensor(f"kvev{j}", [128, TCS], F32)) for j in range(4)]
            xppL = [E2(nc.sbuf_tensor(f"xpp{j}", [128, TCS], F32)) for j in range(4)]
            ps_sL = [E2(nc.psum_tensor(f"ps_s{j}", [128, TCS], F32)) for j in range(2)]
            ps_qL = [E2(nc.psum_tensor(f"ps_q{j}", [128, TCS], F32)) for j in range(2)]
            ps_kv = [E2(nc.psum_tensor(f"ps_kv{i}", [128, TCS], F32)) for i in range(2)]

            for tc in range(NTCS):
                t0 = tc * TCS
                par = tc % 2
                xn1b = xn1bL[par]
                mtile, rstdt, vart, msqt = mtileL[par], rstdtL[par], vartL[par], msqtL[par]
                ps_s, ps_q = ps_sL[par], ps_qL[par]
                for c in range(NCH):
                    if tc == 0:
                        p.op("sync", (lambda c: lambda e: e.dma_start(out=xt[c][:, 1:1 + TCS], in_=xT[c * 128:(c + 1) * 128, 0:TCS]))(c),
                             writes=[xt[c]], dma=True, dkey=xt[c])
                        p.op("sync", (lambda c: lambda e: e.dma_start(out=xt[c][:, 0:1], in_=xT[c * 128:(c + 1) * 128, 0:1], allow_slow_non_contiguous=True))(c),
                             writes=[xt[c]], dma=True, dkey=xt[c])
                    else:
                        p.op("sync", (lambda c, t0: lambda e: e.dma_start(out=xt[c][:], in_=xT[c * 128:(c + 1) * 128, t0 - 1:t0 + TCS]))(c, t0),
                             writes=[xt[c]], dma=True, dkey=xt[c])
                for c in range(NCH):
                    xb = xb16L[c % 2]
                    p.op("scalar", (lambda c, xb: lambda e: e.activation(xb[:], xt[c][:, 1:1 + TCS], AF.Copy))(c, xb),
                         reads=[xt[c]], writes=[xb])
                    p.op("tensor", (lambda c, xb: lambda e: e.matmul(ps_s[:], ones_b[:], xb[:], start=(c == 0), stop=(c == NCH - 1)))(c, xb),
                         reads=[ones_b, xb], writes=[ps_s])
                for c in range(NCH):
                    xq = xsqL[c % 2]
                    p.op("scalar", (lambda c, xq: lambda e: e.activation(xq[:], xt[c][:, 1:1 + TCS], AF.Square))(c, xq),
                         reads=[xt[c]], writes=[xq])
                    p.op("tensor", (lambda c, xq: lambda e: e.matmul(ps_q[:], ones_b[:], xq[:], start=(c == 0), stop=(c == NCH - 1)))(c, xq),
                         reads=[ones_b, xq], writes=[ps_q])
                p.op("vector", lambda e: e.tensor_scalar(mtile[:], ps_s[:], 1.0 / C, None, ALU.mult), reads=[ps_s], writes=[mtile])
                p.op("scalar", lambda e: e.activation(msqt[:], mtile[:], AF.Square), reads=[mtile], writes=[msqt])
                p.op("vector", lambda e: e.scalar_tensor_tensor(vart[:], ps_q[:], 1.0 / C, msqt[:], ALU.mult, ALU.subtract),
                     reads=[ps_q, msqt], writes=[vart])
                p.op("scalar", lambda e: e.activation(vart[:], vart[:], AF.Ln, bias=epsT[:]), reads=[vart, epsT], writes=[vart])
                p.op("scalar", lambda e: e.activation(rstdt[:], vart[:], AF.Exp, scale=-0.5), reads=[vart], writes=[rstdt])
                for c in range(NCH):
                    dsc = dscL[(tc * NCH + c) % 4]
                    p.op("vector" if c % 2 == 0 else "gpsimd", (lambda c, dsc: lambda e: e.tensor_sub(dsc[:], xt[c][:, 1:1 + TCS], mtile[:]))(c, dsc),
                         reads=[xt[c], mtile], writes=[dsc])
                    p.op("vector", (lambda c, dsc: lambda e: e.scalar_tensor_tensor(dsc[:], dsc[:], vs(vt["vg1"], c), rstdt[:], ALU.mult, ALU.mult))(c, dsc),
                         reads=[dsc, rstdt, vt["vg1"]], writes=[dsc])
                    p.op("vector", (lambda c, dsc: lambda e: e.tensor_scalar(xn1b[c][:], dsc[:], vs(vt["vb1"], c), None, ALU.add))(c, dsc),
                         reads=[dsc, vt["vb1"]], writes=[xn1b[c]])
                m_order = [mm for cc_ in range(8) for mm in (cc_, cc_ + 8)]
                for mi, m in enumerate(m_order):
                    pkv = ps_kv[mi % 2]

                    def mmgrp(e, m=m, pkv=pkv):
                        for c in range(NCH):
                            inst = e.matmul(pkv[:], wtmkv_sb[:, c * 2048 + m * 128:c * 2048 + (m + 1) * 128],
                                            xn1b[c][:], start=(c == 0), stop=(c == NCH - 1))
                        return inst
                    p.op("tensor", mmgrp, reads=[wtmkv_sb] + [xn1b[c] for c in range(NCH)], writes=[pkv])
                    isk = m < 8
                    cc = m % 8
                    mixv = vt["vtmk"] if isk else vt["vtmv"]
                    omv = om["vtmk"] if isk else om["vtmv"]
                    kvt = kvevL[mi % 4]
                    xpp = xppL[mi % 4]
                    dstbuf = kbuf if isk else vbuf
                    p.op("vector" if mi % 2 == 0 else "gpsimd", (lambda cc, omv, xpp: lambda e: e.tensor_scalar(xpp[:], xt[cc][:, 0:TCS], vs(omv, cc), None, ALU.mult))(cc, omv, xpp),
                         reads=[xt[cc], omv], writes=[xpp])
                    p.op("vector", (lambda pkv, mixv, cc, kvt, xpp: lambda e: e.scalar_tensor_tensor(kvt[:], pkv[:], vs(mixv, cc), xpp[:], ALU.mult, ALU.add))(pkv, mixv, cc, kvt, xpp),
                         reads=[pkv, mixv, xpp], writes=[kvt])
                    p.op("gpsimd", (lambda dstbuf, cc, t0, kvt: lambda e: e.dma_start(out=dstbuf[cc * 128:(cc + 1) * 128, t0:t0 + TCS], in_=kvt[:]))(dstbuf, cc, t0, kvt),
                         reads=[kvt], dma=True, dkey=kvt, dram_out=("kv", 0 if isk else 1, tc))

            wbc = E2(nc.sbuf_tensor("wbc", [128, NCH * LW], F32))
            for c in range(NCH):
                p.op("scalar", (lambda c: lambda e: e.memzero(wbc[:, c * LW:(c + 1) * LW]))(c), writes=[wbc])
            for c in range(NCH):
                p.op("vector", (lambda c: lambda e: e.tensor_scalar(wbc[:, c * LW:(c + 1) * LW], wbc[:, c * LW:(c + 1) * LW], vs(vt["vw"], c), None, ALU.add))(c),
                     reads=[vt["vw"]], writes=[wbc])

            def mkset(j):
                return {nm: E2(nc.sbuf_tensor(f"wkv{nm}{j}", [128, 1 + LW], F32)) for nm in ("mT", "SgH", "Sg", "aaT", "sT")} |                        {nm: E2(nc.sbuf_tensor(f"wkv{nm}{j}", [128, LW], F32)) for nm in ("KT", "VT", "w1", "w2", "w3", "w4", "w5", "w6", "yt")}
            sets = [mkset(0), mkset(1)]

            unit = 0
            for wt in range(NLW):
                t0 = wt * LW
                for c in range(NCH):
                    S = sets[unit % 2]
                    unit += 1
                    mT, SgH, Sg, aaT, sT = S["mT"], S["SgH"], S["Sg"], S["aaT"], S["sT"]
                    KT, VT = S["KT"], S["VT"]
                    w1, w2, w3, w4, w5, w6, yt = S["w1"], S["w2"], S["w3"], S["w4"], S["w5"], S["w6"], S["yt"]
                    car = carryL[c]
                    wv_c = vs(vt["vw"], c)
                    wu_c = vs(vt["vu"], c)
                    wbc_c = wbc[:, c * LW:(c + 1) * LW]
                    p.op("sync", (lambda c, t0, KT: lambda e: e.dma_start(out=KT[:], in_=kbuf[c * 128:(c + 1) * 128, t0:t0 + LW]))(c, t0, KT),
                         writes=[KT], dma=True, dkey=KT, dram_in=[("kv", 0, 2 * wt), ("kv", 0, 2 * wt + 1)] if LW == 1024 else [("kv", 0, wt)])
                    p.op("sync", (lambda c, t0, VT: lambda e: e.dma_start(out=VT[:], in_=vbuf[c * 128:(c + 1) * 128, t0:t0 + LW]))(c, t0, VT),
                         writes=[VT], dma=True, dkey=VT, dram_in=[("kv", 1, 2 * wt), ("kv", 1, 2 * wt + 1)] if LW == 1024 else [("kv", 1, wt)])
                    p.op("scalar", (lambda car, mT: lambda e: e.copy(mT[:, 0:1], car[:, 0:1]))(car, mT), reads=[car], writes=[mT])
                    p.op("vector", (lambda wbc_c, car, mT, KT: lambda e: e.tensor_tensor_scan(mT[:, 1:1 + LW], wbc_c, KT[:], car[:, 0:1], ALU.add, ALU.max))(wbc_c, car, mT, KT),
                         reads=[wbc, KT, car], writes=[mT])
                    p.op("vector", (lambda wv_c, mT, w1: lambda e: e.scalar_tensor_tensor(w1[:], mT[:, 0:LW], wv_c, mT[:, 1:1 + LW], ALU.add, ALU.subtract))(wv_c, mT, w1),
                         reads=[mT, vt["vw"]], writes=[w1])
                    p.op("scalar", (lambda w1: lambda e: e.activation(w1[:], w1[:], AF.Exp))(w1), reads=[w1], writes=[w1])
                    p.op("gpsimd", (lambda KT, mT, w2: lambda e: e.tensor_sub(w2[:], KT[:], mT[:, 1:1 + LW]))(KT, mT, w2), reads=[KT, mT], writes=[w2])
                    p.op("scalar", (lambda w2: lambda e: e.activation(w2[:], w2[:], AF.Exp))(w2), reads=[w2], writes=[w2])
                    p.op("scalar", (lambda car, SgH: lambda e: e.copy(SgH[:, 0:1], car[:, 1:2]))(car, SgH), reads=[car], writes=[SgH])
                    p.op("vector", (lambda car, SgH, w1, w2: lambda e: e.tensor_tensor_scan(SgH[:, 1:1 + LW], w1[:], w2[:], car[:, 1:2], ALU.mult, ALU.add))(car, SgH, w1, w2),
                         reads=[w1, w2, car], writes=[SgH])
                    p.op("gpsimd", (lambda w3, w1, SgH: lambda e: e.tensor_mul(w3[:], w1[:], SgH[:, 0:LW]))(w3, w1, SgH), reads=[w1, SgH], writes=[w3])
                    p.op("vector", (lambda w3, w2: lambda e: e.tensor_max(w3[:], w3[:], w2[:]))(w3, w2), reads=[w3, w2], writes=[w3])
                    p.op("vector", (lambda w3, w2: lambda e: e.scalar_tensor_tensor(w3[:], w3[:], WKV_EPS, w2[:], ALU.mult, ALU.add))(w3, w2),
                         reads=[w3, w2], writes=[w3])
                    p.op("scalar", (lambda car, Sg: lambda e: e.copy(Sg[:, 0:1], car[:, 2:3]))(car, Sg), reads=[car], writes=[Sg])
                    p.op("vector", (lambda car, Sg, w1, w3: lambda e: e.tensor_tensor_scan(Sg[:, 1:1 + LW], w1[:], w3[:], car[:, 2:3], ALU.mult, ALU.add))(car, Sg, w1, w3),
                         reads=[w1, w3, car], writes=[Sg])
                    p.op("scalar", (lambda Sg, sT: lambda e: e.activation(sT[:], Sg[:], AF.Ln))(Sg, sT), reads=[Sg], writes=[sT])
                    p.op("gpsimd", (lambda sT, mT: lambda e: e.tensor_add(sT[:], mT[:], sT[:]))(sT, mT), reads=[mT, sT], writes=[sT])
                    p.op("vector", (lambda wv_c, sT, KT, w4: lambda e: e.scalar_tensor_tensor(w4[:], sT[:, 0:LW], wv_c, KT[:], ALU.add, ALU.subtract))(wv_c, sT, KT, w4),
                         reads=[sT, vt["vw"], KT], writes=[w4])
                    p.op("gpsimd", (lambda w5, w4: lambda e: e.tensor_scalar_min(w5[:], w4[:], 0.0))(w5, w4), reads=[w4], writes=[w5])
                    p.op("scalar", (lambda w5: lambda e: e.activation(w5[:], w5[:], AF.Exp))(w5), reads=[w5], writes=[w5])
                    p.op("gpsimd", (lambda w4: lambda e: e.tensor_scalar_max(w4[:], w4[:], 0.0))(w4), reads=[w4], writes=[w4])
                    p.op("scalar", (lambda w4: lambda e: e.activation(w4[:], w4[:], AF.Exp, scale=-1.0))(w4), reads=[w4], writes=[w4])
                    p.op("gpsimd", (lambda w4, VT: lambda e: e.tensor_mul(w4[:], w4[:], VT[:]))(w4, VT), reads=[w4, VT], writes=[w4])
                    p.op("scalar", (lambda car, aaT: lambda e: e.copy(aaT[:, 0:1], car[:, 3:4]))(car, aaT), reads=[car], writes=[aaT])
                    p.op("vector", (lambda car, aaT, w5, w4: lambda e: e.tensor_tensor_scan(aaT[:, 1:1 + LW], w5[:], w4[:], car[:, 3:4], ALU.mult, ALU.add))(car, aaT, w5, w4),
                         reads=[w5, w4, car], writes=[aaT])
                    p.op("scalar", (lambda car, mT: lambda e: e.copy(car[:, 0:1], mT[:, LW:1 + LW]))(car, mT), reads=[mT], writes=[car])
                    p.op("scalar", (lambda car, SgH: lambda e: e.copy(car[:, 1:2], SgH[:, LW:1 + LW]))(car, SgH), reads=[SgH], writes=[car])
                    p.op("scalar", (lambda car, Sg: lambda e: e.copy(car[:, 2:3], Sg[:, LW:1 + LW]))(car, Sg), reads=[Sg], writes=[car])
                    p.op("scalar", (lambda car, aaT: lambda e: e.copy(car[:, 3:4], aaT[:, LW:1 + LW]))(car, aaT), reads=[aaT], writes=[car])
                    p.op("vector", (lambda wu_c, KT, sT, w6: lambda e: e.scalar_tensor_tensor(w6[:], KT[:], wu_c, sT[:, 0:LW], ALU.add, ALU.subtract))(wu_c, KT, sT, w6),
                         reads=[KT, vt["vu"], sT], writes=[w6])
                    p.op("gpsimd", (lambda w1, w6: lambda e: e.tensor_scalar_min(w1[:], w6[:], 0.0))(w1, w6), reads=[w6], writes=[w1])
                    p.op("scalar", (lambda w1: lambda e: e.activation(w1[:], w1[:], AF.Exp))(w1), reads=[w1], writes=[w1])
                    p.op("gpsimd", (lambda w6: lambda e: e.tensor_scalar_max(w6[:], w6[:], 0.0))(w6), reads=[w6], writes=[w6])
                    p.op("scalar", (lambda w6: lambda e: e.activation(w6[:], w6[:], AF.Exp, scale=-1.0))(w6), reads=[w6], writes=[w6])
                    p.op("gpsimd", (lambda w2, w6, aaT: lambda e: e.tensor_mul(w2[:], w6[:], aaT[:, 0:LW]))(w2, w6, aaT), reads=[w6, aaT], writes=[w2])
                    p.op("vector", (lambda w3, w1, VT: lambda e: e.tensor_mul(w3[:], w1[:], VT[:]))(w3, w1, VT), reads=[w1, VT], writes=[w3])
                    p.op("vector", (lambda w2, w3: lambda e: e.tensor_add(w2[:], w2[:], w3[:]))(w2, w3), reads=[w2, w3], writes=[w2])
                    p.op("vector", (lambda w3, w6, w1: lambda e: e.scalar_tensor_tensor(w3[:], w6[:], WKV_EPS, w1[:], ALU.add, ALU.add))(w3, w6, w1),
                         reads=[w6, w1], writes=[w3])
                    p.op("scalar", (lambda w3: lambda e: e.activation(w3[:], w3[:], AF.Ln))(w3), reads=[w3], writes=[w3])
                    p.op("scalar", (lambda w3: lambda e: e.activation(w3[:], w3[:], AF.Exp, scale=-1.0))(w3), reads=[w3], writes=[w3])
                    p.op("vector", (lambda yt, w2, w3: lambda e: e.tensor_mul(yt[:], w2[:], w3[:]))(yt, w2, w3), reads=[w2, w3], writes=[yt])
                    p.op("gpsimd", (lambda c, t0, yt: lambda e: e.dma_start(out=ybuf[c * 128:(c + 1) * 128, t0:t0 + LW], in_=yt[:]))(c, t0, yt),
                         reads=[yt], dma=True, dkey=yt, dram_out=("y", wt))
        p.barrier()

        with contextlib.ExitStack() as es3:
            E3 = es3.enter_context
            bufA = [E3(nc.sbuf_tensor(f"bufA{c}", [128, SW], F32)) for c in range(NCH)]
            bufB = [E3(nc.sbuf_tensor(f"bufB{c}", [128, SW], F32)) for c in range(NCH)]
            bufC = [E3(nc.sbuf_tensor(f"bufC{c}", [128, SW], F32)) for c in range(NCH)]
            bufD = [E3(nc.sbuf_tensor(f"bufD{c}", [128, SW], F32)) for c in range(NCH)]
            bufE = [E3(nc.sbuf_tensor(f"bufE{c}", [128, SW], F32)) for c in range(NCH)]
            x2d = [E3(nc.sbuf_tensor(f"x2d{c}", [128, SW], F32)) for c in range(NCH)]
            pool = [E3(nc.sbuf_tensor(f"pool{i}", [128, SW], BF16)) for i in range(32)]
            nb16 = [E3(nc.sbuf_tensor(f"nb16_{c}", [128, SLOT], BF16)) for c in range(NCH)]
            ycn = [E3(nc.sbuf_tensor(f"ycn{j}", [128, 544], F32)) for j in range(4)]
            dscA = [E3(nc.sbuf_tensor(f"dscA{j}", [128, SW], F32)) for j in range(2)]
            cbt2 = E3(nc.sbuf_tensor("cbt2", [128, 4096], BF16))
            mtile2 = E3(nc.sbuf_tensor("mtile2", [128, SW], F32))
            rstdt2 = E3(nc.sbuf_tensor("rstdt2", [128, SW], F32))
            vart2 = E3(nc.sbuf_tensor("vart2", [128, SW], F32))
            msqt2 = E3(nc.sbuf_tensor("msqt2", [128, SW], F32))
            dsc2 = E3(nc.sbuf_tensor("dsc2", [128, SW], F32))
            dsc3 = E3(nc.sbuf_tensor("dsc3", [128, SW], F32))
            xn2f = E3(nc.sbuf_tensor("xn2f", [128, SLOT], F32))
            xb16pL = [E3(nc.sbuf_tensor(f"xb16p{j}", [128, SW], BF16)) for j in range(2)]
            xsqpL = [E3(nc.sbuf_tensor(f"xsqp{j}", [128, SW], BF16)) for j in range(2)]
            wsmL = [E3(nc.sbuf_tensor(f"wsm{j}", [128, 1024], BF16)) for j in range(2)]
            wbigL = [E3(nc.sbuf_tensor(f"wbig{j}", [128, 4096], BF16)) for j in range(2)]
            outt = E3(nc.sbuf_tensor("outt", [128, SLOT], F32))
            ps_a = E3(nc.psum_tensor("ps_a", [128, SW], F32))
            ps_b = E3(nc.psum_tensor("ps_b", [128, SW], F32))
            ps_c = E3(nc.psum_tensor("ps_c", [128, SLOT], F32))
            ps_d = E3(nc.psum_tensor("ps_d", [128, SLOT], F32))
            ps_n1 = E3(nc.psum_tensor("ps_n1", [128, 1], F32))

            wcm_r = wcm_b.rearrange("(m p) f -> m p f", p=128)
            wcp_r = wcp_b.rearrange("(o p) f -> o p f", p=128)
            wpj_r = wpj_b.rearrange("(g p) f -> g p f", p=128)

            def ln_stats(src_tiles, width, off):
                pieces = [(0, min(width, SLOT))]
                if width > SLOT:
                    pieces.append((SLOT, width))
                for (ps, actf, scrL) in ((ps_a, AF.Copy, xb16pL), (ps_b, AF.Square, xsqpL)):
                    for lo, hi in pieces:
                        for c in range(NCH):
                            scratch = scrL[c % 2]
                            p.op("scalar", (lambda c, lo, hi, actf, scratch: lambda e: e.activation(
                                scratch[:, lo:hi], src_tiles[c][:, off + lo:off + hi], actf))(c, lo, hi, actf, scratch),
                                 reads=[src_tiles[c]], writes=[scratch])
                            p.op("tensor", (lambda c, lo, hi, ps, scratch: lambda e: e.matmul(
                                ps[:, lo:hi], ones_b[:], scratch[:, lo:hi],
                                start=(c == 0), stop=(c == NCH - 1)))(c, lo, hi, ps, scratch),
                                 reads=[ones_b, scratch], writes=[ps])
                p.op("vector", lambda e: e.tensor_scalar(mtile2[:, 0:width], ps_a[:, 0:width], 1.0 / C, None, ALU.mult),
                     reads=[ps_a], writes=[mtile2])
                p.op("scalar", lambda e: e.activation(msqt2[:, 0:width], mtile2[:, 0:width], AF.Square),
                     reads=[mtile2], writes=[msqt2])
                p.op("vector", lambda e: e.scalar_tensor_tensor(vart2[:, 0:width], ps_b[:, 0:width], 1.0 / C, msqt2[:, 0:width], ALU.mult, ALU.subtract),
                     reads=[ps_b, msqt2], writes=[vart2])
                p.op("scalar", lambda e: e.activation(vart2[:, 0:width], vart2[:, 0:width], AF.Ln, bias=epsT[:]),
                     reads=[vart2, epsT], writes=[vart2])
                p.op("scalar", lambda e: e.activation(rstdt2[:, 0:width], vart2[:, 0:width], AF.Exp, scale=-0.5),
                     reads=[vart2], writes=[rstdt2])

            import os as _os
            if _os.environ.get("KDBG2"):
                dbg_early = dram("dbg_early", [C, T], kind="ExternalOutput")
                dearly = E3(nc.sbuf_tensor("dearly", [128, 1024], F32))
                for c in range(NCH):
                    for tt_ in range(4):
                        p.op("sync", (lambda c, tt_: lambda e: e.dma_start(out=dearly[:], in_=ybuf[c * 128:(c + 1) * 128, tt_ * 1024:(tt_ + 1) * 1024]))(c, tt_),
                             writes=[dearly], dma=True, dkey=dearly, dram_in=[("y", w_) for w_ in range(NLW)])
                        p.op("gpsimd", (lambda c, tt_: lambda e: e.dma_start(out=dbg_early[c * 128:(c + 1) * 128, tt_ * 1024:(tt_ + 1) * 1024], in_=dearly[:]))(c, tt_),
                             reads=[dearly], dma=True, dkey=dearly)
                dbg_x2 = dram("dbg_x2", [C, NSLOT * SW], kind="ExternalOutput")
                dbg_yb = dram("dbg_yb", [C, NSLOT * SW], kind="ExternalOutput")
                dbg_rt = dram("dbg_rt", [C, NSLOT * SW], kind="ExternalOutput")
                dbg_x1 = dram("dbg_x1", [C, NSLOT * SLOT], kind="ExternalOutput")
                dbg_fb = dram("dbg_fb", [C, NSLOT * SLOT], kind="ExternalOutput")
            for i in range(NSLOT):
                xe = bufD
                xpr = bufA
                yb = bufB
                rt = bufC
                x2 = bufE
                for c in range(NCH):
                    p.op("sync", (lambda c, i: lambda e: e.dma_start(out=bufD[c][:], in_=xe_in[c * 128:(c + 1) * 128, i * SW:(i + 1) * SW]))(c, i),
                         writes=[bufD[c]], dma=True, dkey=bufD[c])
                    p.op("sync", (lambda c, i: lambda e: e.dma_start(out=bufA[c][:], in_=xpr_in[c * 128:(c + 1) * 128, i * SW:(i + 1) * SW]))(c, i),
                         writes=[bufA[c]], dma=True, dkey=bufA[c])
                s0_ = (2 * i) * 512
                s1_ = (2 * i + 1) * 512
                yreg0 = sorted({("y", (s0_ // 512) % NLW), ("y", ((s0_ - 1) // 512) % NLW)})
                yreg1 = sorted({("y", (s1_ // 512) % NLW), ("y", ((s1_ - 1) // 512) % NLW)})
                for c in range(NCH):
                    s0 = (2 * i) * 512
                    s1 = (2 * i + 1) * 512
                    y0t = ycn[(c % 2) * 2]
                    y1t = ycn[(c % 2) * 2 + 1]
                    y0 = y0t[:, 0:SW]
                    y1 = y1t[:, 0:SW]
                    if i == 0:
                        p.op("sync", (lambda c, y0: lambda e: e.dma_start(out=y0[:, 0:1], in_=ybuf[c * 128:(c + 1) * 128, T - 1:T], allow_slow_non_contiguous=True))(c, y0),
                             writes=[y0], dma=True, dkey=y0t, dram_in=yreg0)
                        p.op("sync", (lambda c, y0: lambda e: e.dma_start(out=y0[:, 1:SW], in_=ybuf[c * 128:(c + 1) * 128, 0:512]))(c, y0),
                             writes=[y0], dma=True, dkey=y0t, dram_in=yreg0)
                    else:
                        p.op("sync", (lambda c, s0, y0: lambda e: e.dma_start(out=y0, in_=ybuf[c * 128:(c + 1) * 128, s0 - 1:s0 + 512]))(c, s0, y0),
                             writes=[y0], dma=True, dkey=y0t, dram_in=yreg0)
                    p.op("sync", (lambda c, s1, y1: lambda e: e.dma_start(out=y1, in_=ybuf[c * 128:(c + 1) * 128, s1 - 1:s1 + 512]))(c, s1, y1),
                         writes=[y1], dma=True, dkey=y1t, dram_in=yreg1)
                    p.op("vector", (lambda c, y1: lambda e: e.tensor_scalar(bufB[c][:], y1, maskt[:], None, ALU.mult))(c, y1),
                         reads=[y1, maskt], writes=[bufB[c]])
                    p.op("vector", (lambda c, y0: lambda e: e.scalar_tensor_tensor(bufB[c][:], y0, maskom[:], bufB[c][:], ALU.mult, ALU.add))(c, y0),
                         reads=[bufB[c], maskom, y0], writes=[bufB[c]])
                ln_stats(xe, SW, 0)
                for c in range(NCH):
                    ds = dscA[c % 2]
                    p.op("vector" if c % 2 == 0 else "gpsimd", (lambda c, ds: lambda e: e.tensor_sub(ds[:], xe[c][:], mtile2[:]))(c, ds),
                         reads=[xe[c], mtile2], writes=[ds])
                    p.op("vector", (lambda c, ds: lambda e: e.scalar_tensor_tensor(ds[:], ds[:], vs(vt["vg1"], c), rstdt2[:], ALU.mult, ALU.mult))(c, ds),
                         reads=[ds, rstdt2, vt["vg1"]], writes=[ds])
                    p.op("vector", (lambda c, ds: lambda e: e.tensor_scalar(pool[c][:], ds[:], vs(vt["vb1"], c), None, ALU.add))(c, ds),
                         reads=[ds, vt["vb1"]], writes=[pool[c]])
                psr = [ps_c, ps_d]
                for m in range(NCH):
                    wbt = wsmL[m % 2]
                    wb = wbt[:]
                    pr = psr[m % 2]
                    p.op("gpsimd", (lambda m, wb: lambda e: e.dma_start(out=wb, in_=wtmr_in[m * 128:(m + 1) * 128, :]))(m, wb),
                         writes=[wbt], dma=True, dkey=wbt)

                    def rgrp(e, wb=wb, pr=pr):
                        for c in range(NCH):
                            inst = e.matmul(pr[:], wb[:, c * 128:(c + 1) * 128],
                                            pool[c][:, 0:SLOT], start=(c == 0), stop=(c == NCH - 1))
                        return inst
                    p.op("tensor", rgrp, reads=[wbt] + [pool[c] for c in range(NCH)], writes=[pr])

                    def rgrp1(e, wb=wb):
                        for c in range(NCH):
                            inst = e.matmul(ps_n1[:], wb[:, c * 128:(c + 1) * 128],
                                            pool[c][:, SLOT:SW], start=(c == 0), stop=(c == NCH - 1))
                        return inst
                    p.op("tensor", rgrp1, reads=[wbt] + [pool[c] for c in range(NCH)], writes=[ps_n1])
                    ds = dscA[m % 2]
                    p.op("gpsimd", (lambda m, ds: lambda e: e.tensor_scalar(ds[:], xpr[m][:], vs(om["vtmr"], m), None, ALU.mult))(m, ds),
                         reads=[xpr[m], om["vtmr"]], writes=[ds])
                    p.op("vector", (lambda m, ds, pr: lambda e: e.scalar_tensor_tensor(rt[m][:, 0:SLOT], pr[:], vs(vt["vtmr"], m), ds[:, 0:SLOT], ALU.mult, ALU.add))(m, ds, pr),
                         reads=[pr, vt["vtmr"], ds], writes=[rt[m]])
                    p.op("vector", (lambda m, ds: lambda e: e.scalar_tensor_tensor(rt[m][:, SLOT:SW], ps_n1[:], vs(vt["vtmr"], m), ds[:, SLOT:SW], ALU.mult, ALU.add))(m, ds),
                         reads=[ps_n1, vt["vtmr"], ds], writes=[rt[m]])
                    p.op("scalar", (lambda m: lambda e: e.activation(rt[m][:], rt[m][:], AF.Sigmoid))(m),
                         reads=[rt[m]], writes=[rt[m]])
                for c in range(NCH):
                    e1_ = "gpsimd" if c % 2 == 0 else "vector"
                    e2_ = "vector" if c % 2 == 0 else "gpsimd"
                    p.op(e1_, (lambda c: lambda e: e.tensor_mul(x2[c][:], rt[c][:], yb[c][:]))(c),
                         reads=[rt[c], yb[c]], writes=[x2[c]])
                    p.op(e2_, (lambda c: lambda e: e.tensor_add(x2[c][:], x2[c][:], xe[c][:]))(c),
                         reads=[x2[c], xe[c]], writes=[x2[c]])
                    p.op(e1_, (lambda c: lambda e: e.tensor_add(x2d[c][:], x2[c][:], xe[c][:]))(c),
                         reads=[x2[c], xe[c]], writes=[x2d[c]])
                import os as _os
                if _os.environ.get("KDBG2"):
                    for c in range(NCH):
                        p.op("gpsimd", (lambda c, i: lambda e: e.dma_start(out=dbg_x2[c * 128:(c + 1) * 128, i * SW:(i + 1) * SW], in_=x2[c][:]))(c, i),
                             reads=[x2[c]], dma=True, dkey=x2[c])
                        p.op("gpsimd", (lambda c, i: lambda e: e.dma_start(out=dbg_yb[c * 128:(c + 1) * 128, i * SW:(i + 1) * SW], in_=bufB[c][:]))(c, i),
                             reads=[bufB[c]], dma=True, dkey=bufB[c])
                        p.op("gpsimd", (lambda c, i: lambda e: e.dma_start(out=dbg_rt[c * 128:(c + 1) * 128, i * SW:(i + 1) * SW], in_=rt[c][:]))(c, i),
                             reads=[rt[c]], dma=True, dkey=rt[c])
                ln_stats(x2, SLOT, 1)
                xppre = bufA
                rr = bufB
                xn2fL = [xn2f, dsc2]
                for c in range(NCH):
                    ds = dscA[c % 2]
                    xf = xn2fL[c % 2]
                    p.op("vector" if c % 2 == 0 else "gpsimd", (lambda c, ds: lambda e: e.tensor_sub(ds[:, 0:SLOT], x2[c][:, 1:SW], mtile2[:, 0:SLOT]))(c, ds),
                         reads=[x2[c], mtile2], writes=[ds])
                    p.op("vector", (lambda c, ds, xf: lambda e: e.scalar_tensor_tensor(xf[:, 0:SLOT], ds[:, 0:SLOT], vs(vt["vg2"], c), rstdt2[:, 0:SLOT], ALU.mult, ALU.mult))(c, ds, xf),
                         reads=[ds, rstdt2, vt["vg2"]], writes=[xf])
                    p.op("vector", (lambda c, xf: lambda e: e.tensor_scalar(xf[:, 0:SLOT], xf[:, 0:SLOT], vs(vt["vb2"], c), None, ALU.add))(c, xf),
                         reads=[xf, vt["vb2"]], writes=[xf])
                    p.op("scalar", (lambda c, xf: lambda e: e.activation(nb16[c][:], xf[:, 0:SLOT], AF.Copy))(c, xf),
                         reads=[xf], writes=[nb16[c]])
                    p.op("gpsimd", (lambda c, ds: lambda e: e.tensor_scalar(ds[:, 0:SLOT], x2[c][:, 0:SLOT], vs(om["vcmr"], c), None, ALU.mult))(c, ds),
                         reads=[x2[c], om["vcmr"]], writes=[ds])
                    p.op("vector", (lambda c, ds, xf: lambda e: e.scalar_tensor_tensor(rr[c][:, 0:SLOT], xf[:, 0:SLOT], vs(vt["vcmr"], c), ds[:, 0:SLOT], ALU.mult, ALU.add))(c, ds, xf),
                         reads=[xf, vt["vcmr"], ds], writes=[rr[c]])
                    p.op("scalar", (lambda c: lambda e: e.activation(rr[c][:, 0:SLOT], rr[c][:, 0:SLOT], AF.Sigmoid))(c),
                         reads=[rr[c]], writes=[rr[c]])
                    p.op("gpsimd", (lambda c: lambda e: e.tensor_scalar(xppre[c][:, 0:SLOT], x2[c][:, 0:SLOT], vs(om["vcmk"], c), None, ALU.mult))(c),
                         reads=[x2[c], om["vcmk"]], writes=[xppre[c]])
                for m in range(32):
                    wbt = wsmL[m % 2]
                    wb = wbt[:]
                    p.op("sync", (lambda m, wb: lambda e: e.dma_start(out=wb, in_=wcm_r[m]))(m, wb),
                         writes=[wbt], dma=True, dkey=wbt, dram_in="wcm")
                    pk = ps_c if (m % 2 == 0) else ps_d

                    def cmgrp(e, wb=wb, pk=pk):
                        for c in range(NCH):
                            inst = e.matmul(pk[:], wb[:, c * 128:(c + 1) * 128],
                                            nb16[c][:], start=(c == 0), stop=(c == NCH - 1))
                        return inst
                    p.op("tensor", cmgrp, reads=[wbt] + [nb16[c] for c in range(NCH)], writes=[pk])
                    cc = m % 8
                    ds = dscA[m % 2]
                    p.op("vector", (lambda cc, pk, ds: lambda e: e.scalar_tensor_tensor(ds[:, 0:SLOT], pk[:], vs(vt["vcmk"], cc), xppre[cc][:, 0:SLOT], ALU.mult, ALU.add))(cc, pk, ds),
                         reads=[pk, vt["vcmk"], xppre[cc]], writes=[ds])
                    p.op("vector", (lambda m, ds: lambda e: e.scalar_tensor_tensor(pool[m][:, 0:SLOT], ds[:, 0:SLOT], 0.0, ds[:, 0:SLOT], ALU.max, ALU.mult))(m, ds),
                         reads=[ds], writes=[pool[m]])
                for o in range(NCH):
                    wbt = wbigL[o % 2]
                    wb = wbt[:]
                    p.op("sync", (lambda o, wb: lambda e: e.dma_start(out=wb, in_=wcp_r[o]))(o, wb),
                         writes=[wbt], dma=True, dkey=wbt, dram_in="wcp")
                    pk = ps_c if (o % 2 == 0) else ps_d

                    def cpgrp(e, wb=wb, pk=pk):
                        for k in range(32):
                            inst = e.matmul(pk[:], wb[:, k * 128:(k + 1) * 128],
                                            pool[k][:, 0:SLOT], start=(k == 0), stop=(k == 31))
                        return inst
                    p.op("tensor", cpgrp, reads=[wbt] + [pool[k] for k in range(32)], writes=[pk])
                    ds = dscA[o % 2]
                    p.op("vector", (lambda o, pk, ds: lambda e: e.tensor_mul(ds[:, 0:SLOT], pk[:], rr[o][:, 0:SLOT]))(o, pk, ds),
                         reads=[pk, rr[o]], writes=[ds])
                    p.op("vector", (lambda o, ds: lambda e: e.tensor_add(x2d[o][:, 1:SW], x2d[o][:, 1:SW], ds[:, 0:SLOT]))(o, ds),
                         reads=[x2d[o], ds], writes=[x2d[o]])
                ln_stats(x2d, SLOT, 1)
                x1f = bufC
                x1b = nb16
                for c in range(NCH):
                    p.op("vector" if c % 2 == 0 else "gpsimd", (lambda c: lambda e: e.tensor_sub(x1f[c][:, 0:SLOT], x2d[c][:, 1:SW], mtile2[:, 0:SLOT]))(c),
                         reads=[x2d[c], mtile2], writes=[x1f[c]])
                    p.op("vector", (lambda c: lambda e: e.tensor_mul(x1f[c][:, 0:SLOT], x1f[c][:, 0:SLOT], rstdt2[:, 0:SLOT]))(c),
                         reads=[x1f[c], rstdt2], writes=[x1f[c]])
                    p.op("scalar", (lambda c: lambda e: e.activation(x1b[c][:], x1f[c][:, 0:SLOT], AF.Copy))(c),
                         reads=[x1f[c]], writes=[x1b[c]])
                faccA = bufD
                faccB = bufE
                for kg in range(4):
                    cbt_ = cbt2
                    cb = cbt_[:]
                    p.op("sync", (lambda kg, cb: lambda e: e.dma_start(out=cb, in_=cbd_b[kg * 128:(kg + 1) * 128, :]))(kg, cb),
                         writes=[cbt_], dma=True, dkey=cbt_, dram_in="cbd")
                    for kl in range(32):
                        k = kg * 32 + kl
                        c = (k // 4) % 8
                        pz = ps_c if (kl % 2 == 0) else ps_d
                        p.op("tensor", (lambda kl, c, pz, cb: lambda e: e.matmul(pz[:], cb[:, kl * 128:(kl + 1) * 128],
                                                                                 x1b[c][:], start=True, stop=True))(kl, c, pz, cb),
                             reads=[cbt_, x1b[c]], writes=[pz])
                        p.op("scalar", (lambda kl, pz: lambda e: e.activation(pool[kl][:, 0:SLOT], pz[:], AF.Gelu))(kl, pz),
                             reads=[pz], writes=[pool[kl]])
                    for o in range(NCH):
                        g = kg * 8 + o
                        wbt = wbigL[o % 2]
                        wb = wbt[:]
                        p.op("sync", (lambda g, wb: lambda e: e.dma_start(out=wb, in_=wpj_r[g]))(g, wb),
                             writes=[wbt], dma=True, dkey=wbt, dram_in="wpj")
                        pf = ps_a if (o % 2 == 0) else ps_b

                        def pjgrp(e, wb=wb, pf=pf):
                            for kl in range(32):
                                inst = e.matmul(pf[:, 0:SLOT], wb[:, kl * 128:(kl + 1) * 128],
                                                pool[kl][:, 0:SLOT], start=(kl == 0), stop=(kl == 31))
                            return inst
                        p.op("tensor", pjgrp, reads=[wbt] + [pool[kl] for kl in range(32)], writes=[pf])
                        if kg == 0:
                            p.op("scalar", (lambda o, pf: lambda e: e.activation(faccA[o][:, 0:SLOT], pf[:, 0:SLOT], AF.Copy))(o, pf),
                                 reads=[pf], writes=[faccA[o]])
                        elif kg % 2 == 1:
                            p.op("vector", (lambda o, pf: lambda e: e.tensor_add(faccB[o][:, 0:SLOT], faccA[o][:, 0:SLOT], pf[:, 0:SLOT]))(o, pf),
                                 reads=[faccA[o], pf], writes=[faccB[o]])
                        else:
                            p.op("vector", (lambda o, pf: lambda e: e.tensor_add(faccA[o][:, 0:SLOT], faccB[o][:, 0:SLOT], pf[:, 0:SLOT]))(o, pf),
                                 reads=[faccB[o], pf], writes=[faccA[o]])
                if _os.environ.get("KDBG2"):
                    for c in range(NCH):
                        p.op("gpsimd", (lambda c, i: lambda e: e.dma_start(out=dbg_x1[c * 128:(c + 1) * 128, i * SLOT:(i + 1) * SLOT], in_=x1f[c][:, 0:SLOT]))(c, i),
                             reads=[x1f[c]], dma=True, dkey=x1f[c])
                        p.op("gpsimd", (lambda c, i: lambda e: e.dma_start(out=dbg_fb[c * 128:(c + 1) * 128, i * SLOT:(i + 1) * SLOT], in_=faccB[c][:, 0:SLOT]))(c, i),
                             reads=[faccB[c]], dma=True, dkey=faccB[c])
                for o in range(NCH):
                    p.op("gpsimd", (lambda o: lambda e: e.tensor_add(faccB[o][:, 0:SLOT], faccB[o][:, 0:SLOT], x1f[o][:, 0:SLOT]))(o),
                         reads=[faccB[o], x1f[o]], writes=[faccB[o]])
                ln_stats(faccB, SLOT, 0)
                for c in range(NCH):
                    p.op("vector" if c % 2 == 0 else "gpsimd", (lambda c: lambda e: e.tensor_sub(outt[:], faccB[c][:, 0:SLOT], mtile2[:, 0:SLOT]))(c),
                         reads=[faccB[c], mtile2], writes=[outt])
                    p.op("vector", (lambda c: lambda e: e.tensor_mul(outt[:], outt[:], rstdt2[:, 0:SLOT]))(c),
                         reads=[outt, rstdt2], writes=[outt])
                    p.op("gpsimd", (lambda c, i: lambda e: e.dma_start(out=out[c * 128:(c + 1) * 128, i * SLOT:(i + 1) * SLOT], in_=outt[:]))(c, i),
                         reads=[outt], dma=True, dkey=outt, dram_out="out")

            import os
            if os.environ.get("KDBG"):
                dbg_k = dram("dbg_k", [C, T], kind="ExternalOutput")
                dbg_v = dram("dbg_v", [C, T], kind="ExternalOutput")
                dbg_y = dram("dbg_y", [C, T], kind="ExternalOutput")
                dtile = dearly if os.environ.get("KDBG2") else E3(nc.sbuf_tensor("dtile", [128, 1024], F32))
                for (srcb, dstb) in ((kbuf, dbg_k), (vbuf, dbg_v), (ybuf, dbg_y)):
                    for c in range(NCH):
                        for tt_ in range(4):
                            p.op("sync", (lambda c, tt_, srcb: lambda e: e.dma_start(out=dtile[:], in_=srcb[c * 128:(c + 1) * 128, tt_ * 1024:(tt_ + 1) * 1024]))(c, tt_, srcb),
                                 writes=[dtile], dma=True, dkey=dtile, dram_in=[("kv", 0, t_) for t_ in range(NTCS)] + [("kv", 1, t_) for t_ in range(NTCS)] + [("y", w_) for w_ in range(NLW)])
                            p.op("gpsimd", (lambda c, tt_, dstb: lambda e: e.dma_start(out=dstb[c * 128:(c + 1) * 128, tt_ * 1024:(tt_ + 1) * 1024], in_=dtile[:]))(c, tt_, dstb),
                                 reads=[dtile], dma=True, dkey=dtile)
            p.op("gpsimd", lambda e: e.engine_nop(),
                 extra_waits=[ev for ev in p.all_events() if ev[0] != "gpsimd"])
            p.op("gpsimd", lambda e: e.engine_nop(),
                 extra_waits=[("gpsimd", p.cnt["gpsimd"])])

            block = E(nc.Block())
            p.replay(block)
    return nc


_CACHE = {}


def kernel(L0_attn_x, L0_attn_time_decay, L0_attn_time_first, L0_attn_W_tm,
           L0_attn_g1, L0_attn_b1, L0_attn_tmk, L0_attn_tmv, L0_attn_tmr,
           L0_attn_W_cm, L0_attn_W_cp, L0_attn_g2, L0_attn_b2, L0_attn_cmk,
           L0_attn_cmr, L0_ffn_rail_maker, L0_ffn_rail_up_proj, L0_ffn_w_cproj):
    f32 = lambda a: np.ascontiguousarray(np.asarray(a), dtype=np.float32)
    x = f32(L0_attn_x)
    Wtm = f32(L0_attn_W_tm)
    Wcm = f32(L0_attn_W_cm)
    Wcp = f32(L0_attn_W_cp)
    Wpj = f32(L0_ffn_w_cproj)
    RM = f32(L0_ffn_rail_maker)
    RU = f32(L0_ffn_rail_up_proj)

    wtm_kv = np.ascontiguousarray(Wtm[:, :2048])
    wtm_r = Wtm[:, 2048:]
    wtmr_host = np.ascontiguousarray(wtm_r.reshape(8, 128, 8, 128).transpose(2, 1, 0, 3).reshape(8 * 128, 8 * 128))
    wcm_host = np.ascontiguousarray(Wcm.reshape(8, 128, 32, 128).transpose(2, 1, 0, 3).reshape(32 * 128, 8 * 128))
    wcp_host = np.ascontiguousarray(Wcp.reshape(32, 128, 8, 128).transpose(2, 1, 0, 3).reshape(8 * 128, 32 * 128))
    WpjT = Wpj.T
    hh, cc_, gg, rl, k16 = np.meshgrid(np.arange(4), np.arange(8), np.arange(4), np.arange(8), np.arange(16), indexing="ij")
    morig = (hh * 4096 + (cc_ * 8 + rl) * 64 + gg * 16 + k16).reshape(-1)
    wpjT_perm = WpjT[morig]
    wpj_host = np.ascontiguousarray(
        wpjT_perm.reshape(4, 32, 128, 8, 128).transpose(0, 3, 2, 1, 4).reshape(32 * 128, 32 * 128))
    rm_host = np.zeros((4, 8, 128, 128), np.float32)
    ru_host = np.zeros((4, 8, 4, 128, 128), np.float32)
    for h4 in range(4):
        for c8 in range(8):
            for rl8 in range(8):
                r = c8 * 8 + rl8
                rm_host[h4, c8, rl8 * 16:(rl8 + 1) * 16, rl8 * 16:(rl8 + 1) * 16] = RM[h4, r].T
                for g4 in range(4):
                    ru_host[h4, c8, g4, rl8 * 16:(rl8 + 1) * 16, rl8 * 16:(rl8 + 1) * 16] = RU[h4, r, g4 * 16:(g4 + 1) * 16, :].T
    rm_host = np.ascontiguousarray(rm_host.reshape(32 * 128, 128))
    ru_host = np.ascontiguousarray(ru_host.reshape(128 * 128, 128))

    vecs = {
        "vw": f32(L0_attn_time_decay).reshape(C, 1),
        "vu": f32(L0_attn_time_first).reshape(C, 1),
        "vtmk": f32(L0_attn_tmk).reshape(C, 1),
        "vtmv": f32(L0_attn_tmv).reshape(C, 1),
        "vtmr": f32(L0_attn_tmr).reshape(C, 1),
        "vcmk": f32(L0_attn_cmk).reshape(C, 1),
        "vcmr": f32(L0_attn_cmr).reshape(C, 1),
        "vg1": f32(L0_attn_g1).reshape(C, 1),
        "vb1": f32(L0_attn_b1).reshape(C, 1),
        "vg2": f32(L0_attn_g2).reshape(C, 1),
        "vb2": f32(L0_attn_b2).reshape(C, 1),
    }

    if "nc" not in _CACHE:
        _CACHE["nc"] = build_program()
    nc = _CACHE["nc"]

    in_maps = []
    for core in range(8):
        b, h = core // 2, core % 2
        xb = np.ascontiguousarray(x[b].T)
        xe_cols = []
        xpr_cols = []
        for i in range(NSLOT):
            s = (2 * i + h) * 512
            toks = (np.arange(-1, SLOT) + s) % T
            xe_cols.append(toks)
            xpr_cols.append(np.where(toks == 0, 0, (toks - 1) % T))
        xe_idx = np.concatenate(xe_cols)
        xpr_idx = np.concatenate(xpr_cols)
        m = {
            "xT": xb,
            "xe_in": np.ascontiguousarray(xb[:, xe_idx]),
            "xpr_in": np.ascontiguousarray(xb[:, xpr_idx]),
            "wtmkv_in": wtm_kv,
            "wtmr_in": wtmr_host,
            "wcm_in": wcm_host,
            "wcp_in": wcp_host,
            "wpj_in": wpj_host,
            "rm_in": rm_host,
            "ru_in": ru_host,
            "mask_in": np.full((128, 1), float(h), np.float32),
        }
        m.update(vecs)
        in_maps.append(m)

    res = run_bass_kernel_spmd(nc, in_maps, core_ids=list(range(8)))
    _CACHE["res"] = res
    full = np.empty((B, T, C), np.float32)
    for core in range(8):
        b, h = core // 2, core % 2
        shard = res.results[core]["out"]
        for i in range(NSLOT):
            s = (2 * i + h) * 512
            full[b, s:s + 512, :] = shard[:, i * SLOT:(i + 1) * SLOT].T
    return full


# version 07
# speedup vs baseline: 1.5088x; 1.2746x over previous
import contextlib
import numpy as np
import concourse.bass as bass
import concourse.mybir as mybir
from concourse.bass_utils import run_bass_kernel_spmd

F32 = mybir.dt.float32
BF16 = mybir.dt.bfloat16
AF = mybir.ActivationFunctionType
ALU = mybir.AluOpType

B, T, C = 4, 4096, 1024
NCH = 8
TCS = 512
NTCS = T // TCS
LW = 512
NLW = T // LW
SLOT = 512
NSLOT = 4
SW = SLOT + 1
LN_EPS = 1e-5
WKV_EPS = 1e-8
NEG = -1e38

ENGINES = ("sync", "tensor", "vector", "scalar", "gpsimd")


class Prog:

    def __init__(self, nc, es):
        self.nc = nc
        self.es = es
        self.ops = {e: [] for e in ENGINES}
        self.sems = {}
        self.cnt = {e: 0 for e in ENGINES}
        self.lastw = {}
        self.reads = {}
        self.dramw = {}
        self.dsem = {}
        self.ndsem = 0
        self.water = {e: {} for e in ENGINES}
        self.pending = {e: [] for e in ENGINES}

    def make_sems(self):
        for e in ENGINES:
            self.sems[e] = self.es.enter_context(self.nc.semaphore(f"sem_{e}"))

    @staticmethod
    def _key(ap):
        t = getattr(ap, "tensor", ap)
        return id(t)

    def _dma_sem(self, dkey):
        k = dkey if isinstance(dkey, str) else self._key(dkey)
        if k not in self.dsem:
            nm = f"dma{self.ndsem}"
            self.ndsem += 1
            self.sems[nm] = self.es.enter_context(self.nc.semaphore(nm))
            self.cnt[nm] = 0
            self.dsem[k] = nm
        return self.dsem[k]

    def op(self, eng, fn, reads=(), writes=(), dma=False, dkey=None,
           dram_in=None, dram_out=None, extra_waits=()):
        waits = {}

        def addw(ev):
            if ev is None:
                return
            sem, val = ev[0], ev[1]
            if sem == eng:
                return
            if waits.get(sem, 0) < val:
                waits[sem] = val

        for ap in reads:
            addw(self.lastw.get(self._key(ap)))
        for ap in writes:
            k = self._key(ap)
            addw(self.lastw.get(k))
            for ev in self.reads.get(k, ()):
                addw(ev)
        if dram_in is not None:
            if not isinstance(dram_in, (list, tuple)):
                dram_in = [dram_in]
            for k_ in dram_in:
                for sem, val in self.dramw.get(k_, {}).items():
                    addw((sem, val))
        for ev in extra_waits:
            addw(ev)
        for ev in self.pending[eng]:
            addw(ev)
        self.pending[eng] = []
        final_waits = []
        for sem, val in waits.items():
            if self.water[eng].get(sem, 0) >= val:
                continue
            self.water[eng][sem] = val
            final_waits.append((sem, val))
        if dma:
            assert dkey is not None, "dma op needs dkey (sbuf-side tile)"
            semn = self._dma_sem(dkey)
            self.cnt[semn] += 16
            ev = (semn, self.cnt[semn])
        else:
            self.cnt[eng] += 1
            ev = (eng, self.cnt[eng])
        self.ops[eng].append((fn, tuple(final_waits), 16 if dma else 1, ev[0]))
        for ap in writes:
            k = self._key(ap)
            self.lastw[k] = ev
            self.reads[k] = []
        for ap in reads:
            self.reads.setdefault(self._key(ap), []).append(ev)
        if dram_out is not None:
            d = self.dramw.setdefault(dram_out, {})
            d[ev[0]] = max(d.get(ev[0], 0), ev[1])
        return ev

    def all_events(self):
        return [(s, c) for s, c in self.cnt.items() if c > 0]

    def barrier(self):
        evs = self.all_events()
        for e in ENGINES:
            self.pending[e] = [ev for ev in evs if ev[0] != e]

    def replay(self, block):
        for e in ENGINES:
            eng_ops = self.ops[e]
            sems = self.sems

            def mk(ops_list):
                def body(engine):
                    for fn, waits, amt, incsem in ops_list:
                        for (sem, val) in waits:
                            engine.wait_ge(sems[sem], val)
                        inst = fn(engine)
                        inst.then_inc(sems[incsem], amt)
                return body
            getattr(block, e)(mk(eng_ops))


def build_program():
    nc = bass.Bass("TRN2", target_bir_lowering=False)

    def dram(name, shape, dt=F32, kind="Internal"):
        return nc.dram_tensor(name, shape, dt, kind=kind).ap()

    xT = dram("xT", [C, T], kind="ExternalInput")
    xe_in = dram("xe_in", [C, NSLOT * SW], kind="ExternalInput")
    xpr_in = dram("xpr_in", [C, NSLOT * SW], kind="ExternalInput")
    wtmkv_in = dram("wtmkv_in", [C, 2048], kind="ExternalInput")
    wtmr_in = dram("wtmr_in", [8 * 128, 8 * 128], kind="ExternalInput")
    wcm_in = dram("wcm_in", [32 * 128, 8 * 128], kind="ExternalInput")
    wcp_in = dram("wcp_in", [8 * 128, 32 * 128], kind="ExternalInput")
    wpj_in = dram("wpj_in", [32 * 128, 32 * 128], kind="ExternalInput")
    rm_in = dram("rm_in", [32 * 128, 128], kind="ExternalInput")
    ru_in = dram("ru_in", [128 * 128, 128], kind="ExternalInput")
    VNAMES = ("vw", "vu", "vtmk", "vtmv", "vtmr", "vcmk", "vcmr", "vg1", "vb1", "vg2", "vb2")
    vecs_in = {nm: dram(nm, [C, 1], kind="ExternalInput") for nm in VNAMES}
    mask_in = dram("mask_in", [128, 1], kind="ExternalInput")
    out = dram("out", [C, NSLOT * SLOT], kind="ExternalOutput")

    kbuf = dram("kbuf", [C, T])
    vbuf = dram("vbuf", [C, T])
    ybuf = dram("ybuf", [C, T])
    wtmr_b = dram("wtmr_b", [8 * 128, 8 * 128], BF16)
    wcm_b = dram("wcm_b", [32 * 128, 8 * 128], BF16)
    wcp_b = dram("wcp_b", [8 * 128, 32 * 128], BF16)
    wpj_b = dram("wpj_b", [32 * 128, 32 * 128], BF16)
    cbd_b = dram("cbd_b", [4 * 128, 32 * 128], BF16)

    with contextlib.ExitStack() as es:
        E = es.enter_context
        p = Prog(nc, es)
        p.make_sems()

        vt = {nm: E(nc.sbuf_tensor(nm + "_t", [128, NCH], F32)) for nm in VNAMES}
        OMN = ("vtmk", "vtmv", "vtmr", "vcmk", "vcmr")
        om = {nm: E(nc.sbuf_tensor("om_" + nm, [128, NCH], F32)) for nm in OMN}
        maskt = E(nc.sbuf_tensor("maskt", [128, 1], F32))
        epsT = E(nc.sbuf_tensor("epsT", [128, 1], F32))
        ones_b = E(nc.sbuf_tensor("ones_b", [128, 128], BF16))
        carryL = [E(nc.sbuf_tensor(f"carry{c}", [128, 4], F32)) for c in range(NCH)]

        def vs(tile, c):
            return tile[:, c:c + 1]

        for nm, d in vecs_in.items():
            for c in range(NCH):
                p.op("sync", (lambda nm, c, d: lambda e: e.dma_start(out=vs(vt[nm], c), in_=d[c * 128:(c + 1) * 128, :]))(nm, c, d),
                     writes=[vt[nm]], dma=True, dkey=vt[nm])
        p.op("sync", lambda e: e.dma_start(out=maskt[:], in_=mask_in), writes=[maskt], dma=True, dkey=maskt)
        maskom = E(nc.sbuf_tensor("maskom", [128, 1], F32))
        p.op("vector", lambda e: e.tensor_scalar(maskom[:], maskt[:], -1.0, 1.0, ALU.mult, ALU.add),
             reads=[maskt], writes=[maskom])
        for nm in OMN:
            for c in range(NCH):
                p.op("vector", (lambda nm, c: lambda e: e.tensor_scalar(vs(om[nm], c), vs(vt[nm], c), -1.0, 1.0, ALU.mult, ALU.add))(nm, c),
                     reads=[vt[nm]], writes=[om[nm]])
        p.op("vector", lambda e: e.memset(epsT[:], LN_EPS), writes=[epsT])
        p.op("vector", lambda e: e.memset(ones_b[:], 1.0), writes=[ones_b])
        for c in range(NCH):
            p.op("vector", (lambda c: lambda e: e.memset(carryL[c][:], 0.0))(c), writes=[carryL[c]])
            p.op("vector", (lambda c: lambda e: e.memset(carryL[c][:, 0:1], NEG))(c), writes=[carryL[c]])
            p.op("vector", (lambda c: lambda e: e.memset(carryL[c][:, 1:3], 1.0))(c), writes=[carryL[c]])

        with contextlib.ExitStack() as es1:
            E1 = es1.enter_context
            for (srcw, dstw, nm) in ((wcm_in, wcm_b, "wcm"), (wcp_in, wcp_b, "wcp"), (wpj_in, wpj_b, "wpj")):
                p.op("gpsimd", (lambda srcw, dstw: lambda e: e.dma_start(out=dstw, in_=srcw))(srcw, dstw),
                     dma=True, dkey="d2d_" + nm, dram_out=nm)
            rmt = E1(nc.sbuf_tensor("rmt", [128, 32 * 128], BF16))
            rut = E1(nc.sbuf_tensor("rut", [128, 128 * 128], BF16))
            cbt = E1(nc.sbuf_tensor("cbt", [128, 32 * 128], BF16))
            pmc = E1(nc.psum_tensor("pmc", [128, 512], F32))
            rm_r = rm_in.rearrange("(t p) f -> p t f", p=128)
            ru_r = ru_in.rearrange("(t p) f -> p t f", p=128)
            p.op("gpsimd", lambda e: e.dma_start(out=rmt[:].rearrange("p (t f) -> p t f", f=128), in_=rm_r),
                 writes=[rmt], dma=True, dkey=rmt)
            for q_ in range(4):
                p.op("gpsimd", (lambda q_: lambda e: e.dma_start(
                    out=rut[:, q_ * 4096:(q_ + 1) * 4096].rearrange("p (t f) -> p t f", f=128),
                    in_=ru_r[:, q_ * 32:(q_ + 1) * 32, :]))(q_),
                    writes=[rut], dma=True, dkey=rut)
            for kg in range(4):
                for kl in range(32):
                    k = kg * 32 + kl
                    hc = k // 4
                    p.op("tensor", (lambda kl, hc, k: lambda e: e.matmul(pmc[:, 0:128], rmt[:, hc * 128:(hc + 1) * 128],
                                                                         rut[:, k * 128:(k + 1) * 128], start=True, stop=True))(kl, hc, k),
                         reads=[rmt, rut], writes=[pmc])
                    p.op("scalar", (lambda kl: lambda e: e.activation(cbt[:, kl * 128:(kl + 1) * 128], pmc[:, 0:128], AF.Copy))(kl),
                         reads=[pmc], writes=[cbt])
                p.op("gpsimd", (lambda kg: lambda e: e.dma_start(out=cbd_b[kg * 128:(kg + 1) * 128, :], in_=cbt[:]))(kg),
                     reads=[cbt], dma=True, dkey=cbt, dram_out="cbd")
        p.barrier()

        with contextlib.ExitStack() as es2:
            E2 = es2.enter_context
            wtmkv_sb = E2(nc.sbuf_tensor("wtmkv_sb", [128, NCH * 2048], BF16))
            for c in range(NCH):
                p.op("gpsimd", (lambda c: lambda e: e.dma_start(out=wtmkv_sb[:, c * 2048:(c + 1) * 2048], in_=wtmkv_in[c * 128:(c + 1) * 128, :]))(c),
                     writes=[wtmkv_sb], dma=True, dkey=wtmkv_sb)

            xt = [E2(nc.sbuf_tensor(f"xt{c}", [128, 1 + TCS], F32)) for c in range(NCH)]
            xb16L = [E2(nc.sbuf_tensor(f"xb16_{j}", [128, TCS], BF16)) for j in range(2)]
            xsqL = [E2(nc.sbuf_tensor(f"xsq{j}", [128, TCS], BF16)) for j in range(2)]
            xn1bL = [[E2(nc.sbuf_tensor(f"xn1b{j}_{c}", [128, TCS], BF16)) for c in range(NCH)] for j in range(2)]
            mtileL = [E2(nc.sbuf_tensor(f"mtile{j}", [128, TCS], F32)) for j in range(2)]
            rstdtL = [E2(nc.sbuf_tensor(f"rstdt{j}", [128, TCS], F32)) for j in range(2)]
            vartL = [E2(nc.sbuf_tensor(f"vart{j}", [128, TCS], F32)) for j in range(2)]
            msqtL = [E2(nc.sbuf_tensor(f"msqt{j}", [128, TCS], F32)) for j in range(2)]
            dscL = [E2(nc.sbuf_tensor(f"sdsc{j}", [128, TCS], F32)) for j in range(4)]
            kvevL = [E2(nc.sbuf_tensor(f"kvev{j}", [128, TCS], F32)) for j in range(4)]
            xppL = [E2(nc.sbuf_tensor(f"xpp{j}", [128, TCS], F32)) for j in range(4)]
            ps_sL = [E2(nc.psum_tensor(f"ps_s{j}", [128, TCS], F32)) for j in range(2)]
            ps_qL = [E2(nc.psum_tensor(f"ps_q{j}", [128, TCS], F32)) for j in range(2)]
            ps_kv = [E2(nc.psum_tensor(f"ps_kv{i}", [128, TCS], F32)) for i in range(2)]

            for tc in range(NTCS):
                t0 = tc * TCS
                par = tc % 2
                xn1b = xn1bL[par]
                mtile, rstdt, vart, msqt = mtileL[par], rstdtL[par], vartL[par], msqtL[par]
                ps_s, ps_q = ps_sL[par], ps_qL[par]
                for c in range(NCH):
                    if tc == 0:
                        p.op("sync", (lambda c: lambda e: e.dma_start(out=xt[c][:, 1:1 + TCS], in_=xT[c * 128:(c + 1) * 128, 0:TCS]))(c),
                             writes=[xt[c]], dma=True, dkey=xt[c])
                        p.op("sync", (lambda c: lambda e: e.dma_start(out=xt[c][:, 0:1], in_=xT[c * 128:(c + 1) * 128, 0:1], allow_slow_non_contiguous=True))(c),
                             writes=[xt[c]], dma=True, dkey=xt[c])
                    else:
                        p.op("sync", (lambda c, t0: lambda e: e.dma_start(out=xt[c][:], in_=xT[c * 128:(c + 1) * 128, t0 - 1:t0 + TCS]))(c, t0),
                             writes=[xt[c]], dma=True, dkey=xt[c])
                for c in range(NCH):
                    xb = xb16L[c % 2]
                    p.op("scalar", (lambda c, xb: lambda e: e.activation(xb[:], xt[c][:, 1:1 + TCS], AF.Copy))(c, xb),
                         reads=[xt[c]], writes=[xb])
                    p.op("tensor", (lambda c, xb: lambda e: e.matmul(ps_s[:], ones_b[:], xb[:], start=(c == 0), stop=(c == NCH - 1)))(c, xb),
                         reads=[ones_b, xb], writes=[ps_s])
                for c in range(NCH):
                    xq = xsqL[c % 2]
                    p.op("scalar", (lambda c, xq: lambda e: e.activation(xq[:], xt[c][:, 1:1 + TCS], AF.Square))(c, xq),
                         reads=[xt[c]], writes=[xq])
                    p.op("tensor", (lambda c, xq: lambda e: e.matmul(ps_q[:], ones_b[:], xq[:], start=(c == 0), stop=(c == NCH - 1)))(c, xq),
                         reads=[ones_b, xq], writes=[ps_q])
                p.op("vector", lambda e: e.tensor_scalar(mtile[:], ps_s[:], 1.0 / C, None, ALU.mult), reads=[ps_s], writes=[mtile])
                p.op("scalar", lambda e: e.activation(msqt[:], mtile[:], AF.Square), reads=[mtile], writes=[msqt])
                p.op("vector", lambda e: e.scalar_tensor_tensor(vart[:], ps_q[:], 1.0 / C, msqt[:], ALU.mult, ALU.subtract),
                     reads=[ps_q, msqt], writes=[vart])
                p.op("scalar", lambda e: e.activation(vart[:], vart[:], AF.Ln, bias=epsT[:]), reads=[vart, epsT], writes=[vart])
                p.op("scalar", lambda e: e.activation(rstdt[:], vart[:], AF.Exp, scale=-0.5), reads=[vart], writes=[rstdt])
                for c in range(NCH):
                    dsc = dscL[(tc * NCH + c) % 4]
                    p.op("vector" if c % 2 == 0 else "gpsimd", (lambda c, dsc: lambda e: e.tensor_sub(dsc[:], xt[c][:, 1:1 + TCS], mtile[:]))(c, dsc),
                         reads=[xt[c], mtile], writes=[dsc])
                    p.op("vector", (lambda c, dsc: lambda e: e.scalar_tensor_tensor(dsc[:], dsc[:], vs(vt["vg1"], c), rstdt[:], ALU.mult, ALU.mult))(c, dsc),
                         reads=[dsc, rstdt, vt["vg1"]], writes=[dsc])
                    p.op("vector", (lambda c, dsc: lambda e: e.tensor_scalar(xn1b[c][:], dsc[:], vs(vt["vb1"], c), None, ALU.add))(c, dsc),
                         reads=[dsc, vt["vb1"]], writes=[xn1b[c]])
                m_order = [mm for cc_ in range(8) for mm in (cc_, cc_ + 8)]
                for mi, m in enumerate(m_order):
                    pkv = ps_kv[mi % 2]

                    def mmgrp(e, m=m, pkv=pkv):
                        for c in range(NCH):
                            inst = e.matmul(pkv[:], wtmkv_sb[:, c * 2048 + m * 128:c * 2048 + (m + 1) * 128],
                                            xn1b[c][:], start=(c == 0), stop=(c == NCH - 1))
                        return inst
                    p.op("tensor", mmgrp, reads=[wtmkv_sb] + [xn1b[c] for c in range(NCH)], writes=[pkv])
                    isk = m < 8
                    cc = m % 8
                    mixv = vt["vtmk"] if isk else vt["vtmv"]
                    omv = om["vtmk"] if isk else om["vtmv"]
                    kvt = kvevL[mi % 4]
                    xpp = xppL[mi % 4]
                    dstbuf = kbuf if isk else vbuf
                    p.op("vector" if mi % 2 == 0 else "gpsimd", (lambda cc, omv, xpp: lambda e: e.tensor_scalar(xpp[:], xt[cc][:, 0:TCS], vs(omv, cc), None, ALU.mult))(cc, omv, xpp),
                         reads=[xt[cc], omv], writes=[xpp])
                    p.op("vector", (lambda pkv, mixv, cc, kvt, xpp: lambda e: e.scalar_tensor_tensor(kvt[:], pkv[:], vs(mixv, cc), xpp[:], ALU.mult, ALU.add))(pkv, mixv, cc, kvt, xpp),
                         reads=[pkv, mixv, xpp], writes=[kvt])
                    p.op("gpsimd", (lambda dstbuf, cc, t0, kvt: lambda e: e.dma_start(out=dstbuf[cc * 128:(cc + 1) * 128, t0:t0 + TCS], in_=kvt[:]))(dstbuf, cc, t0, kvt),
                         reads=[kvt], dma=True, dkey=kvt, dram_out=("kv", 0 if isk else 1, tc))

            wbc = E2(nc.sbuf_tensor("wbc", [128, NCH * LW], F32))
            for c in range(NCH):
                p.op("scalar", (lambda c: lambda e: e.memzero(wbc[:, c * LW:(c + 1) * LW]))(c), writes=[wbc])
            for c in range(NCH):
                p.op("vector", (lambda c: lambda e: e.tensor_scalar(wbc[:, c * LW:(c + 1) * LW], wbc[:, c * LW:(c + 1) * LW], vs(vt["vw"], c), None, ALU.add))(c),
                     reads=[vt["vw"]], writes=[wbc])

            def mkset(j):
                return {nm: E2(nc.sbuf_tensor(f"wkv{nm}{j}", [128, 1 + LW], F32)) for nm in ("mT", "SgH", "Sg", "aaT", "sT")} |                        {nm: E2(nc.sbuf_tensor(f"wkv{nm}{j}", [128, LW], F32)) for nm in ("KT", "VT", "w1", "w2", "w3", "w4", "w5", "w6", "yt")}
            sets = [mkset(0), mkset(1)]

            unit = 0
            for wt in range(NLW):
                t0 = wt * LW
                for c in range(NCH):
                    S = sets[unit % 2]
                    unit += 1
                    mT, SgH, Sg, aaT, sT = S["mT"], S["SgH"], S["Sg"], S["aaT"], S["sT"]
                    KT, VT = S["KT"], S["VT"]
                    w1, w2, w3, w4, w5, w6, yt = S["w1"], S["w2"], S["w3"], S["w4"], S["w5"], S["w6"], S["yt"]
                    car = carryL[c]
                    wv_c = vs(vt["vw"], c)
                    wu_c = vs(vt["vu"], c)
                    wbc_c = wbc[:, c * LW:(c + 1) * LW]
                    p.op("sync", (lambda c, t0, KT: lambda e: e.dma_start(out=KT[:], in_=kbuf[c * 128:(c + 1) * 128, t0:t0 + LW]))(c, t0, KT),
                         writes=[KT], dma=True, dkey=KT, dram_in=[("kv", 0, 2 * wt), ("kv", 0, 2 * wt + 1)] if LW == 1024 else [("kv", 0, wt)])
                    p.op("sync", (lambda c, t0, VT: lambda e: e.dma_start(out=VT[:], in_=vbuf[c * 128:(c + 1) * 128, t0:t0 + LW]))(c, t0, VT),
                         writes=[VT], dma=True, dkey=VT, dram_in=[("kv", 1, 2 * wt), ("kv", 1, 2 * wt + 1)] if LW == 1024 else [("kv", 1, wt)])
                    p.op("scalar", (lambda car, mT: lambda e: e.copy(mT[:, 0:1], car[:, 0:1]))(car, mT), reads=[car], writes=[mT])
                    p.op("vector", (lambda wbc_c, car, mT, KT: lambda e: e.tensor_tensor_scan(mT[:, 1:1 + LW], wbc_c, KT[:], car[:, 0:1], ALU.add, ALU.max))(wbc_c, car, mT, KT),
                         reads=[wbc, KT, car], writes=[mT])
                    p.op("vector", (lambda wv_c, mT, w1: lambda e: e.scalar_tensor_tensor(w1[:], mT[:, 0:LW], wv_c, mT[:, 1:1 + LW], ALU.add, ALU.subtract))(wv_c, mT, w1),
                         reads=[mT, vt["vw"]], writes=[w1])
                    p.op("scalar", (lambda w1: lambda e: e.activation(w1[:], w1[:], AF.Exp))(w1), reads=[w1], writes=[w1])
                    p.op("gpsimd", (lambda KT, mT, w2: lambda e: e.tensor_sub(w2[:], KT[:], mT[:, 1:1 + LW]))(KT, mT, w2), reads=[KT, mT], writes=[w2])
                    p.op("scalar", (lambda w2: lambda e: e.activation(w2[:], w2[:], AF.Exp))(w2), reads=[w2], writes=[w2])
                    p.op("scalar", (lambda car, SgH: lambda e: e.copy(SgH[:, 0:1], car[:, 1:2]))(car, SgH), reads=[car], writes=[SgH])
                    p.op("vector", (lambda car, SgH, w1, w2: lambda e: e.tensor_tensor_scan(SgH[:, 1:1 + LW], w1[:], w2[:], car[:, 1:2], ALU.mult, ALU.add))(car, SgH, w1, w2),
                         reads=[w1, w2, car], writes=[SgH])
                    p.op("gpsimd", (lambda w3, w1, SgH: lambda e: e.tensor_mul(w3[:], w1[:], SgH[:, 0:LW]))(w3, w1, SgH), reads=[w1, SgH], writes=[w3])
                    p.op("vector", (lambda w3, w2: lambda e: e.tensor_max(w3[:], w3[:], w2[:]))(w3, w2), reads=[w3, w2], writes=[w3])
                    p.op("vector", (lambda w3, w2: lambda e: e.scalar_tensor_tensor(w3[:], w3[:], WKV_EPS, w2[:], ALU.mult, ALU.add))(w3, w2),
                         reads=[w3, w2], writes=[w3])
                    p.op("scalar", (lambda car, Sg: lambda e: e.copy(Sg[:, 0:1], car[:, 2:3]))(car, Sg), reads=[car], writes=[Sg])
                    p.op("vector", (lambda car, Sg, w1, w3: lambda e: e.tensor_tensor_scan(Sg[:, 1:1 + LW], w1[:], w3[:], car[:, 2:3], ALU.mult, ALU.add))(car, Sg, w1, w3),
                         reads=[w1, w3, car], writes=[Sg])
                    p.op("scalar", (lambda Sg, sT: lambda e: e.activation(sT[:], Sg[:], AF.Ln))(Sg, sT), reads=[Sg], writes=[sT])
                    p.op("gpsimd", (lambda sT, mT: lambda e: e.tensor_add(sT[:], mT[:], sT[:]))(sT, mT), reads=[mT, sT], writes=[sT])
                    p.op("vector", (lambda wv_c, sT, KT, w4: lambda e: e.scalar_tensor_tensor(w4[:], sT[:, 0:LW], wv_c, KT[:], ALU.add, ALU.subtract))(wv_c, sT, KT, w4),
                         reads=[sT, vt["vw"], KT], writes=[w4])
                    p.op("scalar", (lambda w5, w4: lambda e: e.activation(w5[:], w4[:], AF.Relu, scale=-1.0))(w5, w4), reads=[w4], writes=[w5])
                    p.op("scalar", (lambda w5: lambda e: e.activation(w5[:], w5[:], AF.Exp, scale=-1.0))(w5), reads=[w5], writes=[w5])
                    p.op("scalar", (lambda w4: lambda e: e.activation(w4[:], w4[:], AF.Relu))(w4), reads=[w4], writes=[w4])
                    p.op("scalar", (lambda w4: lambda e: e.activation(w4[:], w4[:], AF.Exp, scale=-1.0))(w4), reads=[w4], writes=[w4])
                    p.op("vector", (lambda w4, VT: lambda e: e.tensor_mul(w4[:], w4[:], VT[:]))(w4, VT), reads=[w4, VT], writes=[w4])
                    p.op("scalar", (lambda car, aaT: lambda e: e.copy(aaT[:, 0:1], car[:, 3:4]))(car, aaT), reads=[car], writes=[aaT])
                    p.op("vector", (lambda car, aaT, w5, w4: lambda e: e.tensor_tensor_scan(aaT[:, 1:1 + LW], w5[:], w4[:], car[:, 3:4], ALU.mult, ALU.add))(car, aaT, w5, w4),
                         reads=[w5, w4, car], writes=[aaT])
                    p.op("scalar", (lambda car, mT: lambda e: e.copy(car[:, 0:1], mT[:, LW:1 + LW]))(car, mT), reads=[mT], writes=[car])
                    p.op("scalar", (lambda car, SgH: lambda e: e.copy(car[:, 1:2], SgH[:, LW:1 + LW]))(car, SgH), reads=[SgH], writes=[car])
                    p.op("scalar", (lambda car, Sg: lambda e: e.copy(car[:, 2:3], Sg[:, LW:1 + LW]))(car, Sg), reads=[Sg], writes=[car])
                    p.op("scalar", (lambda car, aaT: lambda e: e.copy(car[:, 3:4], aaT[:, LW:1 + LW]))(car, aaT), reads=[aaT], writes=[car])
                    p.op("vector", (lambda wu_c, KT, sT, w6: lambda e: e.scalar_tensor_tensor(w6[:], KT[:], wu_c, sT[:, 0:LW], ALU.add, ALU.subtract))(wu_c, KT, sT, w6),
                         reads=[KT, vt["vu"], sT], writes=[w6])
                    p.op("scalar", (lambda w1, w6: lambda e: e.activation(w1[:], w6[:], AF.Relu, scale=-1.0))(w1, w6), reads=[w6], writes=[w1])
                    p.op("scalar", (lambda w1: lambda e: e.activation(w1[:], w1[:], AF.Exp, scale=-1.0))(w1), reads=[w1], writes=[w1])
                    p.op("scalar", (lambda w6: lambda e: e.activation(w6[:], w6[:], AF.Relu))(w6), reads=[w6], writes=[w6])
                    p.op("scalar", (lambda w6: lambda e: e.activation(w6[:], w6[:], AF.Exp, scale=-1.0))(w6), reads=[w6], writes=[w6])
                    p.op("vector", (lambda w2, w6, aaT: lambda e: e.tensor_mul(w2[:], w6[:], aaT[:, 0:LW]))(w2, w6, aaT), reads=[w6, aaT], writes=[w2])
                    p.op("vector", (lambda w3, w1, VT: lambda e: e.tensor_mul(w3[:], w1[:], VT[:]))(w3, w1, VT), reads=[w1, VT], writes=[w3])
                    p.op("vector", (lambda w2, w3: lambda e: e.tensor_add(w2[:], w2[:], w3[:]))(w2, w3), reads=[w2, w3], writes=[w2])
                    p.op("vector", (lambda w3, w6, w1: lambda e: e.scalar_tensor_tensor(w3[:], w6[:], WKV_EPS, w1[:], ALU.add, ALU.add))(w3, w6, w1),
                         reads=[w6, w1], writes=[w3])
                    p.op("scalar", (lambda w3: lambda e: e.activation(w3[:], w3[:], AF.Ln))(w3), reads=[w3], writes=[w3])
                    p.op("scalar", (lambda w3: lambda e: e.activation(w3[:], w3[:], AF.Exp, scale=-1.0))(w3), reads=[w3], writes=[w3])
                    p.op("vector", (lambda yt, w2, w3: lambda e: e.tensor_mul(yt[:], w2[:], w3[:]))(yt, w2, w3), reads=[w2, w3], writes=[yt])
                    p.op("gpsimd", (lambda c, t0, yt: lambda e: e.dma_start(out=ybuf[c * 128:(c + 1) * 128, t0:t0 + LW], in_=yt[:]))(c, t0, yt),
                         reads=[yt], dma=True, dkey=yt, dram_out=("y", wt))
        p.barrier()

        with contextlib.ExitStack() as es3:
            E3 = es3.enter_context
            bufA = [E3(nc.sbuf_tensor(f"bufA{c}", [128, SW], F32)) for c in range(NCH)]
            bufB = [E3(nc.sbuf_tensor(f"bufB{c}", [128, SW], F32)) for c in range(NCH)]
            bufC = [E3(nc.sbuf_tensor(f"bufC{c}", [128, SW], F32)) for c in range(NCH)]
            bufD = [E3(nc.sbuf_tensor(f"bufD{c}", [128, SW], F32)) for c in range(NCH)]
            bufE = [E3(nc.sbuf_tensor(f"bufE{c}", [128, SW], F32)) for c in range(NCH)]
            x2d = [E3(nc.sbuf_tensor(f"x2d{c}", [128, SW], F32)) for c in range(NCH)]
            pool = [E3(nc.sbuf_tensor(f"pool{i}", [128, SW], BF16)) for i in range(32)]
            nb16 = [E3(nc.sbuf_tensor(f"nb16_{c}", [128, SLOT], BF16)) for c in range(NCH)]
            ycn = [E3(nc.sbuf_tensor(f"ycn{j}", [128, 544], F32)) for j in range(4)]
            dscA = [E3(nc.sbuf_tensor(f"dscA{j}", [128, SW], F32)) for j in range(2)]
            cbt2 = E3(nc.sbuf_tensor("cbt2", [128, 4096], BF16))
            mtile2 = E3(nc.sbuf_tensor("mtile2", [128, SW], F32))
            rstdt2 = E3(nc.sbuf_tensor("rstdt2", [128, SW], F32))
            vart2 = E3(nc.sbuf_tensor("vart2", [128, SW], F32))
            msqt2 = E3(nc.sbuf_tensor("msqt2", [128, SW], F32))
            dsc2 = E3(nc.sbuf_tensor("dsc2", [128, SW], F32))
            dsc3 = E3(nc.sbuf_tensor("dsc3", [128, SW], F32))
            xn2f = E3(nc.sbuf_tensor("xn2f", [128, SLOT], F32))
            xb16pL = [E3(nc.sbuf_tensor(f"xb16p{j}", [128, SW], BF16)) for j in range(2)]
            xsqpL = [E3(nc.sbuf_tensor(f"xsqp{j}", [128, SW], BF16)) for j in range(2)]
            wsmL = [E3(nc.sbuf_tensor(f"wsm{j}", [128, 1024], BF16)) for j in range(2)]
            wbigL = [E3(nc.sbuf_tensor(f"wbig{j}", [128, 4096], BF16)) for j in range(2)]
            outt = E3(nc.sbuf_tensor("outt", [128, SLOT], F32))
            ps_a = E3(nc.psum_tensor("ps_a", [128, SW], F32))
            ps_b = E3(nc.psum_tensor("ps_b", [128, SW], F32))
            ps_c = E3(nc.psum_tensor("ps_c", [128, SLOT], F32))
            ps_d = E3(nc.psum_tensor("ps_d", [128, SLOT], F32))
            ps_n1 = E3(nc.psum_tensor("ps_n1", [128, 1], F32))

            wcm_r = wcm_b.rearrange("(m p) f -> m p f", p=128)
            wcp_r = wcp_b.rearrange("(o p) f -> o p f", p=128)
            wpj_r = wpj_b.rearrange("(g p) f -> g p f", p=128)

            def ln_stats(src_tiles, width, off):
                pieces = [(0, min(width, SLOT))]
                if width > SLOT:
                    pieces.append((SLOT, width))
                for (ps, actf, scrL) in ((ps_a, AF.Copy, xb16pL), (ps_b, AF.Square, xsqpL)):
                    for lo, hi in pieces:
                        for c in range(NCH):
                            scratch = scrL[c % 2]
                            p.op("scalar", (lambda c, lo, hi, actf, scratch: lambda e: e.activation(
                                scratch[:, lo:hi], src_tiles[c][:, off + lo:off + hi], actf))(c, lo, hi, actf, scratch),
                                 reads=[src_tiles[c]], writes=[scratch])
                            p.op("tensor", (lambda c, lo, hi, ps, scratch: lambda e: e.matmul(
                                ps[:, lo:hi], ones_b[:], scratch[:, lo:hi],
                                start=(c == 0), stop=(c == NCH - 1)))(c, lo, hi, ps, scratch),
                                 reads=[ones_b, scratch], writes=[ps])
                p.op("vector", lambda e: e.tensor_scalar(mtile2[:, 0:width], ps_a[:, 0:width], 1.0 / C, None, ALU.mult),
                     reads=[ps_a], writes=[mtile2])
                p.op("scalar", lambda e: e.activation(msqt2[:, 0:width], mtile2[:, 0:width], AF.Square),
                     reads=[mtile2], writes=[msqt2])
                p.op("vector", lambda e: e.scalar_tensor_tensor(vart2[:, 0:width], ps_b[:, 0:width], 1.0 / C, msqt2[:, 0:width], ALU.mult, ALU.subtract),
                     reads=[ps_b, msqt2], writes=[vart2])
                p.op("scalar", lambda e: e.activation(vart2[:, 0:width], vart2[:, 0:width], AF.Ln, bias=epsT[:]),
                     reads=[vart2, epsT], writes=[vart2])
                p.op("scalar", lambda e: e.activation(rstdt2[:, 0:width], vart2[:, 0:width], AF.Exp, scale=-0.5),
                     reads=[vart2], writes=[rstdt2])

            import os as _os
            if _os.environ.get("KDBG2"):
                dbg_early = dram("dbg_early", [C, T], kind="ExternalOutput")
                dearly = E3(nc.sbuf_tensor("dearly", [128, 1024], F32))
                for c in range(NCH):
                    for tt_ in range(4):
                        p.op("sync", (lambda c, tt_: lambda e: e.dma_start(out=dearly[:], in_=ybuf[c * 128:(c + 1) * 128, tt_ * 1024:(tt_ + 1) * 1024]))(c, tt_),
                             writes=[dearly], dma=True, dkey=dearly, dram_in=[("y", w_) for w_ in range(NLW)])
                        p.op("gpsimd", (lambda c, tt_: lambda e: e.dma_start(out=dbg_early[c * 128:(c + 1) * 128, tt_ * 1024:(tt_ + 1) * 1024], in_=dearly[:]))(c, tt_),
                             reads=[dearly], dma=True, dkey=dearly)
                dbg_x2 = dram("dbg_x2", [C, NSLOT * SW], kind="ExternalOutput")
                dbg_yb = dram("dbg_yb", [C, NSLOT * SW], kind="ExternalOutput")
                dbg_rt = dram("dbg_rt", [C, NSLOT * SW], kind="ExternalOutput")
                dbg_x1 = dram("dbg_x1", [C, NSLOT * SLOT], kind="ExternalOutput")
                dbg_fb = dram("dbg_fb", [C, NSLOT * SLOT], kind="ExternalOutput")
            for i in range(NSLOT):
                xe = bufD
                xpr = bufA
                yb = bufB
                rt = bufC
                x2 = bufE
                for c in range(NCH):
                    p.op("sync", (lambda c, i: lambda e: e.dma_start(out=bufD[c][:], in_=xe_in[c * 128:(c + 1) * 128, i * SW:(i + 1) * SW]))(c, i),
                         writes=[bufD[c]], dma=True, dkey=bufD[c])
                    p.op("sync", (lambda c, i: lambda e: e.dma_start(out=bufA[c][:], in_=xpr_in[c * 128:(c + 1) * 128, i * SW:(i + 1) * SW]))(c, i),
                         writes=[bufA[c]], dma=True, dkey=bufA[c])
                s0_ = (2 * i) * 512
                s1_ = (2 * i + 1) * 512
                yreg0 = sorted({("y", (s0_ // 512) % NLW), ("y", ((s0_ - 1) // 512) % NLW)})
                yreg1 = sorted({("y", (s1_ // 512) % NLW), ("y", ((s1_ - 1) // 512) % NLW)})
                for c in range(NCH):
                    s0 = (2 * i) * 512
                    s1 = (2 * i + 1) * 512
                    y0t = ycn[(c % 2) * 2]
                    y1t = ycn[(c % 2) * 2 + 1]
                    y0 = y0t[:, 0:SW]
                    y1 = y1t[:, 0:SW]
                    if i == 0:
                        p.op("sync", (lambda c, y0: lambda e: e.dma_start(out=y0[:, 0:1], in_=ybuf[c * 128:(c + 1) * 128, T - 1:T], allow_slow_non_contiguous=True))(c, y0),
                             writes=[y0], dma=True, dkey=y0t, dram_in=yreg0)
                        p.op("sync", (lambda c, y0: lambda e: e.dma_start(out=y0[:, 1:SW], in_=ybuf[c * 128:(c + 1) * 128, 0:512]))(c, y0),
                             writes=[y0], dma=True, dkey=y0t, dram_in=yreg0)
                    else:
                        p.op("sync", (lambda c, s0, y0: lambda e: e.dma_start(out=y0, in_=ybuf[c * 128:(c + 1) * 128, s0 - 1:s0 + 512]))(c, s0, y0),
                             writes=[y0], dma=True, dkey=y0t, dram_in=yreg0)
                    p.op("sync", (lambda c, s1, y1: lambda e: e.dma_start(out=y1, in_=ybuf[c * 128:(c + 1) * 128, s1 - 1:s1 + 512]))(c, s1, y1),
                         writes=[y1], dma=True, dkey=y1t, dram_in=yreg1)
                    p.op("vector", (lambda c, y1: lambda e: e.tensor_scalar(bufB[c][:], y1, maskt[:], None, ALU.mult))(c, y1),
                         reads=[y1, maskt], writes=[bufB[c]])
                    p.op("vector", (lambda c, y0: lambda e: e.scalar_tensor_tensor(bufB[c][:], y0, maskom[:], bufB[c][:], ALU.mult, ALU.add))(c, y0),
                         reads=[bufB[c], maskom, y0], writes=[bufB[c]])
                ln_stats(xe, SW, 0)
                for c in range(NCH):
                    ds = dscA[c % 2]
                    p.op("vector" if c % 2 == 0 else "gpsimd", (lambda c, ds: lambda e: e.tensor_sub(ds[:], xe[c][:], mtile2[:]))(c, ds),
                         reads=[xe[c], mtile2], writes=[ds])
                    p.op("vector", (lambda c, ds: lambda e: e.scalar_tensor_tensor(ds[:], ds[:], vs(vt["vg1"], c), rstdt2[:], ALU.mult, ALU.mult))(c, ds),
                         reads=[ds, rstdt2, vt["vg1"]], writes=[ds])
                    p.op("vector", (lambda c, ds: lambda e: e.tensor_scalar(pool[c][:], ds[:], vs(vt["vb1"], c), None, ALU.add))(c, ds),
                         reads=[ds, vt["vb1"]], writes=[pool[c]])
                psr = [ps_c, ps_d]
                for m in range(NCH):
                    wbt = wsmL[m % 2]
                    wb = wbt[:]
                    pr = psr[m % 2]
                    p.op("gpsimd", (lambda m, wb: lambda e: e.dma_start(out=wb, in_=wtmr_in[m * 128:(m + 1) * 128, :]))(m, wb),
                         writes=[wbt], dma=True, dkey=wbt)

                    def rgrp(e, wb=wb, pr=pr):
                        for c in range(NCH):
                            inst = e.matmul(pr[:], wb[:, c * 128:(c + 1) * 128],
                                            pool[c][:, 0:SLOT], start=(c == 0), stop=(c == NCH - 1))
                        return inst
                    p.op("tensor", rgrp, reads=[wbt] + [pool[c] for c in range(NCH)], writes=[pr])

                    def rgrp1(e, wb=wb):
                        for c in range(NCH):
                            inst = e.matmul(ps_n1[:], wb[:, c * 128:(c + 1) * 128],
                                            pool[c][:, SLOT:SW], start=(c == 0), stop=(c == NCH - 1))
                        return inst
                    p.op("tensor", rgrp1, reads=[wbt] + [pool[c] for c in range(NCH)], writes=[ps_n1])
                    ds = dscA[m % 2]
                    p.op("gpsimd", (lambda m, ds: lambda e: e.tensor_scalar(ds[:], xpr[m][:], vs(om["vtmr"], m), None, ALU.mult))(m, ds),
                         reads=[xpr[m], om["vtmr"]], writes=[ds])
                    p.op("vector", (lambda m, ds, pr: lambda e: e.scalar_tensor_tensor(rt[m][:, 0:SLOT], pr[:], vs(vt["vtmr"], m), ds[:, 0:SLOT], ALU.mult, ALU.add))(m, ds, pr),
                         reads=[pr, vt["vtmr"], ds], writes=[rt[m]])
                    p.op("vector", (lambda m, ds: lambda e: e.scalar_tensor_tensor(rt[m][:, SLOT:SW], ps_n1[:], vs(vt["vtmr"], m), ds[:, SLOT:SW], ALU.mult, ALU.add))(m, ds),
                         reads=[ps_n1, vt["vtmr"], ds], writes=[rt[m]])
                    p.op("scalar", (lambda m: lambda e: e.activation(rt[m][:], rt[m][:], AF.Sigmoid))(m),
                         reads=[rt[m]], writes=[rt[m]])
                for c in range(NCH):
                    e1_ = "gpsimd" if c % 2 == 0 else "vector"
                    e2_ = "vector" if c % 2 == 0 else "gpsimd"
                    p.op(e1_, (lambda c: lambda e: e.tensor_mul(x2[c][:], rt[c][:], yb[c][:]))(c),
                         reads=[rt[c], yb[c]], writes=[x2[c]])
                    p.op(e2_, (lambda c: lambda e: e.tensor_add(x2[c][:], x2[c][:], xe[c][:]))(c),
                         reads=[x2[c], xe[c]], writes=[x2[c]])
                    p.op(e1_, (lambda c: lambda e: e.tensor_add(x2d[c][:], x2[c][:], xe[c][:]))(c),
                         reads=[x2[c], xe[c]], writes=[x2d[c]])
                import os as _os
                if _os.environ.get("KDBG2"):
                    for c in range(NCH):
                        p.op("gpsimd", (lambda c, i: lambda e: e.dma_start(out=dbg_x2[c * 128:(c + 1) * 128, i * SW:(i + 1) * SW], in_=x2[c][:]))(c, i),
                             reads=[x2[c]], dma=True, dkey=x2[c])
                        p.op("gpsimd", (lambda c, i: lambda e: e.dma_start(out=dbg_yb[c * 128:(c + 1) * 128, i * SW:(i + 1) * SW], in_=bufB[c][:]))(c, i),
                             reads=[bufB[c]], dma=True, dkey=bufB[c])
                        p.op("gpsimd", (lambda c, i: lambda e: e.dma_start(out=dbg_rt[c * 128:(c + 1) * 128, i * SW:(i + 1) * SW], in_=rt[c][:]))(c, i),
                             reads=[rt[c]], dma=True, dkey=rt[c])
                ln_stats(x2, SLOT, 1)
                xppre = bufA
                rr = bufB
                xn2fL = [xn2f, dsc2]
                for c in range(NCH):
                    ds = dscA[c % 2]
                    xf = xn2fL[c % 2]
                    p.op("vector" if c % 2 == 0 else "gpsimd", (lambda c, ds: lambda e: e.tensor_sub(ds[:, 0:SLOT], x2[c][:, 1:SW], mtile2[:, 0:SLOT]))(c, ds),
                         reads=[x2[c], mtile2], writes=[ds])
                    p.op("vector", (lambda c, ds, xf: lambda e: e.scalar_tensor_tensor(xf[:, 0:SLOT], ds[:, 0:SLOT], vs(vt["vg2"], c), rstdt2[:, 0:SLOT], ALU.mult, ALU.mult))(c, ds, xf),
                         reads=[ds, rstdt2, vt["vg2"]], writes=[xf])
                    p.op("vector", (lambda c, xf: lambda e: e.tensor_scalar(xf[:, 0:SLOT], xf[:, 0:SLOT], vs(vt["vb2"], c), None, ALU.add))(c, xf),
                         reads=[xf, vt["vb2"]], writes=[xf])
                    p.op("scalar", (lambda c, xf: lambda e: e.activation(nb16[c][:], xf[:, 0:SLOT], AF.Copy))(c, xf),
                         reads=[xf], writes=[nb16[c]])
                    p.op("gpsimd", (lambda c, ds: lambda e: e.tensor_scalar(ds[:, 0:SLOT], x2[c][:, 0:SLOT], vs(om["vcmr"], c), None, ALU.mult))(c, ds),
                         reads=[x2[c], om["vcmr"]], writes=[ds])
                    p.op("vector", (lambda c, ds, xf: lambda e: e.scalar_tensor_tensor(rr[c][:, 0:SLOT], xf[:, 0:SLOT], vs(vt["vcmr"], c), ds[:, 0:SLOT], ALU.mult, ALU.add))(c, ds, xf),
                         reads=[xf, vt["vcmr"], ds], writes=[rr[c]])
                    p.op("scalar", (lambda c: lambda e: e.activation(rr[c][:, 0:SLOT], rr[c][:, 0:SLOT], AF.Sigmoid))(c),
                         reads=[rr[c]], writes=[rr[c]])
                    p.op("gpsimd", (lambda c: lambda e: e.tensor_scalar(xppre[c][:, 0:SLOT], x2[c][:, 0:SLOT], vs(om["vcmk"], c), None, ALU.mult))(c),
                         reads=[x2[c], om["vcmk"]], writes=[xppre[c]])
                for m in range(32):
                    wbt = wsmL[m % 2]
                    wb = wbt[:]
                    p.op("sync", (lambda m, wb: lambda e: e.dma_start(out=wb, in_=wcm_r[m]))(m, wb),
                         writes=[wbt], dma=True, dkey=wbt, dram_in="wcm")
                    pk = ps_c if (m % 2 == 0) else ps_d

                    def cmgrp(e, wb=wb, pk=pk):
                        for c in range(NCH):
                            inst = e.matmul(pk[:], wb[:, c * 128:(c + 1) * 128],
                                            nb16[c][:], start=(c == 0), stop=(c == NCH - 1))
                        return inst
                    p.op("tensor", cmgrp, reads=[wbt] + [nb16[c] for c in range(NCH)], writes=[pk])
                    cc = m % 8
                    ds = dscA[m % 2]
                    p.op("vector", (lambda cc, pk, ds: lambda e: e.scalar_tensor_tensor(ds[:, 0:SLOT], pk[:], vs(vt["vcmk"], cc), xppre[cc][:, 0:SLOT], ALU.mult, ALU.add))(cc, pk, ds),
                         reads=[pk, vt["vcmk"], xppre[cc]], writes=[ds])
                    p.op("vector", (lambda m, ds: lambda e: e.scalar_tensor_tensor(pool[m][:, 0:SLOT], ds[:, 0:SLOT], 0.0, ds[:, 0:SLOT], ALU.max, ALU.mult))(m, ds),
                         reads=[ds], writes=[pool[m]])
                for o in range(NCH):
                    wbt = wbigL[o % 2]
                    wb = wbt[:]
                    p.op("sync", (lambda o, wb: lambda e: e.dma_start(out=wb, in_=wcp_r[o]))(o, wb),
                         writes=[wbt], dma=True, dkey=wbt, dram_in="wcp")
                    pk = ps_c if (o % 2 == 0) else ps_d

                    def cpgrp(e, wb=wb, pk=pk):
                        for k in range(32):
                            inst = e.matmul(pk[:], wb[:, k * 128:(k + 1) * 128],
                                            pool[k][:, 0:SLOT], start=(k == 0), stop=(k == 31))
                        return inst
                    p.op("tensor", cpgrp, reads=[wbt] + [pool[k] for k in range(32)], writes=[pk])
                    ds = dscA[o % 2]
                    p.op("vector", (lambda o, pk, ds: lambda e: e.tensor_mul(ds[:, 0:SLOT], pk[:], rr[o][:, 0:SLOT]))(o, pk, ds),
                         reads=[pk, rr[o]], writes=[ds])
                    p.op("vector", (lambda o, ds: lambda e: e.tensor_add(x2d[o][:, 1:SW], x2d[o][:, 1:SW], ds[:, 0:SLOT]))(o, ds),
                         reads=[x2d[o], ds], writes=[x2d[o]])
                ln_stats(x2d, SLOT, 1)
                x1f = bufC
                x1b = nb16
                for c in range(NCH):
                    p.op("vector" if c % 2 == 0 else "gpsimd", (lambda c: lambda e: e.tensor_sub(x1f[c][:, 0:SLOT], x2d[c][:, 1:SW], mtile2[:, 0:SLOT]))(c),
                         reads=[x2d[c], mtile2], writes=[x1f[c]])
                    p.op("vector", (lambda c: lambda e: e.tensor_mul(x1f[c][:, 0:SLOT], x1f[c][:, 0:SLOT], rstdt2[:, 0:SLOT]))(c),
                         reads=[x1f[c], rstdt2], writes=[x1f[c]])
                    p.op("scalar", (lambda c: lambda e: e.activation(x1b[c][:], x1f[c][:, 0:SLOT], AF.Copy))(c),
                         reads=[x1f[c]], writes=[x1b[c]])
                faccA = bufD
                faccB = bufE
                for kg in range(4):
                    cbt_ = cbt2
                    cb = cbt_[:]
                    p.op("sync", (lambda kg, cb: lambda e: e.dma_start(out=cb, in_=cbd_b[kg * 128:(kg + 1) * 128, :]))(kg, cb),
                         writes=[cbt_], dma=True, dkey=cbt_, dram_in="cbd")
                    for kl in range(32):
                        k = kg * 32 + kl
                        c = (k // 4) % 8
                        pz = ps_c if (kl % 2 == 0) else ps_d
                        p.op("tensor", (lambda kl, c, pz, cb: lambda e: e.matmul(pz[:], cb[:, kl * 128:(kl + 1) * 128],
                                                                                 x1b[c][:], start=True, stop=True))(kl, c, pz, cb),
                             reads=[cbt_, x1b[c]], writes=[pz])
                        p.op("scalar", (lambda kl, pz: lambda e: e.activation(pool[kl][:, 0:SLOT], pz[:], AF.Gelu))(kl, pz),
                             reads=[pz], writes=[pool[kl]])
                    for o in range(NCH):
                        g = kg * 8 + o
                        wbt = wbigL[o % 2]
                        wb = wbt[:]
                        p.op("sync", (lambda g, wb: lambda e: e.dma_start(out=wb, in_=wpj_r[g]))(g, wb),
                             writes=[wbt], dma=True, dkey=wbt, dram_in="wpj")
                        pf = ps_a if (o % 2 == 0) else ps_b

                        def pjgrp(e, wb=wb, pf=pf):
                            for kl in range(32):
                                inst = e.matmul(pf[:, 0:SLOT], wb[:, kl * 128:(kl + 1) * 128],
                                                pool[kl][:, 0:SLOT], start=(kl == 0), stop=(kl == 31))
                            return inst
                        p.op("tensor", pjgrp, reads=[wbt] + [pool[kl] for kl in range(32)], writes=[pf])
                        if kg == 0:
                            p.op("scalar", (lambda o, pf: lambda e: e.activation(faccA[o][:, 0:SLOT], pf[:, 0:SLOT], AF.Copy))(o, pf),
                                 reads=[pf], writes=[faccA[o]])
                        elif kg % 2 == 1:
                            p.op("vector", (lambda o, pf: lambda e: e.tensor_add(faccB[o][:, 0:SLOT], faccA[o][:, 0:SLOT], pf[:, 0:SLOT]))(o, pf),
                                 reads=[faccA[o], pf], writes=[faccB[o]])
                        else:
                            p.op("vector", (lambda o, pf: lambda e: e.tensor_add(faccA[o][:, 0:SLOT], faccB[o][:, 0:SLOT], pf[:, 0:SLOT]))(o, pf),
                                 reads=[faccB[o], pf], writes=[faccA[o]])
                if _os.environ.get("KDBG2"):
                    for c in range(NCH):
                        p.op("gpsimd", (lambda c, i: lambda e: e.dma_start(out=dbg_x1[c * 128:(c + 1) * 128, i * SLOT:(i + 1) * SLOT], in_=x1f[c][:, 0:SLOT]))(c, i),
                             reads=[x1f[c]], dma=True, dkey=x1f[c])
                        p.op("gpsimd", (lambda c, i: lambda e: e.dma_start(out=dbg_fb[c * 128:(c + 1) * 128, i * SLOT:(i + 1) * SLOT], in_=faccB[c][:, 0:SLOT]))(c, i),
                             reads=[faccB[c]], dma=True, dkey=faccB[c])
                for o in range(NCH):
                    p.op("gpsimd", (lambda o: lambda e: e.tensor_add(faccB[o][:, 0:SLOT], faccB[o][:, 0:SLOT], x1f[o][:, 0:SLOT]))(o),
                         reads=[faccB[o], x1f[o]], writes=[faccB[o]])
                ln_stats(faccB, SLOT, 0)
                for c in range(NCH):
                    p.op("vector" if c % 2 == 0 else "gpsimd", (lambda c: lambda e: e.tensor_sub(outt[:], faccB[c][:, 0:SLOT], mtile2[:, 0:SLOT]))(c),
                         reads=[faccB[c], mtile2], writes=[outt])
                    p.op("vector", (lambda c: lambda e: e.tensor_mul(outt[:], outt[:], rstdt2[:, 0:SLOT]))(c),
                         reads=[outt, rstdt2], writes=[outt])
                    p.op("gpsimd", (lambda c, i: lambda e: e.dma_start(out=out[c * 128:(c + 1) * 128, i * SLOT:(i + 1) * SLOT], in_=outt[:]))(c, i),
                         reads=[outt], dma=True, dkey=outt, dram_out="out")

            import os
            if os.environ.get("KDBG"):
                dbg_k = dram("dbg_k", [C, T], kind="ExternalOutput")
                dbg_v = dram("dbg_v", [C, T], kind="ExternalOutput")
                dbg_y = dram("dbg_y", [C, T], kind="ExternalOutput")
                dtile = dearly if os.environ.get("KDBG2") else E3(nc.sbuf_tensor("dtile", [128, 1024], F32))
                for (srcb, dstb) in ((kbuf, dbg_k), (vbuf, dbg_v), (ybuf, dbg_y)):
                    for c in range(NCH):
                        for tt_ in range(4):
                            p.op("sync", (lambda c, tt_, srcb: lambda e: e.dma_start(out=dtile[:], in_=srcb[c * 128:(c + 1) * 128, tt_ * 1024:(tt_ + 1) * 1024]))(c, tt_, srcb),
                                 writes=[dtile], dma=True, dkey=dtile, dram_in=[("kv", 0, t_) for t_ in range(NTCS)] + [("kv", 1, t_) for t_ in range(NTCS)] + [("y", w_) for w_ in range(NLW)])
                            p.op("gpsimd", (lambda c, tt_, dstb: lambda e: e.dma_start(out=dstb[c * 128:(c + 1) * 128, tt_ * 1024:(tt_ + 1) * 1024], in_=dtile[:]))(c, tt_, dstb),
                                 reads=[dtile], dma=True, dkey=dtile)
            p.op("gpsimd", lambda e: e.engine_nop(),
                 extra_waits=[ev for ev in p.all_events() if ev[0] != "gpsimd"])
            p.op("gpsimd", lambda e: e.engine_nop(),
                 extra_waits=[("gpsimd", p.cnt["gpsimd"])])

            block = E(nc.Block())
            p.replay(block)
    return nc


_CACHE = {}


def kernel(L0_attn_x, L0_attn_time_decay, L0_attn_time_first, L0_attn_W_tm,
           L0_attn_g1, L0_attn_b1, L0_attn_tmk, L0_attn_tmv, L0_attn_tmr,
           L0_attn_W_cm, L0_attn_W_cp, L0_attn_g2, L0_attn_b2, L0_attn_cmk,
           L0_attn_cmr, L0_ffn_rail_maker, L0_ffn_rail_up_proj, L0_ffn_w_cproj):
    f32 = lambda a: np.ascontiguousarray(np.asarray(a), dtype=np.float32)
    x = f32(L0_attn_x)
    Wtm = f32(L0_attn_W_tm)
    Wcm = f32(L0_attn_W_cm)
    Wcp = f32(L0_attn_W_cp)
    Wpj = f32(L0_ffn_w_cproj)
    RM = f32(L0_ffn_rail_maker)
    RU = f32(L0_ffn_rail_up_proj)

    wtm_kv = np.ascontiguousarray(Wtm[:, :2048])
    wtm_r = Wtm[:, 2048:]
    wtmr_host = np.ascontiguousarray(wtm_r.reshape(8, 128, 8, 128).transpose(2, 1, 0, 3).reshape(8 * 128, 8 * 128))
    wcm_host = np.ascontiguousarray(Wcm.reshape(8, 128, 32, 128).transpose(2, 1, 0, 3).reshape(32 * 128, 8 * 128))
    wcp_host = np.ascontiguousarray(Wcp.reshape(32, 128, 8, 128).transpose(2, 1, 0, 3).reshape(8 * 128, 32 * 128))
    WpjT = Wpj.T
    hh, cc_, gg, rl, k16 = np.meshgrid(np.arange(4), np.arange(8), np.arange(4), np.arange(8), np.arange(16), indexing="ij")
    morig = (hh * 4096 + (cc_ * 8 + rl) * 64 + gg * 16 + k16).reshape(-1)
    wpjT_perm = WpjT[morig]
    wpj_host = np.ascontiguousarray(
        wpjT_perm.reshape(4, 32, 128, 8, 128).transpose(0, 3, 2, 1, 4).reshape(32 * 128, 32 * 128))
    rm_host = np.zeros((4, 8, 128, 128), np.float32)
    ru_host = np.zeros((4, 8, 4, 128, 128), np.float32)
    for h4 in range(4):
        for c8 in range(8):
            for rl8 in range(8):
                r = c8 * 8 + rl8
                rm_host[h4, c8, rl8 * 16:(rl8 + 1) * 16, rl8 * 16:(rl8 + 1) * 16] = RM[h4, r].T
                for g4 in range(4):
                    ru_host[h4, c8, g4, rl8 * 16:(rl8 + 1) * 16, rl8 * 16:(rl8 + 1) * 16] = RU[h4, r, g4 * 16:(g4 + 1) * 16, :].T
    rm_host = np.ascontiguousarray(rm_host.reshape(32 * 128, 128))
    ru_host = np.ascontiguousarray(ru_host.reshape(128 * 128, 128))

    vecs = {
        "vw": f32(L0_attn_time_decay).reshape(C, 1),
        "vu": f32(L0_attn_time_first).reshape(C, 1),
        "vtmk": f32(L0_attn_tmk).reshape(C, 1),
        "vtmv": f32(L0_attn_tmv).reshape(C, 1),
        "vtmr": f32(L0_attn_tmr).reshape(C, 1),
        "vcmk": f32(L0_attn_cmk).reshape(C, 1),
        "vcmr": f32(L0_attn_cmr).reshape(C, 1),
        "vg1": f32(L0_attn_g1).reshape(C, 1),
        "vb1": f32(L0_attn_b1).reshape(C, 1),
        "vg2": f32(L0_attn_g2).reshape(C, 1),
        "vb2": f32(L0_attn_b2).reshape(C, 1),
    }

    if "nc" not in _CACHE:
        _CACHE["nc"] = build_program()
    nc = _CACHE["nc"]

    in_maps = []
    for core in range(8):
        b, h = core // 2, core % 2
        xb = np.ascontiguousarray(x[b].T)
        xe_cols = []
        xpr_cols = []
        for i in range(NSLOT):
            s = (2 * i + h) * 512
            toks = (np.arange(-1, SLOT) + s) % T
            xe_cols.append(toks)
            xpr_cols.append(np.where(toks == 0, 0, (toks - 1) % T))
        xe_idx = np.concatenate(xe_cols)
        xpr_idx = np.concatenate(xpr_cols)
        m = {
            "xT": xb,
            "xe_in": np.ascontiguousarray(xb[:, xe_idx]),
            "xpr_in": np.ascontiguousarray(xb[:, xpr_idx]),
            "wtmkv_in": wtm_kv,
            "wtmr_in": wtmr_host,
            "wcm_in": wcm_host,
            "wcp_in": wcp_host,
            "wpj_in": wpj_host,
            "rm_in": rm_host,
            "ru_in": ru_host,
            "mask_in": np.full((128, 1), float(h), np.float32),
        }
        m.update(vecs)
        in_maps.append(m)

    res = run_bass_kernel_spmd(nc, in_maps, core_ids=list(range(8)))
    _CACHE["res"] = res
    full = np.empty((B, T, C), np.float32)
    for core in range(8):
        b, h = core // 2, core % 2
        shard = res.results[core]["out"]
        for i in range(NSLOT):
            s = (2 * i + h) * 512
            full[b, s:s + 512, :] = shard[:, i * SLOT:(i + 1) * SLOT].T
    return full


# version 08
# speedup vs baseline: 1.5806x; 1.0476x over previous
import contextlib
import numpy as np
import concourse.bass as bass
import concourse.mybir as mybir
from concourse.bass_utils import run_bass_kernel_spmd

F32 = mybir.dt.float32
BF16 = mybir.dt.bfloat16
AF = mybir.ActivationFunctionType
ALU = mybir.AluOpType

B, T, C = 4, 4096, 1024
NCH = 8
TCS = 512
NTCS = T // TCS
LW = 512
NLW = T // LW
SLOT = 512
NSLOT = 4
SW = SLOT + 1
LN_EPS = 1e-5
WKV_EPS = 1e-8
NEG = -1e38

ENGINES = ("sync", "tensor", "vector", "scalar", "gpsimd")


class Prog:

    def __init__(self, nc, es):
        self.nc = nc
        self.es = es
        self.ops = {e: [] for e in ENGINES}
        self.sems = {}
        self.cnt = {e: 0 for e in ENGINES}
        self.lastw = {}
        self.reads = {}
        self.dramw = {}
        self.dsem = {}
        self.ndsem = 0
        self.water = {e: {} for e in ENGINES}
        self.pending = {e: [] for e in ENGINES}

    def make_sems(self):
        for e in ENGINES:
            self.sems[e] = self.es.enter_context(self.nc.semaphore(f"sem_{e}"))

    @staticmethod
    def _key(ap):
        t = getattr(ap, "tensor", ap)
        return id(t)

    def _dma_sem(self, dkey):
        k = dkey if isinstance(dkey, str) else self._key(dkey)
        if k not in self.dsem:
            nm = f"dma{self.ndsem}"
            self.ndsem += 1
            self.sems[nm] = self.es.enter_context(self.nc.semaphore(nm))
            self.cnt[nm] = 0
            self.dsem[k] = nm
        return self.dsem[k]

    def op(self, eng, fn, reads=(), writes=(), dma=False, dkey=None,
           dram_in=None, dram_out=None, extra_waits=()):
        waits = {}

        def addw(ev):
            if ev is None:
                return
            sem, val = ev[0], ev[1]
            if sem == eng:
                return
            if waits.get(sem, 0) < val:
                waits[sem] = val

        for ap in reads:
            addw(self.lastw.get(self._key(ap)))
        for ap in writes:
            k = self._key(ap)
            addw(self.lastw.get(k))
            for ev in self.reads.get(k, ()):
                addw(ev)
        if dram_in is not None:
            if not isinstance(dram_in, (list, tuple)):
                dram_in = [dram_in]
            for k_ in dram_in:
                for sem, val in self.dramw.get(k_, {}).items():
                    addw((sem, val))
        for ev in extra_waits:
            addw(ev)
        for ev in self.pending[eng]:
            addw(ev)
        self.pending[eng] = []
        final_waits = []
        for sem, val in waits.items():
            if self.water[eng].get(sem, 0) >= val:
                continue
            self.water[eng][sem] = val
            final_waits.append((sem, val))
        if dma:
            assert dkey is not None, "dma op needs dkey (sbuf-side tile)"
            semn = self._dma_sem(dkey)
            self.cnt[semn] += 16
            ev = (semn, self.cnt[semn])
        else:
            self.cnt[eng] += 1
            ev = (eng, self.cnt[eng])
        self.ops[eng].append((fn, tuple(final_waits), 16 if dma else 1, ev[0]))
        for ap in writes:
            k = self._key(ap)
            self.lastw[k] = ev
            self.reads[k] = []
        for ap in reads:
            self.reads.setdefault(self._key(ap), []).append(ev)
        if dram_out is not None:
            d = self.dramw.setdefault(dram_out, {})
            d[ev[0]] = max(d.get(ev[0], 0), ev[1])
        return ev

    def all_events(self):
        return [(s, c) for s, c in self.cnt.items() if c > 0]

    def barrier(self):
        evs = self.all_events()
        for e in ENGINES:
            self.pending[e] = [ev for ev in evs if ev[0] != e]

    def replay(self, block):
        for e in ENGINES:
            eng_ops = self.ops[e]
            sems = self.sems

            def mk(ops_list):
                def body(engine):
                    for fn, waits, amt, incsem in ops_list:
                        for (sem, val) in waits:
                            engine.wait_ge(sems[sem], val)
                        inst = fn(engine)
                        inst.then_inc(sems[incsem], amt)
                return body
            getattr(block, e)(mk(eng_ops))


def build_program():
    nc = bass.Bass("TRN2", target_bir_lowering=False)

    def dram(name, shape, dt=F32, kind="Internal"):
        return nc.dram_tensor(name, shape, dt, kind=kind).ap()

    xT = dram("xT", [C, T], kind="ExternalInput")
    xe_in = dram("xe_in", [C, NSLOT * SW], kind="ExternalInput")
    xpr_in = dram("xpr_in", [C, NSLOT * SW], kind="ExternalInput")
    wtmkv_in = dram("wtmkv_in", [C, 2048], kind="ExternalInput")
    wtmr_in = dram("wtmr_in", [8 * 128, 8 * 128], kind="ExternalInput")
    wcm_in = dram("wcm_in", [32 * 128, 8 * 128], kind="ExternalInput")
    wcp_in = dram("wcp_in", [8 * 128, 32 * 128], kind="ExternalInput")
    wpj_in = dram("wpj_in", [32 * 128, 32 * 128], kind="ExternalInput")
    rm_in = dram("rm_in", [32 * 128, 128], kind="ExternalInput")
    ru_in = dram("ru_in", [128 * 128, 128], kind="ExternalInput")
    VNAMES = ("vw", "vu", "vtmk", "vtmv", "vtmr", "vcmk", "vcmr", "vg1", "vb1", "vg2", "vb2")
    vecs_in = {nm: dram(nm, [C, 1], kind="ExternalInput") for nm in VNAMES}
    mask_in = dram("mask_in", [128, 1], kind="ExternalInput")
    out = dram("out", [C, NSLOT * SLOT], kind="ExternalOutput")

    kbuf = dram("kbuf", [C, T])
    vbuf = dram("vbuf", [C, T])
    ybuf = dram("ybuf", [C, T])
    wtmr_b = dram("wtmr_b", [8 * 128, 8 * 128], BF16)
    wcm_b = dram("wcm_b", [32 * 128, 8 * 128], BF16)
    wcp_b = dram("wcp_b", [8 * 128, 32 * 128], BF16)
    wpj_b = dram("wpj_b", [32 * 128, 32 * 128], BF16)
    cbd_b = dram("cbd_b", [4 * 128, 32 * 128], BF16)

    with contextlib.ExitStack() as es:
        E = es.enter_context
        p = Prog(nc, es)
        p.make_sems()

        vt = {nm: E(nc.sbuf_tensor(nm + "_t", [128, NCH], F32)) for nm in VNAMES}
        OMN = ("vtmk", "vtmv", "vtmr", "vcmk", "vcmr")
        om = {nm: E(nc.sbuf_tensor("om_" + nm, [128, NCH], F32)) for nm in OMN}
        maskt = E(nc.sbuf_tensor("maskt", [128, 1], F32))
        epsT = E(nc.sbuf_tensor("epsT", [128, 1], F32))
        ones_b = E(nc.sbuf_tensor("ones_b", [128, 128], BF16))
        carryL = [E(nc.sbuf_tensor(f"carry{c}", [128, 4], F32)) for c in range(NCH)]

        def vs(tile, c):
            return tile[:, c:c + 1]

        for nm, d in vecs_in.items():
            for c in range(NCH):
                p.op("sync", (lambda nm, c, d: lambda e: e.dma_start(out=vs(vt[nm], c), in_=d[c * 128:(c + 1) * 128, :]))(nm, c, d),
                     writes=[vt[nm]], dma=True, dkey=vt[nm])
        p.op("sync", lambda e: e.dma_start(out=maskt[:], in_=mask_in), writes=[maskt], dma=True, dkey=maskt)
        maskom = E(nc.sbuf_tensor("maskom", [128, 1], F32))
        p.op("vector", lambda e: e.tensor_scalar(maskom[:], maskt[:], -1.0, 1.0, ALU.mult, ALU.add),
             reads=[maskt], writes=[maskom])
        for nm in OMN:
            for c in range(NCH):
                p.op("vector", (lambda nm, c: lambda e: e.tensor_scalar(vs(om[nm], c), vs(vt[nm], c), -1.0, 1.0, ALU.mult, ALU.add))(nm, c),
                     reads=[vt[nm]], writes=[om[nm]])
        p.op("vector", lambda e: e.memset(epsT[:], LN_EPS), writes=[epsT])
        p.op("vector", lambda e: e.memset(ones_b[:], 1.0), writes=[ones_b])
        for c in range(NCH):
            p.op("vector", (lambda c: lambda e: e.memset(carryL[c][:], 0.0))(c), writes=[carryL[c]])
            p.op("vector", (lambda c: lambda e: e.memset(carryL[c][:, 0:1], NEG))(c), writes=[carryL[c]])
            p.op("vector", (lambda c: lambda e: e.memset(carryL[c][:, 1:3], 1.0))(c), writes=[carryL[c]])

        with contextlib.ExitStack() as es1:
            E1 = es1.enter_context
            for (srcw, dstw, nm) in ((wcm_in, wcm_b, "wcm"), (wcp_in, wcp_b, "wcp"), (wpj_in, wpj_b, "wpj")):
                p.op("gpsimd", (lambda srcw, dstw: lambda e: e.dma_start(out=dstw, in_=srcw))(srcw, dstw),
                     dma=True, dkey="d2d_" + nm, dram_out=nm)
            rmt = E1(nc.sbuf_tensor("rmt", [128, 32 * 128], BF16))
            rut = E1(nc.sbuf_tensor("rut", [128, 128 * 128], BF16))
            cbt = E1(nc.sbuf_tensor("cbt", [128, 32 * 128], BF16))
            pmc = E1(nc.psum_tensor("pmc", [128, 512], F32))
            rm_r = rm_in.rearrange("(t p) f -> p t f", p=128)
            ru_r = ru_in.rearrange("(t p) f -> p t f", p=128)
            p.op("gpsimd", lambda e: e.dma_start(out=rmt[:].rearrange("p (t f) -> p t f", f=128), in_=rm_r),
                 writes=[rmt], dma=True, dkey=rmt)
            for q_ in range(4):
                p.op("gpsimd", (lambda q_: lambda e: e.dma_start(
                    out=rut[:, q_ * 4096:(q_ + 1) * 4096].rearrange("p (t f) -> p t f", f=128),
                    in_=ru_r[:, q_ * 32:(q_ + 1) * 32, :]))(q_),
                    writes=[rut], dma=True, dkey=rut)
            for kg in range(4):
                for kl in range(32):
                    k = kg * 32 + kl
                    hc = k // 4
                    p.op("tensor", (lambda kl, hc, k: lambda e: e.matmul(pmc[:, 0:128], rmt[:, hc * 128:(hc + 1) * 128],
                                                                         rut[:, k * 128:(k + 1) * 128], start=True, stop=True))(kl, hc, k),
                         reads=[rmt, rut], writes=[pmc])
                    p.op("scalar", (lambda kl: lambda e: e.activation(cbt[:, kl * 128:(kl + 1) * 128], pmc[:, 0:128], AF.Copy))(kl),
                         reads=[pmc], writes=[cbt])
                p.op("gpsimd", (lambda kg: lambda e: e.dma_start(out=cbd_b[kg * 128:(kg + 1) * 128, :], in_=cbt[:]))(kg),
                     reads=[cbt], dma=True, dkey=cbt, dram_out="cbd")
        p.barrier()

        with contextlib.ExitStack() as es2:
            E2 = es2.enter_context
            wtmkv_sb = E2(nc.sbuf_tensor("wtmkv_sb", [128, NCH * 2048], BF16))
            for c in range(NCH):
                p.op("gpsimd", (lambda c: lambda e: e.dma_start(out=wtmkv_sb[:, c * 2048:(c + 1) * 2048], in_=wtmkv_in[c * 128:(c + 1) * 128, :]))(c),
                     writes=[wtmkv_sb], dma=True, dkey=wtmkv_sb)

            xt = [E2(nc.sbuf_tensor(f"xt{c}", [128, 1 + TCS], F32)) for c in range(NCH)]
            xb16L = [E2(nc.sbuf_tensor(f"xb16_{j}", [128, TCS], BF16)) for j in range(2)]
            xsqL = [E2(nc.sbuf_tensor(f"xsq{j}", [128, TCS], BF16)) for j in range(2)]
            xn1bL = [[E2(nc.sbuf_tensor(f"xn1b{j}_{c}", [128, TCS], BF16)) for c in range(NCH)] for j in range(2)]
            mtileL = [E2(nc.sbuf_tensor(f"mtile{j}", [128, TCS], F32)) for j in range(2)]
            rstdtL = [E2(nc.sbuf_tensor(f"rstdt{j}", [128, TCS], F32)) for j in range(2)]
            vartL = [E2(nc.sbuf_tensor(f"vart{j}", [128, TCS], F32)) for j in range(2)]
            msqtL = [E2(nc.sbuf_tensor(f"msqt{j}", [128, TCS], F32)) for j in range(2)]
            dscL = [E2(nc.sbuf_tensor(f"sdsc{j}", [128, TCS], F32)) for j in range(4)]
            kvevL = [E2(nc.sbuf_tensor(f"kvev{j}", [128, TCS], F32)) for j in range(4)]
            xppL = [E2(nc.sbuf_tensor(f"xpp{j}", [128, TCS], F32)) for j in range(4)]
            ps_sL = [E2(nc.psum_tensor(f"ps_s{j}", [128, TCS], F32)) for j in range(2)]
            ps_qL = [E2(nc.psum_tensor(f"ps_q{j}", [128, TCS], F32)) for j in range(2)]
            ps_kv = [E2(nc.psum_tensor(f"ps_kv{i}", [128, TCS], F32)) for i in range(2)]

            for tc in range(NTCS):
                t0 = tc * TCS
                par = tc % 2
                xn1b = xn1bL[par]
                mtile, rstdt, vart, msqt = mtileL[par], rstdtL[par], vartL[par], msqtL[par]
                ps_s, ps_q = ps_sL[par], ps_qL[par]
                for c in range(NCH):
                    if tc == 0:
                        p.op("sync", (lambda c: lambda e: e.dma_start(out=xt[c][:, 1:1 + TCS], in_=xT[c * 128:(c + 1) * 128, 0:TCS]))(c),
                             writes=[xt[c]], dma=True, dkey=xt[c])
                        p.op("sync", (lambda c: lambda e: e.dma_start(out=xt[c][:, 0:1], in_=xT[c * 128:(c + 1) * 128, 0:1], allow_slow_non_contiguous=True))(c),
                             writes=[xt[c]], dma=True, dkey=xt[c])
                    else:
                        p.op("sync", (lambda c, t0: lambda e: e.dma_start(out=xt[c][:], in_=xT[c * 128:(c + 1) * 128, t0 - 1:t0 + TCS]))(c, t0),
                             writes=[xt[c]], dma=True, dkey=xt[c])
                for c in range(NCH):
                    xb = xb16L[c % 2]
                    p.op("scalar", (lambda c, xb: lambda e: e.activation(xb[:], xt[c][:, 1:1 + TCS], AF.Copy))(c, xb),
                         reads=[xt[c]], writes=[xb])
                    p.op("tensor", (lambda c, xb: lambda e: e.matmul(ps_s[:], ones_b[:], xb[:], start=(c == 0), stop=(c == NCH - 1)))(c, xb),
                         reads=[ones_b, xb], writes=[ps_s])
                for c in range(NCH):
                    xq = xsqL[c % 2]
                    p.op("scalar", (lambda c, xq: lambda e: e.activation(xq[:], xt[c][:, 1:1 + TCS], AF.Square))(c, xq),
                         reads=[xt[c]], writes=[xq])
                    p.op("tensor", (lambda c, xq: lambda e: e.matmul(ps_q[:], ones_b[:], xq[:], start=(c == 0), stop=(c == NCH - 1)))(c, xq),
                         reads=[ones_b, xq], writes=[ps_q])
                p.op("vector", lambda e: e.tensor_scalar(mtile[:], ps_s[:], 1.0 / C, None, ALU.mult), reads=[ps_s], writes=[mtile])
                p.op("scalar", lambda e: e.activation(msqt[:], mtile[:], AF.Square), reads=[mtile], writes=[msqt])
                p.op("vector", lambda e: e.scalar_tensor_tensor(vart[:], ps_q[:], 1.0 / C, msqt[:], ALU.mult, ALU.subtract),
                     reads=[ps_q, msqt], writes=[vart])
                p.op("scalar", lambda e: e.activation(vart[:], vart[:], AF.Ln, bias=epsT[:]), reads=[vart, epsT], writes=[vart])
                p.op("scalar", lambda e: e.activation(rstdt[:], vart[:], AF.Exp, scale=-0.5), reads=[vart], writes=[rstdt])
                for c in range(NCH):
                    dsc = dscL[(tc * NCH + c) % 4]
                    p.op("vector" if c % 2 == 0 else "gpsimd", (lambda c, dsc: lambda e: e.tensor_sub(dsc[:], xt[c][:, 1:1 + TCS], mtile[:]))(c, dsc),
                         reads=[xt[c], mtile], writes=[dsc])
                    p.op("vector", (lambda c, dsc: lambda e: e.scalar_tensor_tensor(dsc[:], dsc[:], vs(vt["vg1"], c), rstdt[:], ALU.mult, ALU.mult))(c, dsc),
                         reads=[dsc, rstdt, vt["vg1"]], writes=[dsc])
                    p.op("vector", (lambda c, dsc: lambda e: e.tensor_scalar(xn1b[c][:], dsc[:], vs(vt["vb1"], c), None, ALU.add))(c, dsc),
                         reads=[dsc, vt["vb1"]], writes=[xn1b[c]])
                m_order = [mm for cc_ in range(8) for mm in (cc_, cc_ + 8)]
                for mi, m in enumerate(m_order):
                    pkv = ps_kv[mi % 2]

                    def mmgrp(e, m=m, pkv=pkv):
                        for c in range(NCH):
                            inst = e.matmul(pkv[:], wtmkv_sb[:, c * 2048 + m * 128:c * 2048 + (m + 1) * 128],
                                            xn1b[c][:], start=(c == 0), stop=(c == NCH - 1))
                        return inst
                    p.op("tensor", mmgrp, reads=[wtmkv_sb] + [xn1b[c] for c in range(NCH)], writes=[pkv])
                    isk = m < 8
                    cc = m % 8
                    mixv = vt["vtmk"] if isk else vt["vtmv"]
                    omv = om["vtmk"] if isk else om["vtmv"]
                    kvt = kvevL[mi % 4]
                    xpp = xppL[mi % 4]
                    dstbuf = kbuf if isk else vbuf
                    p.op("vector" if mi % 2 == 0 else "gpsimd", (lambda cc, omv, xpp: lambda e: e.tensor_scalar(xpp[:], xt[cc][:, 0:TCS], vs(omv, cc), None, ALU.mult))(cc, omv, xpp),
                         reads=[xt[cc], omv], writes=[xpp])
                    p.op("vector", (lambda pkv, mixv, cc, kvt, xpp: lambda e: e.scalar_tensor_tensor(kvt[:], pkv[:], vs(mixv, cc), xpp[:], ALU.mult, ALU.add))(pkv, mixv, cc, kvt, xpp),
                         reads=[pkv, mixv, xpp], writes=[kvt])
                    p.op("gpsimd", (lambda dstbuf, cc, t0, kvt: lambda e: e.dma_start(out=dstbuf[cc * 128:(cc + 1) * 128, t0:t0 + TCS], in_=kvt[:]))(dstbuf, cc, t0, kvt),
                         reads=[kvt], dma=True, dkey=kvt, dram_out=("kv", 0 if isk else 1, tc))

            wbc = E2(nc.sbuf_tensor("wbc", [128, NCH * LW], F32))
            for c in range(NCH):
                p.op("scalar", (lambda c: lambda e: e.memzero(wbc[:, c * LW:(c + 1) * LW]))(c), writes=[wbc])
            for c in range(NCH):
                p.op("vector", (lambda c: lambda e: e.tensor_scalar(wbc[:, c * LW:(c + 1) * LW], wbc[:, c * LW:(c + 1) * LW], vs(vt["vw"], c), None, ALU.add))(c),
                     reads=[vt["vw"]], writes=[wbc])

            def mkset(j):
                return {nm: E2(nc.sbuf_tensor(f"wkv{nm}{j}", [128, 1 + LW], F32)) for nm in ("mT", "SgH", "Sg", "aaT", "sT")} |                        {nm: E2(nc.sbuf_tensor(f"wkv{nm}{j}", [128, LW], F32)) for nm in ("KT", "VT", "w1", "w2", "w3", "w4", "w5", "w6", "yt")}
            sets = [mkset(0), mkset(1)]

            def wkv_unit(wt, c, S):
                t0 = wt * LW
                mT, SgH, Sg, aaT, sT = S["mT"], S["SgH"], S["Sg"], S["aaT"], S["sT"]
                KT, VT = S["KT"], S["VT"]
                w1, w2, w3, w4, w5, w6, yt = S["w1"], S["w2"], S["w3"], S["w4"], S["w5"], S["w6"], S["yt"]
                car = carryL[c]
                wv_c = vs(vt["vw"], c)
                wu_c = vs(vt["vu"], c)
                wbc_c = wbc[:, c * LW:(c + 1) * LW]
                yield ("sync", (lambda c, t0, KT: lambda e: e.dma_start(out=KT[:], in_=kbuf[c * 128:(c + 1) * 128, t0:t0 + LW]))(c, t0, KT),
                       dict(writes=[KT], dma=True, dkey=KT, dram_in=[("kv", 0, wt)]))
                yield ("sync", (lambda c, t0, VT: lambda e: e.dma_start(out=VT[:], in_=vbuf[c * 128:(c + 1) * 128, t0:t0 + LW]))(c, t0, VT),
                       dict(writes=[VT], dma=True, dkey=VT, dram_in=[("kv", 1, wt)]))
                yield ("scalar", (lambda car, mT: lambda e: e.copy(mT[:, 0:1], car[:, 0:1]))(car, mT), dict(reads=[car], writes=[mT]))
                yield ("vector", (lambda wbc_c, car, mT, KT: lambda e: e.tensor_tensor_scan(mT[:, 1:1 + LW], wbc_c, KT[:], car[:, 0:1], ALU.add, ALU.max))(wbc_c, car, mT, KT),
                       dict(reads=[wbc, KT, car], writes=[mT]))
                yield ("vector", (lambda wv_c, mT, w1: lambda e: e.scalar_tensor_tensor(w1[:], mT[:, 0:LW], wv_c, mT[:, 1:1 + LW], ALU.add, ALU.subtract))(wv_c, mT, w1),
                       dict(reads=[mT, vt["vw"]], writes=[w1]))
                yield ("scalar", (lambda w1: lambda e: e.activation(w1[:], w1[:], AF.Exp))(w1), dict(reads=[w1], writes=[w1]))
                yield ("vector", (lambda w2, KT, mT: lambda e: e.tensor_sub(w2[:], KT[:], mT[:, 1:1 + LW]))(w2, KT, mT), dict(reads=[KT, mT], writes=[w2]))
                yield ("scalar", (lambda w2: lambda e: e.activation(w2[:], w2[:], AF.Exp))(w2), dict(reads=[w2], writes=[w2]))
                yield ("scalar", (lambda car, SgH: lambda e: e.copy(SgH[:, 0:1], car[:, 1:2]))(car, SgH), dict(reads=[car], writes=[SgH]))
                yield ("vector", (lambda car, SgH, w1, w2: lambda e: e.tensor_tensor_scan(SgH[:, 1:1 + LW], w1[:], w2[:], car[:, 1:2], ALU.mult, ALU.add))(car, SgH, w1, w2),
                       dict(reads=[w1, w2, car], writes=[SgH]))
                yield ("gpsimd", (lambda w3, w1, SgH: lambda e: e.tensor_mul(w3[:], w1[:], SgH[:, 0:LW]))(w3, w1, SgH), dict(reads=[w1, SgH], writes=[w3]))
                yield ("vector", (lambda w3, w2: lambda e: e.tensor_max(w3[:], w3[:], w2[:]))(w3, w2), dict(reads=[w3, w2], writes=[w3]))
                yield ("vector", (lambda w3, w2: lambda e: e.scalar_tensor_tensor(w3[:], w3[:], WKV_EPS, w2[:], ALU.mult, ALU.add))(w3, w2),
                       dict(reads=[w3, w2], writes=[w3]))
                yield ("scalar", (lambda car, Sg: lambda e: e.copy(Sg[:, 0:1], car[:, 2:3]))(car, Sg), dict(reads=[car], writes=[Sg]))
                yield ("vector", (lambda car, Sg, w1, w3: lambda e: e.tensor_tensor_scan(Sg[:, 1:1 + LW], w1[:], w3[:], car[:, 2:3], ALU.mult, ALU.add))(car, Sg, w1, w3),
                       dict(reads=[w1, w3, car], writes=[Sg]))
                yield ("scalar", (lambda Sg, sT: lambda e: e.activation(sT[:], Sg[:], AF.Ln))(Sg, sT), dict(reads=[Sg], writes=[sT]))
                yield ("gpsimd", (lambda sT, mT: lambda e: e.tensor_add(sT[:], mT[:], sT[:]))(sT, mT), dict(reads=[mT, sT], writes=[sT]))
                yield ("vector", (lambda wv_c, sT, KT, w4: lambda e: e.scalar_tensor_tensor(w4[:], sT[:, 0:LW], wv_c, KT[:], ALU.add, ALU.subtract))(wv_c, sT, KT, w4),
                       dict(reads=[sT, vt["vw"], KT], writes=[w4]))
                yield ("scalar", (lambda w5, w4: lambda e: e.activation(w5[:], w4[:], AF.Relu, scale=-1.0))(w5, w4), dict(reads=[w4], writes=[w5]))
                yield ("scalar", (lambda w5: lambda e: e.activation(w5[:], w5[:], AF.Exp, scale=-1.0))(w5), dict(reads=[w5], writes=[w5]))
                yield ("scalar", (lambda w4: lambda e: e.activation(w4[:], w4[:], AF.Relu))(w4), dict(reads=[w4], writes=[w4]))
                yield ("scalar", (lambda w4: lambda e: e.activation(w4[:], w4[:], AF.Exp, scale=-1.0))(w4), dict(reads=[w4], writes=[w4]))
                yield ("vector", (lambda w4, VT: lambda e: e.tensor_mul(w4[:], w4[:], VT[:]))(w4, VT), dict(reads=[w4, VT], writes=[w4]))
                yield ("scalar", (lambda car, aaT: lambda e: e.copy(aaT[:, 0:1], car[:, 3:4]))(car, aaT), dict(reads=[car], writes=[aaT]))
                yield ("vector", (lambda car, aaT, w5, w4: lambda e: e.tensor_tensor_scan(aaT[:, 1:1 + LW], w5[:], w4[:], car[:, 3:4], ALU.mult, ALU.add))(car, aaT, w5, w4),
                       dict(reads=[w5, w4, car], writes=[aaT]))
                yield ("scalar", (lambda car, mT: lambda e: e.copy(car[:, 0:1], mT[:, LW:1 + LW]))(car, mT), dict(reads=[mT], writes=[car]))
                yield ("scalar", (lambda car, SgH: lambda e: e.copy(car[:, 1:2], SgH[:, LW:1 + LW]))(car, SgH), dict(reads=[SgH], writes=[car]))
                yield ("scalar", (lambda car, Sg: lambda e: e.copy(car[:, 2:3], Sg[:, LW:1 + LW]))(car, Sg), dict(reads=[Sg], writes=[car]))
                yield ("scalar", (lambda car, aaT: lambda e: e.copy(car[:, 3:4], aaT[:, LW:1 + LW]))(car, aaT), dict(reads=[aaT], writes=[car]))
                yield ("vector", (lambda wu_c, KT, sT, w6: lambda e: e.scalar_tensor_tensor(w6[:], KT[:], wu_c, sT[:, 0:LW], ALU.add, ALU.subtract))(wu_c, KT, sT, w6),
                       dict(reads=[KT, vt["vu"], sT], writes=[w6]))
                yield ("scalar", (lambda w1, w6: lambda e: e.activation(w1[:], w6[:], AF.Relu, scale=-1.0))(w1, w6), dict(reads=[w6], writes=[w1]))
                yield ("scalar", (lambda w1: lambda e: e.activation(w1[:], w1[:], AF.Exp, scale=-1.0))(w1), dict(reads=[w1], writes=[w1]))
                yield ("scalar", (lambda w6: lambda e: e.activation(w6[:], w6[:], AF.Relu))(w6), dict(reads=[w6], writes=[w6]))
                yield ("scalar", (lambda w6: lambda e: e.activation(w6[:], w6[:], AF.Exp, scale=-1.0))(w6), dict(reads=[w6], writes=[w6]))
                yield ("vector", (lambda w2, w6, aaT: lambda e: e.tensor_mul(w2[:], w6[:], aaT[:, 0:LW]))(w2, w6, aaT), dict(reads=[w6, aaT], writes=[w2]))
                yield ("vector", (lambda w3, w1, VT: lambda e: e.tensor_mul(w3[:], w1[:], VT[:]))(w3, w1, VT), dict(reads=[w1, VT], writes=[w3]))
                yield ("vector", (lambda w2, w3: lambda e: e.tensor_add(w2[:], w2[:], w3[:]))(w2, w3), dict(reads=[w2, w3], writes=[w2]))
                yield ("vector", (lambda w3, w6, w1: lambda e: e.scalar_tensor_tensor(w3[:], w6[:], WKV_EPS, w1[:], ALU.add, ALU.add))(w3, w6, w1),
                       dict(reads=[w6, w1], writes=[w3]))
                yield ("scalar", (lambda w3: lambda e: e.activation(w3[:], w3[:], AF.Ln))(w3), dict(reads=[w3], writes=[w3]))
                yield ("scalar", (lambda w3: lambda e: e.activation(w3[:], w3[:], AF.Exp, scale=-1.0))(w3), dict(reads=[w3], writes=[w3]))
                yield ("vector", (lambda yt, w2, w3: lambda e: e.tensor_mul(yt[:], w2[:], w3[:]))(yt, w2, w3), dict(reads=[w2, w3], writes=[yt]))
                yield ("gpsimd", (lambda c, t0, yt: lambda e: e.dma_start(out=ybuf[c * 128:(c + 1) * 128, t0:t0 + LW], in_=yt[:]))(c, t0, yt),
                       dict(reads=[yt], dma=True, dkey=yt, dram_out=("y", wt)))

            units = [(wt, c) for wt in range(NLW) for c in range(NCH)]
            gi_ = 0
            while gi_ < len(units):
                pair = units[gi_:gi_ + 2]
                gens = [wkv_unit(wt_, c_, sets[j]) for j, (wt_, c_) in enumerate(pair)]
                gi_ += 2
                live = list(gens)
                while live:
                    nxt = []
                    for g in live:
                        try:
                            eng, fn, kw = next(g)
                            p.op(eng, fn, **kw)
                            nxt.append(g)
                        except StopIteration:
                            pass
                    live = nxt
        p.barrier()

        with contextlib.ExitStack() as es3:
            E3 = es3.enter_context
            bufA = [E3(nc.sbuf_tensor(f"bufA{c}", [128, SW], F32)) for c in range(NCH)]
            bufB = [E3(nc.sbuf_tensor(f"bufB{c}", [128, SW], F32)) for c in range(NCH)]
            bufC = [E3(nc.sbuf_tensor(f"bufC{c}", [128, SW], F32)) for c in range(NCH)]
            bufD = [E3(nc.sbuf_tensor(f"bufD{c}", [128, SW], F32)) for c in range(NCH)]
            bufE = [E3(nc.sbuf_tensor(f"bufE{c}", [128, SW], F32)) for c in range(NCH)]
            x2d = [E3(nc.sbuf_tensor(f"x2d{c}", [128, SW], F32)) for c in range(NCH)]
            pool = [E3(nc.sbuf_tensor(f"pool{i}", [128, SW], BF16)) for i in range(32)]
            nb16 = [E3(nc.sbuf_tensor(f"nb16_{c}", [128, SLOT], BF16)) for c in range(NCH)]
            ycn = [E3(nc.sbuf_tensor(f"ycn{j}", [128, 544], F32)) for j in range(4)]
            dscA = [E3(nc.sbuf_tensor(f"dscA{j}", [128, SW], F32)) for j in range(2)]
            cbt2 = E3(nc.sbuf_tensor("cbt2", [128, 4096], BF16))
            mtile2 = E3(nc.sbuf_tensor("mtile2", [128, SW], F32))
            rstdt2 = E3(nc.sbuf_tensor("rstdt2", [128, SW], F32))
            vart2 = E3(nc.sbuf_tensor("vart2", [128, SW], F32))
            msqt2 = E3(nc.sbuf_tensor("msqt2", [128, SW], F32))
            dsc2 = E3(nc.sbuf_tensor("dsc2", [128, SW], F32))
            dsc3 = E3(nc.sbuf_tensor("dsc3", [128, SW], F32))
            xn2f = E3(nc.sbuf_tensor("xn2f", [128, SLOT], F32))
            xb16pL = [E3(nc.sbuf_tensor(f"xb16p{j}", [128, SW], BF16)) for j in range(2)]
            xsqpL = [E3(nc.sbuf_tensor(f"xsqp{j}", [128, SW], BF16)) for j in range(2)]
            wsmL = [E3(nc.sbuf_tensor(f"wsm{j}", [128, 1024], BF16)) for j in range(2)]
            wbigL = [E3(nc.sbuf_tensor(f"wbig{j}", [128, 4096], BF16)) for j in range(2)]
            outt = E3(nc.sbuf_tensor("outt", [128, SLOT], F32))
            ps_a = E3(nc.psum_tensor("ps_a", [128, SW], F32))
            ps_b = E3(nc.psum_tensor("ps_b", [128, SW], F32))
            ps_c = E3(nc.psum_tensor("ps_c", [128, SLOT], F32))
            ps_d = E3(nc.psum_tensor("ps_d", [128, SLOT], F32))
            ps_n1 = E3(nc.psum_tensor("ps_n1", [128, 1], F32))

            wcm_r = wcm_b.rearrange("(m p) f -> m p f", p=128)
            wcp_r = wcp_b.rearrange("(o p) f -> o p f", p=128)
            wpj_r = wpj_b.rearrange("(g p) f -> g p f", p=128)

            def ln_stats(src_tiles, width, off):
                pieces = [(0, min(width, SLOT))]
                if width > SLOT:
                    pieces.append((SLOT, width))
                for (ps, actf, scrL) in ((ps_a, AF.Copy, xb16pL), (ps_b, AF.Square, xsqpL)):
                    for lo, hi in pieces:
                        for c in range(NCH):
                            scratch = scrL[c % 2]
                            p.op("scalar", (lambda c, lo, hi, actf, scratch: lambda e: e.activation(
                                scratch[:, lo:hi], src_tiles[c][:, off + lo:off + hi], actf))(c, lo, hi, actf, scratch),
                                 reads=[src_tiles[c]], writes=[scratch])
                            p.op("tensor", (lambda c, lo, hi, ps, scratch: lambda e: e.matmul(
                                ps[:, lo:hi], ones_b[:], scratch[:, lo:hi],
                                start=(c == 0), stop=(c == NCH - 1)))(c, lo, hi, ps, scratch),
                                 reads=[ones_b, scratch], writes=[ps])
                p.op("vector", lambda e: e.tensor_scalar(mtile2[:, 0:width], ps_a[:, 0:width], 1.0 / C, None, ALU.mult),
                     reads=[ps_a], writes=[mtile2])
                p.op("scalar", lambda e: e.activation(msqt2[:, 0:width], mtile2[:, 0:width], AF.Square),
                     reads=[mtile2], writes=[msqt2])
                p.op("vector", lambda e: e.scalar_tensor_tensor(vart2[:, 0:width], ps_b[:, 0:width], 1.0 / C, msqt2[:, 0:width], ALU.mult, ALU.subtract),
                     reads=[ps_b, msqt2], writes=[vart2])
                p.op("scalar", lambda e: e.activation(vart2[:, 0:width], vart2[:, 0:width], AF.Ln, bias=epsT[:]),
                     reads=[vart2, epsT], writes=[vart2])
                p.op("scalar", lambda e: e.activation(rstdt2[:, 0:width], vart2[:, 0:width], AF.Exp, scale=-0.5),
                     reads=[vart2], writes=[rstdt2])

            import os as _os
            if _os.environ.get("KDBG2"):
                dbg_early = dram("dbg_early", [C, T], kind="ExternalOutput")
                dearly = E3(nc.sbuf_tensor("dearly", [128, 1024], F32))
                for c in range(NCH):
                    for tt_ in range(4):
                        p.op("sync", (lambda c, tt_: lambda e: e.dma_start(out=dearly[:], in_=ybuf[c * 128:(c + 1) * 128, tt_ * 1024:(tt_ + 1) * 1024]))(c, tt_),
                             writes=[dearly], dma=True, dkey=dearly, dram_in=[("y", w_) for w_ in range(NLW)])
                        p.op("gpsimd", (lambda c, tt_: lambda e: e.dma_start(out=dbg_early[c * 128:(c + 1) * 128, tt_ * 1024:(tt_ + 1) * 1024], in_=dearly[:]))(c, tt_),
                             reads=[dearly], dma=True, dkey=dearly)
                dbg_x2 = dram("dbg_x2", [C, NSLOT * SW], kind="ExternalOutput")
                dbg_yb = dram("dbg_yb", [C, NSLOT * SW], kind="ExternalOutput")
                dbg_rt = dram("dbg_rt", [C, NSLOT * SW], kind="ExternalOutput")
                dbg_x1 = dram("dbg_x1", [C, NSLOT * SLOT], kind="ExternalOutput")
                dbg_fb = dram("dbg_fb", [C, NSLOT * SLOT], kind="ExternalOutput")
            for i in range(NSLOT):
                xe = bufD
                xpr = bufA
                yb = bufB
                rt = bufC
                x2 = bufE
                for c in range(NCH):
                    p.op("sync", (lambda c, i: lambda e: e.dma_start(out=bufD[c][:], in_=xe_in[c * 128:(c + 1) * 128, i * SW:(i + 1) * SW]))(c, i),
                         writes=[bufD[c]], dma=True, dkey=bufD[c])
                    p.op("sync", (lambda c, i: lambda e: e.dma_start(out=bufA[c][:], in_=xpr_in[c * 128:(c + 1) * 128, i * SW:(i + 1) * SW]))(c, i),
                         writes=[bufA[c]], dma=True, dkey=bufA[c])
                s0_ = (2 * i) * 512
                s1_ = (2 * i + 1) * 512
                yreg0 = sorted({("y", (s0_ // 512) % NLW), ("y", ((s0_ - 1) // 512) % NLW)})
                yreg1 = sorted({("y", (s1_ // 512) % NLW), ("y", ((s1_ - 1) // 512) % NLW)})
                for c in range(NCH):
                    s0 = (2 * i) * 512
                    s1 = (2 * i + 1) * 512
                    y0t = ycn[(c % 2) * 2]
                    y1t = ycn[(c % 2) * 2 + 1]
                    y0 = y0t[:, 0:SW]
                    y1 = y1t[:, 0:SW]
                    if i == 0:
                        p.op("sync", (lambda c, y0: lambda e: e.dma_start(out=y0[:, 0:1], in_=ybuf[c * 128:(c + 1) * 128, T - 1:T], allow_slow_non_contiguous=True))(c, y0),
                             writes=[y0], dma=True, dkey=y0t, dram_in=yreg0)
                        p.op("sync", (lambda c, y0: lambda e: e.dma_start(out=y0[:, 1:SW], in_=ybuf[c * 128:(c + 1) * 128, 0:512]))(c, y0),
                             writes=[y0], dma=True, dkey=y0t, dram_in=yreg0)
                    else:
                        p.op("sync", (lambda c, s0, y0: lambda e: e.dma_start(out=y0, in_=ybuf[c * 128:(c + 1) * 128, s0 - 1:s0 + 512]))(c, s0, y0),
                             writes=[y0], dma=True, dkey=y0t, dram_in=yreg0)
                    p.op("sync", (lambda c, s1, y1: lambda e: e.dma_start(out=y1, in_=ybuf[c * 128:(c + 1) * 128, s1 - 1:s1 + 512]))(c, s1, y1),
                         writes=[y1], dma=True, dkey=y1t, dram_in=yreg1)
                    p.op("vector", (lambda c, y1: lambda e: e.tensor_scalar(bufB[c][:], y1, maskt[:], None, ALU.mult))(c, y1),
                         reads=[y1, maskt], writes=[bufB[c]])
                    p.op("vector", (lambda c, y0: lambda e: e.scalar_tensor_tensor(bufB[c][:], y0, maskom[:], bufB[c][:], ALU.mult, ALU.add))(c, y0),
                         reads=[bufB[c], maskom, y0], writes=[bufB[c]])
                ln_stats(xe, SW, 0)
                for c in range(NCH):
                    ds = dscA[c % 2]
                    p.op("vector" if c % 2 == 0 else "gpsimd", (lambda c, ds: lambda e: e.tensor_sub(ds[:], xe[c][:], mtile2[:]))(c, ds),
                         reads=[xe[c], mtile2], writes=[ds])
                    p.op("vector", (lambda c, ds: lambda e: e.scalar_tensor_tensor(ds[:], ds[:], vs(vt["vg1"], c), rstdt2[:], ALU.mult, ALU.mult))(c, ds),
                         reads=[ds, rstdt2, vt["vg1"]], writes=[ds])
                    p.op("vector", (lambda c, ds: lambda e: e.tensor_scalar(pool[c][:], ds[:], vs(vt["vb1"], c), None, ALU.add))(c, ds),
                         reads=[ds, vt["vb1"]], writes=[pool[c]])
                psr = [ps_c, ps_d]
                for m in range(NCH):
                    wbt = wsmL[m % 2]
                    wb = wbt[:]
                    pr = psr[m % 2]
                    p.op("gpsimd", (lambda m, wb: lambda e: e.dma_start(out=wb, in_=wtmr_in[m * 128:(m + 1) * 128, :]))(m, wb),
                         writes=[wbt], dma=True, dkey=wbt)

                    def rgrp(e, wb=wb, pr=pr):
                        for c in range(NCH):
                            inst = e.matmul(pr[:], wb[:, c * 128:(c + 1) * 128],
                                            pool[c][:, 0:SLOT], start=(c == 0), stop=(c == NCH - 1))
                        return inst
                    p.op("tensor", rgrp, reads=[wbt] + [pool[c] for c in range(NCH)], writes=[pr])

                    def rgrp1(e, wb=wb):
                        for c in range(NCH):
                            inst = e.matmul(ps_n1[:], wb[:, c * 128:(c + 1) * 128],
                                            pool[c][:, SLOT:SW], start=(c == 0), stop=(c == NCH - 1))
                        return inst
                    p.op("tensor", rgrp1, reads=[wbt] + [pool[c] for c in range(NCH)], writes=[ps_n1])
                    ds = dscA[m % 2]
                    p.op("gpsimd", (lambda m, ds: lambda e: e.tensor_scalar(ds[:], xpr[m][:], vs(om["vtmr"], m), None, ALU.mult))(m, ds),
                         reads=[xpr[m], om["vtmr"]], writes=[ds])
                    p.op("vector", (lambda m, ds, pr: lambda e: e.scalar_tensor_tensor(rt[m][:, 0:SLOT], pr[:], vs(vt["vtmr"], m), ds[:, 0:SLOT], ALU.mult, ALU.add))(m, ds, pr),
                         reads=[pr, vt["vtmr"], ds], writes=[rt[m]])
                    p.op("vector", (lambda m, ds: lambda e: e.scalar_tensor_tensor(rt[m][:, SLOT:SW], ps_n1[:], vs(vt["vtmr"], m), ds[:, SLOT:SW], ALU.mult, ALU.add))(m, ds),
                         reads=[ps_n1, vt["vtmr"], ds], writes=[rt[m]])
                    p.op("scalar", (lambda m: lambda e: e.activation(rt[m][:], rt[m][:], AF.Sigmoid))(m),
                         reads=[rt[m]], writes=[rt[m]])
                for c in range(NCH):
                    e1_ = "gpsimd" if c % 2 == 0 else "vector"
                    e2_ = "vector" if c % 2 == 0 else "gpsimd"
                    p.op(e1_, (lambda c: lambda e: e.tensor_mul(x2[c][:], rt[c][:], yb[c][:]))(c),
                         reads=[rt[c], yb[c]], writes=[x2[c]])
                    p.op(e2_, (lambda c: lambda e: e.tensor_add(x2[c][:], x2[c][:], xe[c][:]))(c),
                         reads=[x2[c], xe[c]], writes=[x2[c]])
                    p.op(e1_, (lambda c: lambda e: e.tensor_add(x2d[c][:], x2[c][:], xe[c][:]))(c),
                         reads=[x2[c], xe[c]], writes=[x2d[c]])
                import os as _os
                if _os.environ.get("KDBG2"):
                    for c in range(NCH):
                        p.op("gpsimd", (lambda c, i: lambda e: e.dma_start(out=dbg_x2[c * 128:(c + 1) * 128, i * SW:(i + 1) * SW], in_=x2[c][:]))(c, i),
                             reads=[x2[c]], dma=True, dkey=x2[c])
                        p.op("gpsimd", (lambda c, i: lambda e: e.dma_start(out=dbg_yb[c * 128:(c + 1) * 128, i * SW:(i + 1) * SW], in_=bufB[c][:]))(c, i),
                             reads=[bufB[c]], dma=True, dkey=bufB[c])
                        p.op("gpsimd", (lambda c, i: lambda e: e.dma_start(out=dbg_rt[c * 128:(c + 1) * 128, i * SW:(i + 1) * SW], in_=rt[c][:]))(c, i),
                             reads=[rt[c]], dma=True, dkey=rt[c])
                ln_stats(x2, SLOT, 1)
                xppre = bufA
                rr = bufB
                xn2fL = [xn2f, dsc2]
                for c in range(NCH):
                    ds = dscA[c % 2]
                    xf = xn2fL[c % 2]
                    p.op("vector" if c % 2 == 0 else "gpsimd", (lambda c, ds: lambda e: e.tensor_sub(ds[:, 0:SLOT], x2[c][:, 1:SW], mtile2[:, 0:SLOT]))(c, ds),
                         reads=[x2[c], mtile2], writes=[ds])
                    p.op("vector", (lambda c, ds, xf: lambda e: e.scalar_tensor_tensor(xf[:, 0:SLOT], ds[:, 0:SLOT], vs(vt["vg2"], c), rstdt2[:, 0:SLOT], ALU.mult, ALU.mult))(c, ds, xf),
                         reads=[ds, rstdt2, vt["vg2"]], writes=[xf])
                    p.op("vector", (lambda c, xf: lambda e: e.tensor_scalar(xf[:, 0:SLOT], xf[:, 0:SLOT], vs(vt["vb2"], c), None, ALU.add))(c, xf),
                         reads=[xf, vt["vb2"]], writes=[xf])
                    p.op("scalar", (lambda c, xf: lambda e: e.activation(nb16[c][:], xf[:, 0:SLOT], AF.Copy))(c, xf),
                         reads=[xf], writes=[nb16[c]])
                    p.op("gpsimd", (lambda c, ds: lambda e: e.tensor_scalar(ds[:, 0:SLOT], x2[c][:, 0:SLOT], vs(om["vcmr"], c), None, ALU.mult))(c, ds),
                         reads=[x2[c], om["vcmr"]], writes=[ds])
                    p.op("vector", (lambda c, ds, xf: lambda e: e.scalar_tensor_tensor(rr[c][:, 0:SLOT], xf[:, 0:SLOT], vs(vt["vcmr"], c), ds[:, 0:SLOT], ALU.mult, ALU.add))(c, ds, xf),
                         reads=[xf, vt["vcmr"], ds], writes=[rr[c]])
                    p.op("scalar", (lambda c: lambda e: e.activation(rr[c][:, 0:SLOT], rr[c][:, 0:SLOT], AF.Sigmoid))(c),
                         reads=[rr[c]], writes=[rr[c]])
                    p.op("gpsimd", (lambda c: lambda e: e.tensor_scalar(xppre[c][:, 0:SLOT], x2[c][:, 0:SLOT], vs(om["vcmk"], c), None, ALU.mult))(c),
                         reads=[x2[c], om["vcmk"]], writes=[xppre[c]])
                for m in range(32):
                    wbt = wsmL[m % 2]
                    wb = wbt[:]
                    p.op("sync", (lambda m, wb: lambda e: e.dma_start(out=wb, in_=wcm_r[m]))(m, wb),
                         writes=[wbt], dma=True, dkey=wbt, dram_in="wcm")
                    pk = ps_c if (m % 2 == 0) else ps_d

                    def cmgrp(e, wb=wb, pk=pk):
                        for c in range(NCH):
                            inst = e.matmul(pk[:], wb[:, c * 128:(c + 1) * 128],
                                            nb16[c][:], start=(c == 0), stop=(c == NCH - 1))
                        return inst
                    p.op("tensor", cmgrp, reads=[wbt] + [nb16[c] for c in range(NCH)], writes=[pk])
                    cc = m % 8
                    ds = dscA[m % 2]
                    p.op("vector", (lambda cc, pk, ds: lambda e: e.scalar_tensor_tensor(ds[:, 0:SLOT], pk[:], vs(vt["vcmk"], cc), xppre[cc][:, 0:SLOT], ALU.mult, ALU.add))(cc, pk, ds),
                         reads=[pk, vt["vcmk"], xppre[cc]], writes=[ds])
                    p.op("vector", (lambda m, ds: lambda e: e.scalar_tensor_tensor(pool[m][:, 0:SLOT], ds[:, 0:SLOT], 0.0, ds[:, 0:SLOT], ALU.max, ALU.mult))(m, ds),
                         reads=[ds], writes=[pool[m]])
                for o in range(NCH):
                    wbt = wbigL[o % 2]
                    wb = wbt[:]
                    p.op("sync", (lambda o, wb: lambda e: e.dma_start(out=wb, in_=wcp_r[o]))(o, wb),
                         writes=[wbt], dma=True, dkey=wbt, dram_in="wcp")
                    pk = ps_c if (o % 2 == 0) else ps_d

                    def cpgrp(e, wb=wb, pk=pk):
                        for k in range(32):
                            inst = e.matmul(pk[:], wb[:, k * 128:(k + 1) * 128],
                                            pool[k][:, 0:SLOT], start=(k == 0), stop=(k == 31))
                        return inst
                    p.op("tensor", cpgrp, reads=[wbt] + [pool[k] for k in range(32)], writes=[pk])
                    ds = dscA[o % 2]
                    p.op("vector", (lambda o, pk, ds: lambda e: e.tensor_mul(ds[:, 0:SLOT], pk[:], rr[o][:, 0:SLOT]))(o, pk, ds),
                         reads=[pk, rr[o]], writes=[ds])
                    p.op("vector", (lambda o, ds: lambda e: e.tensor_add(x2d[o][:, 1:SW], x2d[o][:, 1:SW], ds[:, 0:SLOT]))(o, ds),
                         reads=[x2d[o], ds], writes=[x2d[o]])
                ln_stats(x2d, SLOT, 1)
                x1f = bufC
                x1b = nb16
                for c in range(NCH):
                    p.op("vector" if c % 2 == 0 else "gpsimd", (lambda c: lambda e: e.tensor_sub(x1f[c][:, 0:SLOT], x2d[c][:, 1:SW], mtile2[:, 0:SLOT]))(c),
                         reads=[x2d[c], mtile2], writes=[x1f[c]])
                    p.op("vector", (lambda c: lambda e: e.tensor_mul(x1f[c][:, 0:SLOT], x1f[c][:, 0:SLOT], rstdt2[:, 0:SLOT]))(c),
                         reads=[x1f[c], rstdt2], writes=[x1f[c]])
                    p.op("scalar", (lambda c: lambda e: e.activation(x1b[c][:], x1f[c][:, 0:SLOT], AF.Copy))(c),
                         reads=[x1f[c]], writes=[x1b[c]])
                faccA = bufD
                faccB = bufE
                for kg in range(4):
                    cbt_ = cbt2
                    cb = cbt_[:]
                    p.op("sync", (lambda kg, cb: lambda e: e.dma_start(out=cb, in_=cbd_b[kg * 128:(kg + 1) * 128, :]))(kg, cb),
                         writes=[cbt_], dma=True, dkey=cbt_, dram_in="cbd")
                    for kl in range(32):
                        k = kg * 32 + kl
                        c = (k // 4) % 8
                        pz = ps_c if (kl % 2 == 0) else ps_d
                        p.op("tensor", (lambda kl, c, pz, cb: lambda e: e.matmul(pz[:], cb[:, kl * 128:(kl + 1) * 128],
                                                                                 x1b[c][:], start=True, stop=True))(kl, c, pz, cb),
                             reads=[cbt_, x1b[c]], writes=[pz])
                        p.op("scalar", (lambda kl, pz: lambda e: e.activation(pool[kl][:, 0:SLOT], pz[:], AF.Gelu))(kl, pz),
                             reads=[pz], writes=[pool[kl]])
                    for o in range(NCH):
                        g = kg * 8 + o
                        wbt = wbigL[o % 2]
                        wb = wbt[:]
                        p.op("sync", (lambda g, wb: lambda e: e.dma_start(out=wb, in_=wpj_r[g]))(g, wb),
                             writes=[wbt], dma=True, dkey=wbt, dram_in="wpj")
                        pf = ps_a if (o % 2 == 0) else ps_b

                        def pjgrp(e, wb=wb, pf=pf):
                            for kl in range(32):
                                inst = e.matmul(pf[:, 0:SLOT], wb[:, kl * 128:(kl + 1) * 128],
                                                pool[kl][:, 0:SLOT], start=(kl == 0), stop=(kl == 31))
                            return inst
                        p.op("tensor", pjgrp, reads=[wbt] + [pool[kl] for kl in range(32)], writes=[pf])
                        if kg == 0:
                            p.op("scalar", (lambda o, pf: lambda e: e.activation(faccA[o][:, 0:SLOT], pf[:, 0:SLOT], AF.Copy))(o, pf),
                                 reads=[pf], writes=[faccA[o]])
                        elif kg % 2 == 1:
                            p.op("vector", (lambda o, pf: lambda e: e.tensor_add(faccB[o][:, 0:SLOT], faccA[o][:, 0:SLOT], pf[:, 0:SLOT]))(o, pf),
                                 reads=[faccA[o], pf], writes=[faccB[o]])
                        else:
                            p.op("vector", (lambda o, pf: lambda e: e.tensor_add(faccA[o][:, 0:SLOT], faccB[o][:, 0:SLOT], pf[:, 0:SLOT]))(o, pf),
                                 reads=[faccB[o], pf], writes=[faccA[o]])
                if _os.environ.get("KDBG2"):
                    for c in range(NCH):
                        p.op("gpsimd", (lambda c, i: lambda e: e.dma_start(out=dbg_x1[c * 128:(c + 1) * 128, i * SLOT:(i + 1) * SLOT], in_=x1f[c][:, 0:SLOT]))(c, i),
                             reads=[x1f[c]], dma=True, dkey=x1f[c])
                        p.op("gpsimd", (lambda c, i: lambda e: e.dma_start(out=dbg_fb[c * 128:(c + 1) * 128, i * SLOT:(i + 1) * SLOT], in_=faccB[c][:, 0:SLOT]))(c, i),
                             reads=[faccB[c]], dma=True, dkey=faccB[c])
                for o in range(NCH):
                    p.op("gpsimd", (lambda o: lambda e: e.tensor_add(faccB[o][:, 0:SLOT], faccB[o][:, 0:SLOT], x1f[o][:, 0:SLOT]))(o),
                         reads=[faccB[o], x1f[o]], writes=[faccB[o]])
                ln_stats(faccB, SLOT, 0)
                for c in range(NCH):
                    p.op("vector" if c % 2 == 0 else "gpsimd", (lambda c: lambda e: e.tensor_sub(outt[:], faccB[c][:, 0:SLOT], mtile2[:, 0:SLOT]))(c),
                         reads=[faccB[c], mtile2], writes=[outt])
                    p.op("vector", (lambda c: lambda e: e.tensor_mul(outt[:], outt[:], rstdt2[:, 0:SLOT]))(c),
                         reads=[outt, rstdt2], writes=[outt])
                    p.op("gpsimd", (lambda c, i: lambda e: e.dma_start(out=out[c * 128:(c + 1) * 128, i * SLOT:(i + 1) * SLOT], in_=outt[:]))(c, i),
                         reads=[outt], dma=True, dkey=outt, dram_out="out")

            import os
            if os.environ.get("KDBG"):
                dbg_k = dram("dbg_k", [C, T], kind="ExternalOutput")
                dbg_v = dram("dbg_v", [C, T], kind="ExternalOutput")
                dbg_y = dram("dbg_y", [C, T], kind="ExternalOutput")
                dtile = dearly if os.environ.get("KDBG2") else E3(nc.sbuf_tensor("dtile", [128, 1024], F32))
                for (srcb, dstb) in ((kbuf, dbg_k), (vbuf, dbg_v), (ybuf, dbg_y)):
                    for c in range(NCH):
                        for tt_ in range(4):
                            p.op("sync", (lambda c, tt_, srcb: lambda e: e.dma_start(out=dtile[:], in_=srcb[c * 128:(c + 1) * 128, tt_ * 1024:(tt_ + 1) * 1024]))(c, tt_, srcb),
                                 writes=[dtile], dma=True, dkey=dtile, dram_in=[("kv", 0, t_) for t_ in range(NTCS)] + [("kv", 1, t_) for t_ in range(NTCS)] + [("y", w_) for w_ in range(NLW)])
                            p.op("gpsimd", (lambda c, tt_, dstb: lambda e: e.dma_start(out=dstb[c * 128:(c + 1) * 128, tt_ * 1024:(tt_ + 1) * 1024], in_=dtile[:]))(c, tt_, dstb),
                                 reads=[dtile], dma=True, dkey=dtile)
            p.op("gpsimd", lambda e: e.engine_nop(),
                 extra_waits=[ev for ev in p.all_events() if ev[0] != "gpsimd"])
            p.op("gpsimd", lambda e: e.engine_nop(),
                 extra_waits=[("gpsimd", p.cnt["gpsimd"])])

            block = E(nc.Block())
            p.replay(block)
    return nc


_CACHE = {}


def kernel(L0_attn_x, L0_attn_time_decay, L0_attn_time_first, L0_attn_W_tm,
           L0_attn_g1, L0_attn_b1, L0_attn_tmk, L0_attn_tmv, L0_attn_tmr,
           L0_attn_W_cm, L0_attn_W_cp, L0_attn_g2, L0_attn_b2, L0_attn_cmk,
           L0_attn_cmr, L0_ffn_rail_maker, L0_ffn_rail_up_proj, L0_ffn_w_cproj):
    f32 = lambda a: np.ascontiguousarray(np.asarray(a), dtype=np.float32)
    x = f32(L0_attn_x)
    Wtm = f32(L0_attn_W_tm)
    Wcm = f32(L0_attn_W_cm)
    Wcp = f32(L0_attn_W_cp)
    Wpj = f32(L0_ffn_w_cproj)
    RM = f32(L0_ffn_rail_maker)
    RU = f32(L0_ffn_rail_up_proj)

    wtm_kv = np.ascontiguousarray(Wtm[:, :2048])
    wtm_r = Wtm[:, 2048:]
    wtmr_host = np.ascontiguousarray(wtm_r.reshape(8, 128, 8, 128).transpose(2, 1, 0, 3).reshape(8 * 128, 8 * 128))
    wcm_host = np.ascontiguousarray(Wcm.reshape(8, 128, 32, 128).transpose(2, 1, 0, 3).reshape(32 * 128, 8 * 128))
    wcp_host = np.ascontiguousarray(Wcp.reshape(32, 128, 8, 128).transpose(2, 1, 0, 3).reshape(8 * 128, 32 * 128))
    WpjT = Wpj.T
    hh, cc_, gg, rl, k16 = np.meshgrid(np.arange(4), np.arange(8), np.arange(4), np.arange(8), np.arange(16), indexing="ij")
    morig = (hh * 4096 + (cc_ * 8 + rl) * 64 + gg * 16 + k16).reshape(-1)
    wpjT_perm = WpjT[morig]
    wpj_host = np.ascontiguousarray(
        wpjT_perm.reshape(4, 32, 128, 8, 128).transpose(0, 3, 2, 1, 4).reshape(32 * 128, 32 * 128))
    rm_host = np.zeros((4, 8, 128, 128), np.float32)
    ru_host = np.zeros((4, 8, 4, 128, 128), np.float32)
    for h4 in range(4):
        for c8 in range(8):
            for rl8 in range(8):
                r = c8 * 8 + rl8
                rm_host[h4, c8, rl8 * 16:(rl8 + 1) * 16, rl8 * 16:(rl8 + 1) * 16] = RM[h4, r].T
                for g4 in range(4):
                    ru_host[h4, c8, g4, rl8 * 16:(rl8 + 1) * 16, rl8 * 16:(rl8 + 1) * 16] = RU[h4, r, g4 * 16:(g4 + 1) * 16, :].T
    rm_host = np.ascontiguousarray(rm_host.reshape(32 * 128, 128))
    ru_host = np.ascontiguousarray(ru_host.reshape(128 * 128, 128))

    vecs = {
        "vw": f32(L0_attn_time_decay).reshape(C, 1),
        "vu": f32(L0_attn_time_first).reshape(C, 1),
        "vtmk": f32(L0_attn_tmk).reshape(C, 1),
        "vtmv": f32(L0_attn_tmv).reshape(C, 1),
        "vtmr": f32(L0_attn_tmr).reshape(C, 1),
        "vcmk": f32(L0_attn_cmk).reshape(C, 1),
        "vcmr": f32(L0_attn_cmr).reshape(C, 1),
        "vg1": f32(L0_attn_g1).reshape(C, 1),
        "vb1": f32(L0_attn_b1).reshape(C, 1),
        "vg2": f32(L0_attn_g2).reshape(C, 1),
        "vb2": f32(L0_attn_b2).reshape(C, 1),
    }

    if "nc" not in _CACHE:
        _CACHE["nc"] = build_program()
    nc = _CACHE["nc"]

    in_maps = []
    for core in range(8):
        b, h = core // 2, core % 2
        xb = np.ascontiguousarray(x[b].T)
        xe_cols = []
        xpr_cols = []
        for i in range(NSLOT):
            s = (2 * i + h) * 512
            toks = (np.arange(-1, SLOT) + s) % T
            xe_cols.append(toks)
            xpr_cols.append(np.where(toks == 0, 0, (toks - 1) % T))
        xe_idx = np.concatenate(xe_cols)
        xpr_idx = np.concatenate(xpr_cols)
        m = {
            "xT": xb,
            "xe_in": np.ascontiguousarray(xb[:, xe_idx]),
            "xpr_in": np.ascontiguousarray(xb[:, xpr_idx]),
            "wtmkv_in": wtm_kv,
            "wtmr_in": wtmr_host,
            "wcm_in": wcm_host,
            "wcp_in": wcp_host,
            "wpj_in": wpj_host,
            "rm_in": rm_host,
            "ru_in": ru_host,
            "mask_in": np.full((128, 1), float(h), np.float32),
        }
        m.update(vecs)
        in_maps.append(m)

    res = run_bass_kernel_spmd(nc, in_maps, core_ids=list(range(8)))
    _CACHE["res"] = res
    full = np.empty((B, T, C), np.float32)
    for core in range(8):
        b, h = core // 2, core % 2
        shard = res.results[core]["out"]
        for i in range(NSLOT):
            s = (2 * i + h) * 512
            full[b, s:s + 512, :] = shard[:, i * SLOT:(i + 1) * SLOT].T
    return full
